# version 06
# baseline (speedup 1.0000x reference)
import math
import contextlib
import numpy as np
import concourse.bass as bass
import concourse.mybir as mybir
from concourse.bass_utils import run_bass_kernel_spmd

F32 = mybir.dt.float32
BF16 = mybir.dt.bfloat16
U8 = mybir.dt.uint8
AF = mybir.ActivationFunctionType
ALU = mybir.AluOpType
AX = mybir.AxisListType

C = 1024
CJ = 8
FF = 4096
FJ = 32
II = 2816
IJ = 22
TC = 512
EPS = 1e-5
RNE = 12582912.0

V_LN1W, V_LN1B, V_LN2W, V_LN2B, V_TMK, V_TMV, V_TMR, V_CMK, V_CMR, V_TD, V_TF = range(11)
NV = 11

GUARD_OPS = {
    "tensor_tensor", "tensor_scalar", "tensor_copy", "tensor_add", "tensor_sub",
    "tensor_mul", "tensor_max", "tensor_relu", "tensor_single_scalar",
    "tensor_scalar_add", "tensor_scalar_sub", "tensor_scalar_mul",
    "tensor_scalar_max", "tensor_scalar_min", "tensor_reduce",
    "scalar_tensor_tensor", "tensor_tensor_scan", "reciprocal", "select",
    "copy_predicated",
    "memset", "memzero", "iota", "affine_select", "activation", "copy",
    "add", "mul", "sign", "square", "sqrt",
}

ENG_SEM = {"sync": "te", "scalar": "se", "vector": "ve", "tensor": "te", "gpsimd": "ge"}


class Guard:

    def __init__(self, eng, sem, guard=True):
        self._e = eng
        self._sem = sem
        self._n = 0
        self._guard = guard

    def bump(self, k):
        self._n += k

    def __getattr__(self, name):
        attr = getattr(self._e, name)
        if self._guard and name in GUARD_OPS:
            def call(*a, **k):
                if self._n > 0:
                    self._e.wait_ge(self._sem, self._n)
                inst = attr(*a, **k)
                inst.then_inc(self._sem, 1)
                self._n += 1
                return inst
            return call
        return attr


class Sched:
    ENGINES = ("sync", "scalar", "vector", "tensor", "gpsimd")

    def __init__(self):
        self.streams = {e: [] for e in self.ENGINES}
        self.ctr = {}
        self.sems = {}
        self.sem_names = []
        for s in ("te", "ve", "se", "ge"):
            self.use_sem(s)

    def use_sem(self, name):
        if name not in self.ctr:
            self.ctr[name] = 0
            self.sem_names.append(name)
        return name

    def dma(self, eng, sem, fn):
        self.use_sem(sem)
        self.streams[eng].append(fn)
        return self.inc(sem, 16)

    def inc(self, name, amount):
        self.ctr[name] += amount
        return self.ctr[name]

    def emit(self, eng, fn, n=0):
        self.streams[eng].append(fn)
        if n:
            sem = ENG_SEM[eng]
            assert sem in ("ve", "se", "ge", "te")
            self.ctr[sem] += n
        return self.ctr[ENG_SEM[eng]]

    def wait(self, eng, sem, tgt):
        assert tgt <= self.ctr[sem], (eng, sem, tgt, self.ctr[sem])
        sems = self.sems

        def _w(e, sem=sem, tgt=tgt):
            e.wait_ge(sems[sem], tgt)
        self.streams[eng].append(_w)

    def barrier(self):
        marks = {sem: self.ctr[sem] for sem in self.sem_names}
        for eng in self.ENGINES:
            for sem, tgt in marks.items():
                if tgt:
                    self.wait(eng, sem, tgt)


def build_module(T=4096, L=4, taps=(), stop_after=None):
    NT = T // TC
    TT = T // 128
    nc = bass.Bass("TRN2", target_bir_lowering=False)
    S = Sched()

    def tk(name):
        return "ExternalOutput" if name in taps else "Internal"

    x_h = nc.dram_tensor("x", [T, C], F32, kind="ExternalInput").ap()
    wkT_h = nc.dram_tensor("wkT", [L, C, C], F32, kind="ExternalInput").ap()
    wvT_h = nc.dram_tensor("wvT", [L, C, C], F32, kind="ExternalInput").ap()
    wrT_h = nc.dram_tensor("wrT", [L, C, C], F32, kind="ExternalInput").ap()
    wcrT_h = nc.dram_tensor("wcrT", [L, C, C], F32, kind="ExternalInput").ap()
    woT_h = nc.dram_tensor("woT", [L, C, C], F32, kind="ExternalInput").ap()
    wckT_h = nc.dram_tensor("wckT", [L, C, FF], F32, kind="ExternalInput").ap()
    wcvT_h = nc.dram_tensor("wcvT", [L, FF, C], F32, kind="ExternalInput").ap()
    vecs_h = nc.dram_tensor("vecs", [L, C, NV], F32, kind="ExternalInput").ap()
    ln0_h = nc.dram_tensor("ln0", [2, C], F32, kind="ExternalInput").ap()
    wg_h = nc.dram_tensor("wg", [II, C], F32, kind="ExternalInput").ap()
    wu_h = nc.dram_tensor("wu", [II, C], F32, kind="ExternalInput").ap()
    wd_h = nc.dram_tensor("wd", [C, II], F32, kind="ExternalInput").ap()

    r_hbm = nc.dram_tensor("r_hbm", [C, T], F32, kind=tk("r_hbm")).ap()
    xq_hbm = nc.dram_tensor("xq_hbm", [C, T], BF16, kind=tk("xq_hbm")).ap()
    xk2_hbm = nc.dram_tensor("xk2_hbm", [C, T], BF16, kind=tk("xk2_hbm")).ap()
    xr2_hbm = nc.dram_tensor("xr2_hbm", [C, T], BF16, kind=tk("xr2_hbm")).ap()
    kf_hbm = nc.dram_tensor("kf_hbm", [FF, T], BF16, kind=tk("kf_hbm")).ap()
    rT_hbm = nc.dram_tensor("rT_hbm", [T, C], F32, kind=tk("rT_hbm")).ap()
    src_hbm = nc.dram_tensor("src_hbm", [C, T], BF16, kind=tk("src_hbm")).ap()
    hqT_hbm = nc.dram_tensor("hqT_hbm", [II, T], BF16, kind=tk("hqT_hbm")).ap()
    out_h = nc.dram_tensor("out", [T, C], F32, kind="ExternalOutput").ap()

    x_v = x_h.rearrange("(tt p) c -> tt p c", p=128)
    out_v = out_h.rearrange("(tt p) c -> tt p c", p=128)
    rT_v = rT_hbm.rearrange("(tt p) c -> tt p c", p=128)
    r_v = r_hbm.rearrange("(cj p) t -> p cj t", p=128)
    xq_v = xq_hbm.rearrange("(cj p) t -> p cj t", p=128)
    xk2_v = xk2_hbm.rearrange("(cj p) t -> p cj t", p=128)
    xr2_v = xr2_hbm.rearrange("(cj p) t -> p cj t", p=128)
    kf_vv = kf_hbm.rearrange("(fj p) t -> p fj t", p=128)
    src_vv = src_hbm.rearrange("(cj p) t -> p cj t", p=128)
    hqT_v = hqT_hbm.rearrange("(ij p) t -> p ij t", p=128)

    alloc = contextlib.ExitStack()
    e = alloc.enter_context

    ident32 = e(nc.sbuf_tensor("ident32", [128, 128], F32))
    identbf = e(nc.sbuf_tensor("identbf", [128, 128], BF16))
    onescol_bf = e(nc.sbuf_tensor("onescol_bf", [128, 1], BF16))
    onesrow_f = e(nc.sbuf_tensor("onesrow_f", [1, 128], F32))
    epsc = e(nc.sbuf_tensor("epsc", [128, 1], F32))
    nc.const_aps.register(F32, EPS, epsc.ap())
    psum = [e(nc.psum_tensor(f"ps{i}", [128, 512], F32)) for i in range(7)]
    psbf = e(nc.psum_tensor("psbf", [128, 1024], BF16))

    AR = 8192
    arena = e(nc.sbuf_tensor("w_arena", [128, 5 * AR], BF16))
    pool = e(nc.sbuf_tensor("poolL", [128, 15872], F32))
    r_sb_t = e(nc.sbuf_tensor("r_sb", [128, 2, CJ, TC], F32))
    h_buf_t = e(nc.sbuf_tensor("h_buf", [128, CJ, TC + 1], F32))

    def carve(base, off, shape, dtype=F32):
        n = math.prod(shape[1:])
        if dtype == F32 and base.dtype == F32:
            v = base[:, off:off + n]
        elif dtype == BF16 and base.dtype == F32:
            v = base[:, off:off + (n + 1) // 2].bitcast(BF16)
        elif dtype == BF16 and base.dtype == BF16:
            v = base[:, off:off + n]
        elif dtype == F32 and base.dtype == BF16:
            v = base[:, off:off + 2 * n].bitcast(F32)
        elif dtype == U8 and base.dtype == F32:
            v = base[:, off:off + (n + 3) // 4].bitcast(U8)
        else:
            raise ValueError((base.dtype, dtype))
        if len(shape) == 2:
            return v
        if len(shape) == 3:
            return v.rearrange("p (a b) -> p a b", b=shape[2])
        return v.rearrange("p (a b c) -> p a b c", b=shape[2], c=shape[3])

    def _ident(g):
        g.memset(epsc[:], EPS)
        g.memset(ident32[:], 0.0)
        g.affine_select(out=ident32[:], in_=ident32[:], compare_op=ALU.not_equal,
                        fill=1.0, base=0, pattern=[[-1, 128]], channel_multiplier=1)
        g.memset(identbf[:], 0.0)
        g.affine_select(out=identbf[:], in_=identbf[:], compare_op=ALU.not_equal,
                        fill=1.0, base=0, pattern=[[-1, 128]], channel_multiplier=1)
        g.memset(onescol_bf[:], 1.0)
        g.memset(onesrow_f[:], 1.0)
    ge_ident = S.emit("gpsimd", _ident, n=7)
    for eng in ("sync", "scalar", "vector", "tensor"):
        S.wait(eng, "ge", ge_ident)

    if True:
        xin = carve(pool, 0, [128, 2, C])
        w0b = carve(pool, 2048, [128, C])
        b0b = carve(pool, 3072, [128, C])
        sqs = carve(pool, 4096, [128, C])
        cps = carve(pool, 5120, [128, C])
        nprime = carve(pool, 6144, [128, C])
        xn_t = carve(pool, 7168, [128, C])
        n2 = carve(pool, 8192, [128, C])
        r0_t = carve(pool, 9216, [128, C])
        scol = carve(pool, 10240, [128, 2, 16])
        bmax = carve(pool, 10272, [128, C // 16])
        qsc = carve(pool, 10336, [128, C // 16])
        qinv = carve(pool, 10400, [128, C // 16])
        qy = carve(pool, 10464, [128, C])
        qa = carve(pool, 11488, [128, C])
        qA = carve(arena, 24576, [128, C], F32)
        qB = carve(arena, 26624, [128, C], F32)
        qC = carve(arena, 28672, [128, C], F32)
        qm1 = carve(pool, 12512, [128, C], U8)
        qm2 = carve(pool, 12768, [128, C], U8)
        xq_t = carve(pool, 13024, [128, C], BF16)
        rstage = carve(arena, 0, [128, 2, CJ, 512], F32)
        qstage = carve(arena, 32768, [128, 2, CJ, 512], BF16)

        S.use_sem("cst")

        def _ld0(sy):
            sy.dma_start(out=w0b[:], in_=ln0_h[0:1, :].partition_broadcast(128)).then_inc(S.sems["cst"], 16)
            sy.dma_start(out=b0b[:], in_=ln0_h[1:2, :].partition_broadcast(128)).then_inc(S.sems["cst"], 16)
        S.emit("sync", _ld0)
        ld_ln0 = S.inc("cst", 32)

        ve_tile = []
        se_tile = []
        st_span = []

        for i in range(TT):
            slot = i % 2
            span = i // 4
            sub = i % 4

            if i >= 2:
                S.wait("sync", "ve", ve_tile[i - 2])
                S.wait("sync", "se", se_tile[i - 2])

            xsem = S.use_sem(f"xin{slot}")

            def _ldx(sy, i=i, slot=slot, xsem=xsem):
                sy.dma_start(out=xin[:, slot], in_=x_v[i]).then_inc(S.sems[xsem], 16)
            S.emit("sync", _ldx)
            ld_x = S.inc(xsem, 16)

            S.wait("scalar", xsem, ld_x)

            def _st1(s, slot=slot):
                s.activation(out=cps[:], in_=xin[:, slot], func=AF.Identity,
                             accum_out=scol[:, slot, 0:1])
                s.activation(out=sqs[:], in_=xin[:, slot], func=AF.Square,
                             accum_out=scol[:, slot, 1:2])
            se_st1 = S.emit("scalar", _st1, n=2)

            S.wait("vector", xsem, ld_x)
            S.wait("vector", "se", se_st1)

            def _mv(v, slot=slot):
                v.tensor_scalar(out=scol[:, slot, 2:3], in0=scol[:, slot, 0:1],
                                scalar1=1.0 / C, scalar2=None, op0=ALU.mult)
                v.tensor_tensor(out=scol[:, slot, 3:4], in0=scol[:, slot, 2:3],
                                in1=scol[:, slot, 2:3], op=ALU.mult)
                v.scalar_tensor_tensor(out=scol[:, slot, 4:5], in0=scol[:, slot, 1:2],
                                       scalar=1.0 / C, in1=scol[:, slot, 3:4],
                                       op0=ALU.mult, op1=ALU.subtract)
                v.tensor_scalar(out=nprime[:], in0=xin[:, slot], scalar1=scol[:, slot, 2:3],
                                scalar2=None, op0=ALU.subtract)
            ve_np = S.emit("vector", _mv, n=4)

            S.wait("scalar", "ve", ve_np)

            def _rs(s, slot=slot):
                s.activation(out=scol[:, slot, 5:6], in_=scol[:, slot, 4:5], func=AF.Ln, bias=EPS)
                s.activation(out=scol[:, slot, 6:7], in_=scol[:, slot, 5:6], func=AF.Exp,
                             scale=-0.5)
            se_rs = S.emit("scalar", _rs, n=2)

            S.wait("vector", "se", se_rs)

            def _xn(v, slot=slot):
                v.tensor_scalar(out=xn_t[:], in0=nprime[:], scalar1=scol[:, slot, 6:7],
                                scalar2=None, op0=ALU.mult)
            ve_xn = S.emit("vector", _xn, n=1)

            S.wait("scalar", "ve", ve_xn)

            def _st2(s, slot=slot):
                s.activation(out=cps[:], in_=xn_t[:], func=AF.Identity,
                             accum_out=scol[:, slot, 7:8])
                s.activation(out=sqs[:], in_=xn_t[:], func=AF.Square,
                             accum_out=scol[:, slot, 8:9])
            se_st2 = S.emit("scalar", _st2, n=2)
            S.wait("vector", "se", se_st2)

            def _mv2(v, slot=slot):
                v.tensor_scalar(out=scol[:, slot, 9:10], in0=scol[:, slot, 7:8],
                                scalar1=1.0 / C, scalar2=None, op0=ALU.mult)
                v.tensor_tensor(out=scol[:, slot, 10:11], in0=scol[:, slot, 9:10],
                                in1=scol[:, slot, 9:10], op=ALU.mult)
                v.scalar_tensor_tensor(out=scol[:, slot, 11:12], in0=scol[:, slot, 8:9],
                                       scalar=1.0 / C, in1=scol[:, slot, 10:11],
                                       op0=ALU.mult, op1=ALU.subtract)
                v.tensor_scalar(out=n2[:], in0=xn_t[:], scalar1=scol[:, slot, 9:10],
                                scalar2=None, op0=ALU.subtract)
            ve_var2 = S.emit("vector", _mv2, n=4)
            S.wait("scalar", "ve", ve_var2)

            def _rs2(s, slot=slot):
                s.activation(out=scol[:, slot, 12:13], in_=scol[:, slot, 11:12], func=AF.Ln, bias=EPS)
                s.activation(out=scol[:, slot, 13:14], in_=scol[:, slot, 12:13], func=AF.Exp,
                             scale=-0.5)
            se_rs2 = S.emit("scalar", _rs2, n=2)
            S.wait("vector", "se", se_rs2)
            if i == 0:
                S.wait("vector", "cst", ld_ln0)

            def _r0(v, slot=slot):
                v.tensor_scalar(out=n2[:], in0=n2[:], scalar1=scol[:, slot, 13:14],
                                scalar2=None, op0=ALU.mult)
                v.tensor_tensor(out=r0_t[:], in0=n2[:], in1=w0b[:], op=ALU.mult)
                v.tensor_tensor(out=r0_t[:], in0=r0_t[:], in1=b0b[:], op=ALU.add)
            ve_r0 = S.emit("vector", _r0, n=3)

            def _bm(v, slot=slot):
                v.tensor_reduce(out=bmax[:], in_=xn_t[:].rearrange("p (g k) -> p g k", k=16),
                                axis=AX.X, op=ALU.max, apply_absolute_value=True)
                v.tensor_scalar(out=qsc[:], in0=bmax[:], scalar1=1.0 / 6.0,
                                scalar2=1e-12, op0=ALU.mult, op1=ALU.max)
            ve_bm = S.emit("vector", _bm, n=2)

            def _qi(v, slot=slot):
                v.reciprocal(out=qinv[:], in_=qsc[:])
            S.emit("vector", _qi, n=1)

            def _qy(v, slot=slot):
                v.tensor_tensor(out=qy[:].rearrange("p (g k) -> p g k", k=16),
                                in0=xn_t[:].rearrange("p (g k) -> p g k", k=16),
                                in1=qinv[:].unsqueeze(2).broadcast_to([128, C // 16, 16]),
                                op=ALU.mult)
            ve_qy = S.emit("vector", _qy, n=1)
            S.wait("scalar", "ve", ve_qy)

            def _qabs(s):
                s.activation(out=qa[:], in_=qy[:], func=AF.Abs)
            se_qabs = S.emit("scalar", _qabs, n=1)

            def _qgrid(v):
                v.tensor_scalar(out=qA[:], in0=qy[:], scalar1=2.0, scalar2=RNE,
                                op0=ALU.mult, op1=ALU.add)
                v.tensor_scalar(out=qA[:], in0=qA[:], scalar1=RNE, scalar2=0.5,
                                op0=ALU.subtract, op1=ALU.mult)
                v.tensor_scalar(out=qB[:], in0=qy[:], scalar1=RNE, scalar2=RNE,
                                op0=ALU.add, op1=ALU.subtract)
                v.tensor_scalar(out=qC[:], in0=qy[:], scalar1=0.5, scalar2=RNE,
                                op0=ALU.mult, op1=ALU.add)
                v.tensor_scalar(out=qC[:], in0=qC[:], scalar1=RNE, scalar2=2.0,
                                op0=ALU.subtract, op1=ALU.mult)
            S.emit("vector", _qgrid, n=5)
            S.wait("vector", "se", se_qabs)

            def _qsel(v, slot=slot):
                v.tensor_single_scalar(out=qm1[:], in_=qa[:], scalar=2.25, op=ALU.is_lt)
                v.tensor_single_scalar(out=qm2[:], in_=qa[:], scalar=4.5, op=ALU.is_lt)
                v.tensor_copy(out=qy[:], in_=qC[:])
                v.copy_predicated(out=qy[:], mask=qm2[:], data=qB[:])
                v.copy_predicated(out=qy[:], mask=qm1[:], data=qA[:])
                v.tensor_tensor(out=xq_t[:].rearrange("p (g k) -> p g k", k=16),
                                in0=qy[:].rearrange("p (g k) -> p g k", k=16),
                                in1=qsc[:].unsqueeze(2).broadcast_to([128, C // 16, 16]),
                                op=ALU.mult)
            ve_q = S.emit("vector", _qsel, n=6)

            if i > 0:
                S.wait("tensor", "ve", ve_tile[i - 1])
                S.wait("tensor", "se", se_tile[i - 1])
            S.wait("tensor", "ve", ve_r0)
            S.wait("tensor", "ve", ve_q)

            def _tr(t):
                for cj in range(CJ):
                    t.transpose(psum[cj // 4][:, (cj % 4) * 128:(cj % 4) * 128 + 128],
                                r0_t[:, cj * 128:cj * 128 + 128], ident32[:])
                last = None
                for cj in range(CJ):
                    last = t.transpose(psbf[:, cj * 128:cj * 128 + 128],
                                       xq_t[:, cj * 128:cj * 128 + 128], identbf[:])
                last.then_inc(S.sems["te"], 1)
            S.emit("tensor", _tr)
            te_tr = S.inc("te", 1)

            S.wait("vector", "te", te_tr)

            def _ev(v, slot2=span % 2, sub=sub):
                v.tensor_copy(out=rstage[:, slot2, 0:4, sub * 128:sub * 128 + 128],
                              in_=psum[0][:].rearrange("p (a b) -> p a b", b=128))
                v.tensor_copy(out=rstage[:, slot2, 4:8, sub * 128:sub * 128 + 128],
                              in_=psum[1][:].rearrange("p (a b) -> p a b", b=128))
            ve_ev = S.emit("vector", _ev, n=2)
            S.wait("scalar", "te", te_tr)

            def _evq(s, slot2=span % 2, sub=sub):
                s.copy(out=qstage[:, slot2, :, sub * 128:sub * 128 + 128],
                       in_=psbf[:].rearrange("p (a b) -> p a b", b=128))
            se_evq = S.emit("scalar", _evq, n=1)

            ve_tile.append(ve_ev)
            se_tile.append(se_evq)

            if sub == 3:
                ssem = S.use_sem(f"stg{span % 2}")
                S.wait("scalar", "ve", ve_ev)
                S.wait("scalar", "se", se_evq)

                def _stsp(s, span=span, slot2=span % 2, ssem=ssem):
                    s.dma_start(out=r_v[:, :, span * 512:span * 512 + 512],
                                in_=rstage[:, slot2]).then_inc(S.sems[ssem], 16)
                    s.dma_start(out=xq_v[:, :, span * 512:span * 512 + 512],
                                in_=qstage[:, slot2]).then_inc(S.sems[ssem], 16)
                S.emit("scalar", _stsp)
                st_span.append(S.inc(ssem, 32))
            if sub == 0 and span >= 2:
                ssem2 = f"stg{span % 2}"
                S.wait("vector", ssem2, st_span[span - 2])
                S.wait("scalar", ssem2, st_span[span - 2])

        S.barrier()

    if stop_after != "init":
        _build_layers(nc, S, T=T, L=L, stop_after=stop_after, psum=psum, psbf=psbf,
                      ident32=ident32, identbf=identbf, onescol_bf=onescol_bf,
                      onesrow_f=onesrow_f, alloc=alloc, arena=arena, pool=pool,
                      carve=carve,
                      hb=dict(r_v=r_v, xq_v=xq_v, xk2_v=xk2_v, xr2_v=xr2_v, kf_v=kf_vv,
                              src_v=src_vv, r_sb=r_sb_t, h_buf=h_buf_t,
                              rT_v=rT_v, hqT_v=hqT_v, x_v=x_v, out_v=out_v,
                              wkT=wkT_h, wvT=wvT_h, wrT=wrT_h, wcrT=wcrT_h, woT=woT_h,
                              wckT=wckT_h, wcvT=wcvT_h, vecs=vecs_h,
                              wg=wg_h, wu=wu_h, wd=wd_h))

    store_marks = {s: S.ctr[s] for s in S.sem_names
                   if s.startswith(("stg", "ost")) and S.ctr[s]}
    semstack = contextlib.ExitStack()
    for name in S.sem_names:
        S.sems[name] = semstack.enter_context(nc.semaphore(name))

    def _fin(sy, marks=store_marks):
        for s, tgt in marks.items():
            sy.wait_ge(S.sems[s], tgt)
    S.emit("sync", _fin)

    guards = {}
    with nc.Block() as block:
        for eng_name, deco in (("sync", block.sync), ("scalar", block.scalar),
                               ("vector", block.vector), ("tensor", block.tensor),
                               ("gpsimd", block.gpsimd)):
            if S.streams[eng_name]:
                def _run(eng, eng_name=eng_name, fns=S.streams[eng_name]):
                    g = Guard(eng, S.sems[ENG_SEM[eng_name]],
                              guard=eng_name in ("scalar", "vector", "gpsimd"))
                    guards[eng_name] = g
                    for fn in fns:
                        fn(g)
                deco(_run)
    for eng_name in ("scalar", "vector", "gpsimd"):
        if eng_name in guards:
            got = guards[eng_name]._n
            want = S.ctr[ENG_SEM[eng_name]]
            assert got == want, (eng_name, got, want)
    semstack.close()
    alloc.close()
    return nc


def _build_layers(nc, S, T, L, stop_after, psum, psbf, ident32, identbf,
                  onescol_bf, onesrow_f, alloc, arena, pool, carve, hb):
    NT = T // TC
    e = alloc.enter_context
    AR = 8192
    wk_v = arena[:, 0 * AR:1 * AR].rearrange("p (c o) -> p c o", o=1024)
    wv_v = arena[:, 1 * AR:2 * AR].rearrange("p (c o) -> p c o", o=1024)
    wr_v = arena[:, 2 * AR:3 * AR].rearrange("p (c o) -> p c o", o=1024)
    wo_v = arena[:, 3 * AR:4 * AR].rearrange("p (c o) -> p c o", o=1024)
    wcr_v = arena[:, 4 * AR:5 * AR].rearrange("p (c o) -> p c o", o=1024)
    wck_v = arena[:, 0:4 * AR].rearrange("p (c f) -> p c f", f=4096)
    wcv_v = arena[:, 0:4 * AR].rearrange("p (f c) -> p f c", c=1024)

    r_sb = hb["r_sb"]
    h_buf = hb["h_buf"]
    hb_sav = e(nc.sbuf_tensor("hb_sav", [128, CJ, 1], F32))
    hb2_sav = e(nc.sbuf_tensor("hb2_sav", [128, CJ, 1], F32))
    a_carry = e(nc.sbuf_tensor("a_carry", [128, CJ, 1], F32))
    b_carry = e(nc.sbuf_tensor("b_carry", [128, CJ, 1], F32))
    stripL = e(nc.sbuf_tensor("stripL", [1, 6, TC], F32))
    vec_sb = e(nc.sbuf_tensor("vec_sb", [128, 2, CJ, NV], F32))
    ew_t = e(nc.sbuf_tensor("ew_t", [128, 2, CJ, 1], F32))
    eu_t = e(nc.sbuf_tensor("eu_t", [128, 2, CJ, 1], F32))

    drb = carve(pool, 0, [128, CJ, TC])
    rb_bf = carve(arena, 4 * AR, [128, CJ, TC], BF16)
    rsq_bf = carve(arena, 4 * AR + CJ * TC, [128, CJ, TC], BF16)
    rwkv_bf = carve(pool, 0, [128, CJ, TC], BF16)
    xm_a = carve(pool, 4096, [128, CJ, TC], BF16)
    xm_b = carve(pool, 6144, [128, CJ, TC], BF16)
    sr_full = carve(pool, 8192, [128, CJ, TC], BF16)
    E_b = carve(pool, 10240, [128, 2, TC])
    srt = carve(pool, 10240, [128, TC])
    R_b = carve(pool, 11264, [128, 2, TC])
    A_b = carve(pool, 12288, [128, 2, TC + 1])
    B_b = carve(pool, 14340, [128, 2, TC + 1])
    xk2in = carve(pool, 0, [128, 2, CJ, TC], BF16)
    kfst = carve(pool, 4096, [128, 2, 4, TC], BF16)
    relu_scr = carve(pool, 6144, [128, 2, TC])
    wcrP = carve(pool, 7168, [128, CJ, 1024], BF16)
    xr2inA = carve(pool, 11264, [128, CJ, TC], BF16)
    srtA = carve(pool, 13312, [128, TC])
    srcA = carve(pool, 13824, [128, CJ, TC], BF16)
    kfin = carve(pool, 0, [128, FJ, TC], BF16)
    src_bf = carve(pool, 8192, [128, CJ, TC], BF16)
    kvscr = carve(pool, 10240, [128, TC])

    sems = S.sems
    r_marks = [None] * NT
    xk2_marks = [None] * NT
    xr2_marks = [None] * NT
    src_marks = [None] * NT
    kf_marks = [[None] * 8 for _ in range(NT)]
    region_free = [("te", 0)] * 5

    def dma_load(sem, fn, gate=(), eng="sync"):
        for gsem, gtgt in gate:
            if gtgt:
                S.wait(eng, gsem, gtgt)
        return S.dma(eng, sem, fn)

    for l in range(L):
        ls = l % 2

        cst_t = dma_load("cst", (lambda sy, l=l, ls=ls:
            sy.dma_start(out=vec_sb[:, ls],
                         in_=hb["vecs"][l].rearrange("(cj p) v -> p cj v", p=128)
                         ).then_inc(sems["cst"], 16)))
        S.wait("scalar", "cst", cst_t)

        def _ewu(s, ls=ls):
            s.activation(out=ew_t[:, ls], in_=vec_sb[:, ls, :, V_TD:V_TD + 1], func=AF.Exp)
            s.activation(out=ew_t[:, ls], in_=ew_t[:, ls], func=AF.Exp, scale=-1.0)
            s.activation(out=eu_t[:, ls], in_=vec_sb[:, ls, :, V_TF:V_TF + 1], func=AF.Exp)
        se_ewu = S.emit("scalar", _ewu, n=3)

        wqm = {}
        for name, view, src_h, reg in (("wk", wk_v, hb["wkT"], 0), ("wv", wv_v, hb["wvT"], 1),
                                       ("wr", wr_v, hb["wrT"], 2), ("wo", wo_v, hb["woT"], 3)):
            sem = S.use_sem("wq" + str(reg))
            wqm[name] = dma_load(sem, (lambda g, view=view, src_h=src_h, l=l, sem=sem:
                g.dma_start(out=view,
                            in_=src_h[l].rearrange("(cj p) o -> p cj o", p=128)
                            ).then_inc(sems[sem], 16)),
                gate=[region_free[reg]], eng="gpsimd")

        def _lrst(g):
            g.memset(a_carry[:], 0.0)
            g.memset(b_carry[:], 0.0)
            g.memset(hb_sav[:], 0.0)
            g.memset(hb2_sav[:], 0.0)
        ge_lrst = S.emit("gpsimd", _lrst, n=4)
        S.wait("vector", "ge", ge_lrst)

        H = [dict() for _ in range(NT + 2)]

        for q in range(NT):
            so = 0
            s_ = q % 2
            hq = H[q]
            h1 = H[q - 1] if q > 0 else {}
            h2 = H[q - 2] if q > 1 else {}

            rl_sem = S.use_sem(f"rld{s_}")
            gates = []
            if h2:
                gates += [("ve", h2["ve_end"]), ("se", h2["se_end"]),
                          ("ge", h2["ge_end"]), h2["r_mark"]]
            if r_marks[q] is not None:
                gates.append(r_marks[q])
            hq["rl"] = dma_load(rl_sem, (lambda sy, q=q, s_=s_, rl_sem=rl_sem:
                sy.dma_start(out=r_sb[:, s_], in_=hb["r_v"][:, :, q * TC:q * TC + TC]
                             ).then_inc(sems[rl_sem], 16)), gate=gates)

            if h1:
                S.wait("gpsimd", "ve", h1["ve_mix2"])
            def _hshift(g):
                g.tensor_copy(out=h_buf[:, :, 0:1], in_=hb_sav[:])
            ge_hshift = S.emit("gpsimd", _hshift, n=1)
            S.wait("vector", "ge", ge_hshift)

            S.wait("gpsimd", f"rld{s_}", hq["rl"])
            if h1:
                S.wait("gpsimd", "te", h1["te_stats2"])
            def _rb(g, s_=s_):
                g.tensor_copy(out=rb_bf[:], in_=r_sb[:, s_])
            ge_rb = S.emit("gpsimd", _rb, n=1)
            def _rsq(g, s_=s_):
                g.tensor_tensor(out=rsq_bf[:], in0=r_sb[:, s_], in1=r_sb[:, s_],
                                op=ALU.mult)
            ge_rsq = S.emit("gpsimd", _rsq, n=1)

            S.wait("tensor", "ge", ge_rsq)
            if h1:
                S.wait("tensor", "ve", h1["ve_h2"])
            def _stats(t):
                for cj in range(CJ):
                    t.matmul(psum[4][0:1, :], onescol_bf[:], rb_bf[:, cj],
                             start=(cj == 0), stop=(cj == CJ - 1))
                last_i = None
                for cj in range(CJ):
                    last_i = t.matmul(psum[5][0:1, :], onescol_bf[:], rsq_bf[:, cj],
                                      start=(cj == 0), stop=(cj == CJ - 1))
                last_i.then_inc(sems["te"], 1)
            S.emit("tensor", _stats)
            te_stats = S.inc("te", 1)

            S.wait("vector", "te", te_stats)
            def _strip1(v, so=so):
                v.tensor_scalar(out=stripL[0:1, so + 0, :], in0=psum[4][0:1, :],
                                scalar1=1.0 / C, scalar2=None, op0=ALU.mult)
                v.tensor_tensor(out=stripL[0:1, so + 1, :], in0=stripL[0:1, so + 0, :],
                                in1=stripL[0:1, so + 0, :], op=ALU.mult)
                v.scalar_tensor_tensor(out=stripL[0:1, so + 2, :], in0=psum[5][0:1, :],
                                       scalar=1.0 / C, in1=stripL[0:1, so + 1, :],
                                       op0=ALU.mult, op1=ALU.subtract)
            ve_var = S.emit("vector", _strip1, n=3)
            S.wait("scalar", "ve", ve_var)
            def _rs1(s, so=so):
                s.activation(out=stripL[0:1, so + 3, :], in_=stripL[0:1, so + 2, :],
                             func=AF.Ln, bias=EPS)
                s.activation(out=stripL[0:1, so + 4, :], in_=stripL[0:1, so + 3, :],
                             func=AF.Exp, scale=-0.5)
            se_rs1 = S.emit("scalar", _rs1, n=2)
            S.wait("vector", "se", se_rs1)
            def _nm1(v, so=so):
                v.scalar_tensor_tensor(out=stripL[0:1, so + 5, :], in0=stripL[0:1, so + 0, :],
                                       scalar=-1.0, in1=stripL[0:1, so + 4, :],
                                       op0=ALU.mult, op1=ALU.mult)
            ve_nm1 = S.emit("vector", _nm1, n=1)

            S.wait("tensor", "ve", ve_nm1)
            def _bc1(t, so=so):
                t.matmul(psum[4][:], onesrow_f[:], stripL[0:1, so + 5, :],
                         start=True, stop=True)
                t.matmul(psum[5][:], onesrow_f[:], stripL[0:1, so + 4, :],
                         start=True, stop=True).then_inc(sems["te"], 1)
            S.emit("tensor", _bc1)
            te_bc1 = S.inc("te", 1)

            S.wait("vector", "te", te_bc1)
            def _happ(v, ls=ls, s_=s_):
                v.tensor_tensor(out=h_buf[:, :, 1:TC + 1], in0=r_sb[:, s_],
                                in1=psum[4][:].unsqueeze(1).broadcast_to([128, CJ, TC]),
                                op=ALU.add)
                v.tensor_tensor(out=h_buf[:, :, 1:TC + 1], in0=h_buf[:, :, 1:TC + 1],
                                in1=psum[5][:].unsqueeze(1).broadcast_to([128, CJ, TC]),
                                op=ALU.mult)
                for cj in range(CJ):
                    v.tensor_scalar(out=h_buf[:, cj, 1:TC + 1], in0=h_buf[:, cj, 1:TC + 1],
                                    scalar1=vec_sb[:, ls, cj, V_LN1W:V_LN1W + 1],
                                    scalar2=vec_sb[:, ls, cj, V_LN1B:V_LN1B + 1],
                                    op0=ALU.mult, op1=ALU.add)
            ve_h = S.emit("vector", _happ, n=2 + CJ)

            if h1:
                S.wait("vector", "te", h1["te_o_last"])
            def _dd(v):
                v.tensor_tensor(out=drb[:], in0=h_buf[:, :, 1:TC + 1],
                                in1=h_buf[:, :, 0:TC], op=ALU.subtract)
            S.emit("vector", _dd, n=1)

            if l == 0 and q == 0:
                S.wait("vector", "cst", cst_t)
            if h1:
                S.wait("vector", "te", h1["te_k_last"])
                S.wait("vector", h1["xk2_mark"][0], h1["xk2_mark"][1])
                S.wait("vector", h1["xr2_mark"][0], h1["xr2_mark"][1])
            def _mixr(v, ls=ls):
                for cj in range(CJ):
                    v.scalar_tensor_tensor(out=xm_a[:, cj], in0=drb[:, cj],
                                           scalar=vec_sb[:, ls, cj, V_TMR:V_TMR + 1],
                                           in1=h_buf[:, cj, 0:TC],
                                           op0=ALU.mult, op1=ALU.add)
            ve_xr = S.emit("vector", _mixr, n=CJ)

            if q == 0:
                S.wait("tensor", "wq2", wqm["wr"])
                S.wait("scalar", "se", se_ewu)
            S.wait("tensor", "ve", ve_xr)
            te_r = {}
            se_sr = {}
            for oj in range(CJ):
                sl = oj % 2
                if oj < 2:
                    S.wait("tensor", "ve", ve_h)
                else:
                    S.wait("tensor", "se", se_sr[oj - 2])
                def _rmm(t, oj=oj, sl=sl):
                    last_i = None
                    for cj in range(CJ):
                        last_i = t.matmul(psum[4 + sl][:], wr_v[:, cj, oj * 128:oj * 128 + 128],
                                          xm_a[:, cj], start=(cj == 0), stop=(cj == CJ - 1))
                    last_i.then_inc(sems["te"], 1)
                S.emit("tensor", _rmm)
                te_r[oj] = S.inc("te", 1)
                S.wait("scalar", "te", te_r[oj])
                if oj == 0 and h1:
                    S.wait("scalar", "ve", h1["ve_out_last"])
                if oj == 0 and q == 0:
                    pass
                def _sr(s, oj=oj, sl=sl):
                    s.activation(out=srt[:], in_=psum[4 + sl][:], func=AF.Exp, scale=-1.0)
                    s.activation(out=srt[:], in_=srt[:], func=AF.Ln, bias=1.0)
                    s.activation(out=sr_full[:, oj], in_=srt[:], func=AF.Exp, scale=-1.0)
                se_sr[oj] = S.emit("scalar", _sr, n=3)
            hq["te_r_last"] = te_r[CJ - 1]

            S.wait("vector", "te", te_r[CJ - 1])
            if h1:
                S.wait("vector", "te", h1["te_v_last"])
            def _mixkv(v, ls=ls):
                for cj in range(CJ):
                    v.scalar_tensor_tensor(out=xm_a[:, cj], in0=drb[:, cj],
                                           scalar=vec_sb[:, ls, cj, V_TMK:V_TMK + 1],
                                           in1=h_buf[:, cj, 0:TC],
                                           op0=ALU.mult, op1=ALU.add)
                for cj in range(CJ):
                    v.scalar_tensor_tensor(out=xm_b[:, cj], in0=drb[:, cj],
                                           scalar=vec_sb[:, ls, cj, V_TMV:V_TMV + 1],
                                           in1=h_buf[:, cj, 0:TC],
                                           op0=ALU.mult, op1=ALU.add)
            ve_kv = S.emit("vector", _mixkv, n=2 * CJ)

            if q == 0:
                S.wait("tensor", "wq0", wqm["wk"])
                S.wait("tensor", "wq1", wqm["wv"])
            S.wait("tensor", "ve", ve_kv)
            se_E = {}
            ve_R = {}
            se_dinv = {}
            ve_out = {}
            te_k = {}
            te_v = {}
            for oj in range(CJ):
                sl = oj % 2
                if oj >= 2:
                    S.wait("tensor", "se", se_E[oj - 2])
                def _kmm(t, oj=oj, sl=sl):
                    last_i = None
                    for cj in range(CJ):
                        last_i = t.matmul(psum[sl][:], wk_v[:, cj, oj * 128:oj * 128 + 128],
                                          xm_a[:, cj], start=(cj == 0), stop=(cj == CJ - 1))
                    last_i.then_inc(sems["te"], 1)
                S.emit("tensor", _kmm)
                te_k[oj] = S.inc("te", 1)

                S.wait("scalar", "te", te_k[oj])
                if oj >= 2:
                    S.wait("scalar", "ve", ve_out[oj - 2])
                def _E(s, sl=sl):
                    s.activation(out=E_b[:, sl], in_=psum[sl][:], func=AF.Exp)
                se_E[oj] = S.emit("scalar", _E, n=1)

                if oj >= 2:
                    S.wait("tensor", "ve", ve_R[oj - 2])
                def _vmm(t, oj=oj, sl=sl):
                    last_i = None
                    for cj in range(CJ):
                        last_i = t.matmul(psum[2 + sl][:], wv_v[:, cj, oj * 128:oj * 128 + 128],
                                          xm_b[:, cj], start=(cj == 0), stop=(cj == CJ - 1))
                    last_i.then_inc(sems["te"], 1)
                S.emit("tensor", _vmm)
                te_v[oj] = S.inc("te", 1)

                S.wait("vector", "te", te_v[oj])
                S.wait("vector", "se", se_E[oj])
                if oj >= 2:
                    S.wait("vector", "se", se_dinv[oj - 2])
                def _R(v, sl=sl):
                    v.tensor_tensor(out=R_b[:, sl], in0=E_b[:, sl], in1=psum[2 + sl][:],
                                    op=ALU.mult)
                ve_R[oj] = S.emit("vector", _R, n=1)

                def _wkv(v, oj=oj, sl=sl, ls=ls):
                    v.tensor_copy(out=A_b[:, sl, 0:1], in_=a_carry[:, oj])
                    v.tensor_copy(out=B_b[:, sl, 0:1], in_=b_carry[:, oj])
                    v.tensor_tensor_scan(out=A_b[:, sl, 1:TC + 1],
                                         data0=ew_t[:, ls, oj].broadcast_to([128, TC]),
                                         data1=R_b[:, sl], initial=A_b[:, sl, 0:1],
                                         op0=ALU.mult, op1=ALU.add)
                    v.tensor_tensor_scan(out=B_b[:, sl, 1:TC + 1],
                                         data0=ew_t[:, ls, oj].broadcast_to([128, TC]),
                                         data1=E_b[:, sl], initial=B_b[:, sl, 0:1],
                                         op0=ALU.mult, op1=ALU.add)
                    v.tensor_copy(out=a_carry[:, oj], in_=A_b[:, sl, TC:TC + 1])
                    v.tensor_copy(out=b_carry[:, oj], in_=B_b[:, sl, TC:TC + 1])
                    v.scalar_tensor_tensor(out=R_b[:, sl], in0=R_b[:, sl],
                                           scalar=eu_t[:, ls, oj], in1=A_b[:, sl, 0:TC],
                                           op0=ALU.mult, op1=ALU.add)
                    v.scalar_tensor_tensor(out=E_b[:, sl], in0=E_b[:, sl],
                                           scalar=eu_t[:, ls, oj], in1=B_b[:, sl, 0:TC],
                                           op0=ALU.mult, op1=ALU.add)
                ve_scan = S.emit("vector", _wkv, n=8)

                S.wait("scalar", "ve", ve_scan)
                def _dinv(s, sl=sl):
                    s.activation(out=E_b[:, sl], in_=E_b[:, sl], func=AF.Ln)
                    s.activation(out=E_b[:, sl], in_=E_b[:, sl], func=AF.Exp, scale=-1.0)
                se_dinv[oj] = S.emit("scalar", _dinv, n=2)

                def _emit_out(ojx):
                    slx = ojx % 2
                    S.wait("vector", "se", se_dinv[ojx])
                    if ojx < 2 and h1:
                        S.wait("vector", "te", h1["te_o_last"])
                    def _out(v, ojx=ojx, slx=slx):
                        v.tensor_tensor(out=R_b[:, slx], in0=R_b[:, slx], in1=E_b[:, slx],
                                        op=ALU.mult)
                        v.tensor_tensor(out=rwkv_bf[:, ojx], in0=R_b[:, slx],
                                        in1=sr_full[:, ojx], op=ALU.mult)
                    ve_out[ojx] = S.emit("vector", _out, n=2)
                if oj >= 1:
                    _emit_out(oj - 1)
            _emit_out(CJ - 1)
            hq["te_k_last"] = te_k[CJ - 1]
            hq["te_v_last"] = te_v[CJ - 1]
            hq["ve_out_last"] = ve_out[CJ - 1]

            ve_radd = {}
            te_o = {}
            for cj in range(CJ):
                ob = 6 if cj % 2 == 0 else 0
                if cj == 0:
                    for oj in range(CJ):
                        S.wait("tensor", "ve", ve_out[oj])
                    if q == 0:
                        S.wait("tensor", "wq3", wqm["wo"])
                elif cj == 1:
                    S.wait("tensor", "se", se_E[CJ - 1])
                else:
                    S.wait("tensor", "ve", ve_radd[cj - 2])
                def _omm(t, cj=cj, ob=ob):
                    last_i = None
                    for oj in range(CJ):
                        last_i = t.matmul(psum[ob][:], wo_v[:, oj, cj * 128:cj * 128 + 128],
                                          rwkv_bf[:, oj], start=(oj == 0), stop=(oj == CJ - 1))
                    last_i.then_inc(sems["te"], 1)
                S.emit("tensor", _omm)
                te_o[cj] = S.inc("te", 1)
                S.wait("vector", "te", te_o[cj])
                def _radd(v, cj=cj, s_=s_, ob=ob):
                    v.tensor_tensor(out=r_sb[:, s_, cj], in0=r_sb[:, s_, cj], in1=psum[ob][:],
                                    op=ALU.add)
                ve_radd[cj] = S.emit("vector", _radd, n=1)
            hq["te_o_last"] = te_o[CJ - 1]

            S.wait("gpsimd", "ve", ve_kv)
            def _hsav(g):
                g.tensor_copy(out=hb_sav[:], in_=h_buf[:, :, TC:TC + 1])
                g.tensor_copy(out=h_buf[:, :, 0:1], in_=hb2_sav[:])
            ge_hsav = S.emit("gpsimd", _hsav, n=2)
            S.wait("vector", "ge", ge_hsav)

            S.wait("gpsimd", "ve", ve_radd[CJ - 1])
            S.wait("gpsimd", "te", te_stats)
            def _rb2(g, s_=s_):
                g.tensor_copy(out=rb_bf[:], in_=r_sb[:, s_])
            ge_rb2 = S.emit("gpsimd", _rb2, n=1)
            def _rsq2(g, s_=s_):
                g.tensor_tensor(out=rsq_bf[:], in0=r_sb[:, s_], in1=r_sb[:, s_],
                                op=ALU.mult)
            ge_rsq2 = S.emit("gpsimd", _rsq2, n=1)
            S.wait("tensor", "ge", ge_rsq2)
            def _stats2(t):
                for cj in range(CJ):
                    t.matmul(psum[4][0:1, :], onescol_bf[:], rb_bf[:, cj],
                             start=(cj == 0), stop=(cj == CJ - 1))
                last_i = None
                for cj in range(CJ):
                    last_i = t.matmul(psum[5][0:1, :], onescol_bf[:], rsq_bf[:, cj],
                                      start=(cj == 0), stop=(cj == CJ - 1))
                last_i.then_inc(sems["te"], 1)
            S.emit("tensor", _stats2)
            te_stats2 = S.inc("te", 1)
            hq["te_stats2"] = te_stats2
            S.wait("vector", "te", te_stats2)
            def _strip2(v, so=so):
                v.tensor_scalar(out=stripL[0:1, so + 0, :], in0=psum[4][0:1, :],
                                scalar1=1.0 / C, scalar2=None, op0=ALU.mult)
                v.tensor_tensor(out=stripL[0:1, so + 1, :], in0=stripL[0:1, so + 0, :],
                                in1=stripL[0:1, so + 0, :], op=ALU.mult)
                v.scalar_tensor_tensor(out=stripL[0:1, so + 2, :], in0=psum[5][0:1, :],
                                       scalar=1.0 / C, in1=stripL[0:1, so + 1, :],
                                       op0=ALU.mult, op1=ALU.subtract)
            ve_var2 = S.emit("vector", _strip2, n=3)
            S.wait("scalar", "ve", ve_var2)
            def _rs2(s, so=so):
                s.activation(out=stripL[0:1, so + 3, :], in_=stripL[0:1, so + 2, :],
                             func=AF.Ln, bias=EPS)
                s.activation(out=stripL[0:1, so + 4, :], in_=stripL[0:1, so + 3, :],
                             func=AF.Exp, scale=-0.5)
            se_rs2 = S.emit("scalar", _rs2, n=2)
            S.wait("vector", "se", se_rs2)
            def _nm2(v, so=so):
                v.scalar_tensor_tensor(out=stripL[0:1, so + 5, :], in0=stripL[0:1, so + 0, :],
                                       scalar=-1.0, in1=stripL[0:1, so + 4, :],
                                       op0=ALU.mult, op1=ALU.mult)
            ve_nm2 = S.emit("vector", _nm2, n=1)
            S.wait("tensor", "ve", ve_nm2)
            def _bc2(t, so=so):
                t.matmul(psum[4][:], onesrow_f[:], stripL[0:1, so + 5, :],
                         start=True, stop=True)
                t.matmul(psum[5][:], onesrow_f[:], stripL[0:1, so + 4, :],
                         start=True, stop=True).then_inc(sems["te"], 1)
            S.emit("tensor", _bc2)
            te_bc2 = S.inc("te", 1)
            S.wait("vector", "te", te_bc2)
            def _happ2(v, ls=ls, s_=s_):
                v.tensor_tensor(out=h_buf[:, :, 1:TC + 1], in0=r_sb[:, s_],
                                in1=psum[4][:].unsqueeze(1).broadcast_to([128, CJ, TC]),
                                op=ALU.add)
                v.tensor_tensor(out=h_buf[:, :, 1:TC + 1], in0=h_buf[:, :, 1:TC + 1],
                                in1=psum[5][:].unsqueeze(1).broadcast_to([128, CJ, TC]),
                                op=ALU.mult)
                for cj in range(CJ):
                    v.tensor_scalar(out=h_buf[:, cj, 1:TC + 1], in0=h_buf[:, cj, 1:TC + 1],
                                    scalar1=vec_sb[:, ls, cj, V_LN2W:V_LN2W + 1],
                                    scalar2=vec_sb[:, ls, cj, V_LN2B:V_LN2B + 1],
                                    op0=ALU.mult, op1=ALU.add)
            hq["ve_h2"] = S.emit("vector", _happ2, n=2 + CJ)
            S.wait("vector", "ge", ge_hsav)
            def _dd2(v):
                v.tensor_tensor(out=drb[:], in0=h_buf[:, :, 1:TC + 1],
                                in1=h_buf[:, :, 0:TC], op=ALU.subtract)
            S.emit("vector", _dd2, n=1)
            if h1:
                for m in (h1["xk2_mark"], h1["xr2_mark"]):
                    S.wait("vector", m[0], m[1])
            def _mix2(v, ls=ls):
                for cj in range(CJ):
                    v.scalar_tensor_tensor(out=xm_a[:, cj], in0=drb[:, cj],
                                           scalar=vec_sb[:, ls, cj, V_CMK:V_CMK + 1],
                                           in1=h_buf[:, cj, 0:TC],
                                           op0=ALU.mult, op1=ALU.add)
                for cj in range(CJ):
                    v.scalar_tensor_tensor(out=xm_b[:, cj], in0=drb[:, cj],
                                           scalar=vec_sb[:, ls, cj, V_CMR:V_CMR + 1],
                                           in1=h_buf[:, cj, 0:TC],
                                           op0=ALU.mult, op1=ALU.add)
            ve_mix2 = S.emit("vector", _mix2, n=2 * CJ)
            hq["ve_mix2"] = ve_mix2
            S.wait("gpsimd", "ve", hq["ve_h2"])
            def _hsav2(g):
                g.tensor_copy(out=hb2_sav[:], in_=h_buf[:, :, TC:TC + 1])
            ge_hs2 = S.emit("gpsimd", _hsav2, n=1)
            ve_hs2 = ve_mix2

            S.wait("scalar", "ve", ve_mix2)
            for nmm in (f"rst{s_}", "xk2s", "xr2s"):
                S.use_sem(nmm)
            def _st(s, q=q, s_=s_):
                s.dma_start(out=hb["r_v"][:, :, q * TC:q * TC + TC],
                            in_=r_sb[:, s_]).then_inc(sems[f"rst{s_}"], 16)
                s.dma_start(out=hb["xk2_v"][:, :, q * TC:q * TC + TC],
                            in_=xm_a[:]).then_inc(sems["xk2s"], 16)
                s.dma_start(out=hb["xr2_v"][:, :, q * TC:q * TC + TC],
                            in_=xm_b[:]).then_inc(sems["xr2s"], 16)
            S.emit("scalar", _st)
            r_marks[q] = (f"rst{s_}", S.inc(f"rst{s_}", 16))
            xk2_marks[q] = ("xk2s", S.inc("xk2s", 16))
            xr2_marks[q] = ("xr2s", S.inc("xr2s", 16))
            hq["r_mark"] = r_marks[q]
            hq["xk2_mark"] = xk2_marks[q]
            hq["xr2_mark"] = xr2_marks[q]
            hq["ve_end"] = ve_hs2
            hq["se_end"] = S.ctr["se"]
            hq["ge_end"] = S.ctr["ge"]

        S.barrier()
        if stop_after == f"tm{l}":
            return

        wck_t = []
        for piece in range(4):
            sem = S.use_sem("wq" + str(piece))
            wck_t.append(dma_load(sem, (lambda g, piece=piece, l=l, sem=sem:
                g.dma_start(out=wck_v[:, piece * 2:piece * 2 + 2, :],
                            in_=hb["wckT"][l].rearrange("(cj p) f -> p cj f", p=128)[:, piece * 2:piece * 2 + 2, :]
                            ).then_inc(sems[sem], 16)), eng="gpsimd"))
        semcr = S.use_sem("wq4")
        wq_cr = dma_load(semcr, (lambda g, l=l:
            g.dma_start(out=wcrP[:],
                        in_=hb["wcrT"][l].rearrange("(cj p) o -> p cj o", p=128)
                        ).then_inc(sems["wq4"], 16)), eng="gpsimd")

        prev_kfst = [("te", 0)] * 2
        prev_xk2_mm = [0, 0]
        prev_sekf_last = 0
        prev_src_mm = 0
        prev_src_se = 0
        prev_xr2_mm = 0
        for q in range(NT):
            sl = q % 2
            x_sem = S.use_sem(f"xk2l{sl}")
            gates = [xk2_marks[q]]
            if prev_xk2_mm[sl]:
                gates.append(("te", prev_xk2_mm[sl]))
            xl_t = dma_load(x_sem, (lambda sy, q=q, sl=sl, x_sem=x_sem:
                sy.dma_start(out=xk2in[:, sl], in_=hb["xk2_v"][:, :, q * TC:q * TC + TC]
                             ).then_inc(sems[x_sem], 16)), gate=gates)
            xr_sem = S.use_sem("xr2lA")
            gates = [xr2_marks[q]]
            if prev_xr2_mm:
                gates.append(("te", prev_xr2_mm))
            xr_t = dma_load(xr_sem, (lambda sy, q=q:
                sy.dma_start(out=xr2inA[:], in_=hb["xr2_v"][:, :, q * TC:q * TC + TC]
                             ).then_inc(sems["xr2lA"], 16)), gate=gates)

            if q == 0:
                S.wait("tensor", "wq4", wq_cr)
            if q > 0 and src_marks[q - 1]:
                S.wait("scalar", src_marks[q - 1][0], src_marks[q - 1][1])
            S.wait("tensor", "xr2lA", xr_t)
            se_src = {}
            te_src = {}
            for oj in range(CJ):
                bsl = oj % 2
                if oj >= 2:
                    S.wait("tensor", "se", se_src[oj - 2])
                elif prev_src_se:
                    S.wait("tensor", "se", prev_src_se)
                def _srmm(t, oj=oj, bsl=bsl):
                    last_i = None
                    for cj in range(CJ):
                        last_i = t.matmul(psum[4 + bsl][:], wcrP[:, cj, oj * 128:oj * 128 + 128],
                                          xr2inA[:, cj], start=(cj == 0), stop=(cj == CJ - 1))
                    last_i.then_inc(sems["te"], 1)
                S.emit("tensor", _srmm)
                te_src[oj] = S.inc("te", 1)
                S.wait("scalar", "te", te_src[oj])
                def _srs(s, oj=oj, bsl=bsl):
                    s.activation(out=srtA[:], in_=psum[4 + bsl][:], func=AF.Exp, scale=-1.0)
                    s.activation(out=srtA[:], in_=srtA[:], func=AF.Ln, bias=1.0)
                    s.activation(out=srcA[:, oj], in_=srtA[:], func=AF.Exp, scale=-1.0)
                se_src[oj] = S.emit("scalar", _srs, n=3)
            prev_xr2_mm = te_src[CJ - 1]
            prev_src_se = se_src[CJ - 1]
            S.use_sem("srcs")
            S.wait("scalar", "se", se_src[CJ - 1])
            def _stsrc(s, q=q):
                s.dma_start(out=hb["src_v"][:, :, q * TC:q * TC + TC],
                            in_=srcA[:]).then_inc(sems["srcs"], 16)
            S.emit("scalar", _stsrc)
            src_marks[q] = ("srcs", S.inc("srcs", 16))

            te_kf = {}
            se_kf = {}
            for fj in range(FJ):
                bsl = fj % 2
                grp = fj // 4
                gsl = grp % 2
                if fj == 0:
                    S.wait("tensor", x_sem, xl_t)
                    if q == 0:
                        for piece in range(4):
                            S.wait("tensor", "wq" + str(piece), wck_t[piece])
                    if prev_sekf_last:
                        S.wait("tensor", "se", prev_sekf_last)
                if fj >= 2:
                    S.wait("tensor", "se", se_kf[fj - 2])
                def _kfmm(t, fj=fj, bsl=bsl, sl=sl):
                    last_i = None
                    for cj in range(CJ):
                        last_i = t.matmul(psum[bsl][:], wck_v[:, cj, fj * 128:fj * 128 + 128],
                                          xk2in[:, sl, cj], start=(cj == 0), stop=(cj == CJ - 1))
                    last_i.then_inc(sems["te"], 1)
                S.emit("tensor", _kfmm)
                te_kf[fj] = S.inc("te", 1)
                S.wait("scalar", "te", te_kf[fj])
                if fj % 4 == 0 and prev_kfst[gsl][1]:
                    S.wait("scalar", prev_kfst[gsl][0], prev_kfst[gsl][1])
                def _kfact(s, fj=fj, bsl=bsl, gsl=gsl):
                    s.activation(out=relu_scr[:, bsl], in_=psum[bsl][:], func=AF.Relu)
                    s.activation(out=kfst[:, gsl, fj % 4], in_=relu_scr[:, bsl], func=AF.Square)
                se_kf[fj] = S.emit("scalar", _kfact, n=2)
                if fj % 4 == 3:
                    kf_sem = S.use_sem(f"kfs{gsl}")
                    S.wait("scalar", "se", se_kf[fj])
                    def _kfstore(s, q=q, grp=grp, gsl=gsl, kf_sem=kf_sem):
                        s.dma_start(out=hb["kf_v"][:, grp * 4:grp * 4 + 4, q * TC:q * TC + TC],
                                    in_=kfst[:, gsl]).then_inc(sems[kf_sem], 16)
                    S.emit("scalar", _kfstore)
                    kf_marks[q][grp] = (kf_sem, S.inc(kf_sem, 16))
                    prev_kfst[gsl] = kf_marks[q][grp]
            prev_sekf_last = se_kf[FJ - 1]
            prev_xk2_mm[sl] = te_kf[FJ - 1]

        S.barrier()
        if stop_after == f"cma{l}":
            return

        for piece in range(4):
            sem = S.use_sem("wq" + str(piece))
            dma_load(sem, (lambda g, piece=piece, l=l, sem=sem:
                g.dma_start(out=wcv_v[:, piece * 8:piece * 8 + 8, :],
                            in_=hb["wcvT"][l].rearrange("(fj p) c -> p fj c", p=128)[:, piece * 8:piece * 8 + 8, :]
                            ).then_inc(sems[sem], 16)), eng="gpsimd")
        wcv_t = {piece: S.ctr["wq" + str(piece)] for piece in range(4)}

        last = (l == L - 1)
        prev_ve_done = 0
        prev_se_done = 0
        prev_rstB = None
        prev_rT = None
        prev_kv_last = 0
        prev_src_mm = 0
        for q in range(NT):
            sl = q % 2
            kf_l = []
            for piece in range(4):
                k_sem = S.use_sem("kfl")
                gates = [kf_marks[q][piece * 2], kf_marks[q][piece * 2 + 1]]
                if prev_kv_last:
                    gates.append(("te", prev_kv_last))
                kf_l.append(dma_load(k_sem, (lambda sy, q=q, piece=piece:
                    sy.dma_start(out=kfin[:, piece * 8:piece * 8 + 8, :],
                                 in_=hb["kf_v"][:, piece * 8:piece * 8 + 8, q * TC:q * TC + TC]
                                 ).then_inc(sems["kfl"], 16)), gate=gates))
            rl_sem = S.use_sem(f"rld{sl}")
            gates = [r_marks[q]]
            if prev_ve_done:
                gates.append(("ve", prev_ve_done))
            if prev_rstB:
                gates.append(prev_rstB)
            if prev_rT:
                gates.append(prev_rT)
            rl_t = dma_load(rl_sem, (lambda sy, q=q, sl=sl, rl_sem=rl_sem:
                sy.dma_start(out=r_sb[:, sl], in_=hb["r_v"][:, :, q * TC:q * TC + TC]
                             ).then_inc(sems[rl_sem], 16)), gate=gates)
            sc_sem = S.use_sem("srcl")
            gates = [src_marks[q]]
            if prev_src_mm:
                gates.append(("ve", prev_src_mm))
            sc_t = dma_load(sc_sem, (lambda sy, q=q:
                sy.dma_start(out=src_bf[:], in_=hb["src_v"][:, :, q * TC:q * TC + TC]
                             ).then_inc(sems["srcl"], 16)), gate=gates)

            ve_radd = {}
            te_kv = {}
            for cj in range(CJ):
                bsl = cj % 2
                if cj == 0:
                    S.wait("tensor", "kfl", kf_l[-1])
                    if q == 0:
                        for piece in range(4):
                            S.wait("tensor", "wq" + str(piece), wcv_t[piece])
                if cj >= 2:
                    S.wait("tensor", "ve", ve_radd[cj - 2])
                elif q > 0:
                    S.wait("tensor", "ve", prev_radd[cj + CJ - 2])
                def _kvmm(t, cj=cj, bsl=bsl):
                    last_i = None
                    for fj in range(FJ):
                        last_i = t.matmul(psum[bsl][:], wcv_v[:, fj, cj * 128:cj * 128 + 128],
                                          kfin[:, fj], start=(fj == 0), stop=(fj == FJ - 1))
                    last_i.then_inc(sems["te"], 1)
                S.emit("tensor", _kvmm)
                te_kv[cj] = S.inc("te", 1)
                S.wait("vector", "te", te_kv[cj])
                if cj == 0:
                    S.wait("vector", f"rld{sl}", rl_t)
                    S.wait("vector", "srcl", sc_t)
                def _kvadd(v, cj=cj, bsl=bsl, sl=sl):
                    v.tensor_tensor(out=kvscr[:], in0=src_bf[:, cj], in1=psum[bsl][:],
                                    op=ALU.mult)
                    v.tensor_tensor(out=r_sb[:, sl, cj], in0=r_sb[:, sl, cj], in1=kvscr[:],
                                    op=ALU.add)
                ve_radd[cj] = S.emit("vector", _kvadd, n=2)
            prev_kv_last = te_kv[CJ - 1]
            prev_ve_done = ve_radd[CJ - 1]
            prev_src_mm = ve_radd[CJ - 1]
            prev_radd = ve_radd

            if not last:
                S.wait("scalar", "ve", ve_radd[CJ - 1])
                S.use_sem(f"rst{sl}")
                def _stR(s, q=q, sl=sl):
                    s.dma_start(out=hb["r_v"][:, :, q * TC:q * TC + TC],
                                in_=r_sb[:, sl]).then_inc(sems[f"rst{sl}"], 16)
                S.emit("scalar", _stR)
                r_marks[q] = (f"rst{sl}", S.inc(f"rst{sl}", 16))
                prev_rstB = r_marks[q]
            else:
                rT_stage = carve(pool, 11776, [128, 4, 1024])
                te_tr = {}
                ve_evr = {}
                for cj in range(CJ):
                    S.wait("tensor", "ve", ve_radd[cj])
                    if cj > 0:
                        S.wait("tensor", "ve", ve_evr[cj - 1])
                    if prev_rT and cj == 0:
                        S.wait("tensor", prev_rT[0], prev_rT[1])
                    def _trr(t, cj=cj, sl=sl):
                        last_i = None
                        for tt in range(4):
                            last_i = t.transpose(psum[6][:, tt * 128:tt * 128 + 128],
                                                 r_sb[:, sl, cj, tt * 128:tt * 128 + 128],
                                                 ident32[:])
                        last_i.then_inc(sems["te"], 1)
                    S.emit("tensor", _trr)
                    te_tr[cj] = S.inc("te", 1)
                    S.wait("vector", "te", te_tr[cj])
                    def _evr(v, cj=cj):
                        v.tensor_copy(out=rT_stage[:, :, cj * 128:cj * 128 + 128],
                                      in_=psum[6][:].rearrange("p (a b) -> p a b", b=128))
                    ve_evr[cj] = S.emit("vector", _evr, n=1)
                S.wait("scalar", "ve", ve_evr[CJ - 1])
                rT_sem = S.use_sem("rTs")
                def _stT(s, q=q):
                    s.dma_start(out=hb["rT_v"][q * 4:q * 4 + 4].rearrange("a p c -> p a c"),
                                in_=rT_stage[:]).then_inc(sems["rTs"], 16)
                S.emit("scalar", _stT)
                prev_rT = ("rTs", S.inc("rTs", 16))

        S.barrier()
        if stop_after == f"cm{l}":
            return
        region_free[0] = ("te", prev_kv_last)
        region_free[1] = ("te", prev_kv_last)
        region_free[2] = ("te", prev_kv_last)
        region_free[3] = ("te", prev_kv_last)
        region_free[4] = ("te", 0)

    if stop_after == "layers":
        return
    _emit_ffn(nc, S, hb, psum, psbf, ident32, identbf, T, arena, pool,
              r_sb, h_buf, carve)


def emit_quant(S, q, src, dst, width, pregate=()):
    G = width // 16
    for eng, sem, tgt in pregate:
        if tgt:
            S.wait(eng, sem, tgt)

    def _v1(v, src=src):
        v.tensor_reduce(out=q["bm"][:, 0:G], in_=src.rearrange("p (g k) -> p g k", k=16),
                        axis=AX.X, op=ALU.max, apply_absolute_value=True)
        v.tensor_scalar(out=q["sc"][:, 0:G], in0=q["bm"][:, 0:G], scalar1=1.0 / 6.0,
                        scalar2=1e-12, op0=ALU.mult, op1=ALU.max)
        v.reciprocal(out=q["iv"][:, 0:G], in_=q["sc"][:, 0:G])
        v.tensor_tensor(out=q["y"][:, 0:width].rearrange("p (g k) -> p g k", k=16),
                        in0=src.rearrange("p (g k) -> p g k", k=16),
                        in1=q["iv"][:, 0:G].unsqueeze(2).broadcast_to([128, G, 16]),
                        op=ALU.mult)
    ve_y = S.emit("vector", _v1, n=4)
    S.wait("scalar", "ve", ve_y)

    def _s1(s):
        s.activation(out=q["a"][:, 0:width], in_=q["y"][:, 0:width], func=AF.Abs)
    se_a = S.emit("scalar", _s1, n=1)

    def _v2(v):
        v.tensor_scalar(out=q["qA"][:, 0:width], in0=q["y"][:, 0:width], scalar1=2.0,
                        scalar2=RNE, op0=ALU.mult, op1=ALU.add)
        v.tensor_scalar(out=q["qA"][:, 0:width], in0=q["qA"][:, 0:width], scalar1=RNE,
                        scalar2=0.5, op0=ALU.subtract, op1=ALU.mult)
        v.tensor_scalar(out=q["qB"][:, 0:width], in0=q["y"][:, 0:width], scalar1=RNE,
                        scalar2=RNE, op0=ALU.add, op1=ALU.subtract)
        v.tensor_scalar(out=q["qC"][:, 0:width], in0=q["y"][:, 0:width], scalar1=0.5,
                        scalar2=RNE, op0=ALU.mult, op1=ALU.add)
        v.tensor_scalar(out=q["qC"][:, 0:width], in0=q["qC"][:, 0:width], scalar1=RNE,
                        scalar2=2.0, op0=ALU.subtract, op1=ALU.mult)
    S.emit("vector", _v2, n=5)
    S.wait("vector", "se", se_a)

    def _v3(v, dst=dst):
        v.tensor_single_scalar(out=q["m1"][:, 0:width], in_=q["a"][:, 0:width],
                               scalar=2.25, op=ALU.is_lt)
        v.tensor_single_scalar(out=q["m2"][:, 0:width], in_=q["a"][:, 0:width],
                               scalar=4.5, op=ALU.is_lt)
        v.tensor_copy(out=q["y"][:, 0:width], in_=q["qC"][:, 0:width])
        v.copy_predicated(out=q["y"][:, 0:width], mask=q["m2"][:, 0:width],
                          data=q["qB"][:, 0:width])
        v.copy_predicated(out=q["y"][:, 0:width], mask=q["m1"][:, 0:width],
                          data=q["qA"][:, 0:width])
        v.tensor_tensor(out=dst.rearrange("p (g k) -> p g k", k=16),
                        in0=q["y"][:, 0:width].rearrange("p (g k) -> p g k", k=16),
                        in1=q["sc"][:, 0:G].unsqueeze(2).broadcast_to([128, G, 16]),
                        op=ALU.mult)
    return S.emit("vector", _v3, n=6)


def _emit_ffn(nc, S, hb, psum, psbf, ident32, identbf, T, arena, pool,
              r_sb, h_buf, carve):
    TT = T // 128
    sems = S.sems
    NJ = [(n0, min(512, II - n0)) for n0 in range(0, II, 512)]
    QW = 1408

    rflat = r_sb[:].rearrange("p a b c -> p (a b c)")
    hflat = h_buf[:].rearrange("p a b -> p (a b)")

    wgate_q = carve(arena, 0, [128, CJ, II], BF16)
    wup_main = carve(arena, CJ * II, [128, 6, II], BF16)
    wup_tail = carve(pool, 0, [128, 2, II], BF16)

    def wup_rhs(cj, n0, nw):
        if cj < 6:
            return wup_main[:, cj, n0:n0 + nw]
        return wup_tail[:, cj - 6, n0:n0 + nw]

    q = {"bm": carve(pool, 2816, [128, QW // 16]),
         "sc": carve(pool, 2904, [128, QW // 16]),
         "iv": carve(pool, 2992, [128, QW // 16]),
         "y": carve(pool, 3080, [128, QW]),
         "a": carve(pool, 4488, [128, QW]),
         "qA": carve(pool, 5896, [128, QW]),
         "qB": carve(pool, 7304, [128, QW]),
         "qC": carve(pool, 8712, [128, QW]),
         "m1": carve(pool, 10120, [128, QW], U8),
         "m2": carve(pool, 10472, [128, QW], U8)}
    gate_s = carve(pool, 10824, [128, 2, 512])
    h_f = carve(rflat, 4096, [128, II])
    hq_bf = carve(pool, 11848, [128, 2, II], BF16)
    wtile = carve(pool, 10824, [128, 2, 1024])
    wq_bf = carve(pool, 12872, [128, 2, 1024], BF16)
    xqT_in = carve(rflat, 0, [128, 2, CJ, 512], BF16)
    hqTst = carve(hflat, 0, [128, 2, IJ, 128], BF16)
    wtileD = carve(pool, 10824, [128, II])
    wq_bfD = carve(rflat, 0, [128, 2, II], BF16)
    hqT_in = carve(rflat, 2816, [128, 2, IJ, 128], BF16)
    rx_in = carve(hflat, 0, [128, 2, 2, 1024])
    out_t = carve(pool, 0, [128, 2, 1024])

    ve_wq = {}
    te_wtr = {}
    se_wev = {}
    for gi in range(2 * IJ):
        wi, tile = divmod(gi, IJ)
        sl = gi % 2
        wl_sem = S.use_sem(f"fwl{sl}")
        prev = ve_wq.get(gi - 2)
        if prev:
            S.wait("sync", "ve", prev)
        w_h = hb["wg"] if wi == 0 else hb["wu"]
        wl_t = S.dma("sync", wl_sem, (lambda sy, w_h=w_h, tile=tile, sl=sl, wl_sem=wl_sem:
            sy.dma_start(out=wtile[:, sl],
                         in_=w_h[tile * 128:tile * 128 + 128, :]
                         ).then_inc(sems[wl_sem], 16)))
        gates = [("vector", wl_sem, wl_t)]
        prev_tr = te_wtr.get(gi - 2)
        if prev_tr:
            gates.append(("vector", "te", prev_tr))
        ve_wq[gi] = emit_quant(S, q, wtile[:, sl], wq_bf[:, sl], 1024,
                               pregate=gates)
        S.wait("tensor", "ve", ve_wq[gi])
        if (gi - 1) in se_wev:
            S.wait("tensor", "se", se_wev[gi - 1])
        def _wtr(t, sl=sl):
            last_i = None
            for cj in range(CJ):
                last_i = t.transpose(psbf[:, cj * 128:cj * 128 + 128],
                                     wq_bf[:, sl, cj * 128:cj * 128 + 128], identbf[:])
            last_i.then_inc(sems["te"], 1)
        S.emit("tensor", _wtr)
        te_wtr[gi] = S.inc("te", 1)
        S.wait("scalar", "te", te_wtr[gi])

        if wi == 0:
            def _wev0(s, tile=tile):
                s.copy(out=wgate_q[:, :, tile * 128:tile * 128 + 128],
                       in_=psbf[:].rearrange("p (a b) -> p a b", b=128))
            se_wev[gi] = S.emit("scalar", _wev0, n=1)
        else:
            def _wev1(s, tile=tile):
                s.copy(out=wup_main[:, 0:6, tile * 128:tile * 128 + 128],
                       in_=psbf[:, 0:768].rearrange("p (a b) -> p a b", b=128))
                s.copy(out=wup_tail[:, :, tile * 128:tile * 128 + 128],
                       in_=psbf[:, 768:1024].rearrange("p (a b) -> p a b", b=128))
            se_wev[gi] = S.emit("scalar", _wev1, n=2)

    S.barrier()

    hqs_marks = [None] * TT
    prev_hqst = [None, None]
    prev_silu_last = 0
    prev_h_last = 0
    prev_hev_last = [0]
    hq_ve = [0, 0]
    hq_tr_done = [0, 0]
    xq_t = {}
    xq_mm_last = [0, 0]
    for i in range(TT):
        g4 = i // 4
        xsl = g4 % 2
        tsub = i % 4
        if tsub == 0:
            xq_sem = S.use_sem(f"fxq{xsl}")
            if g4 >= 2:
                S.wait("sync", "te", xq_mm_last[xsl])
            xq_t[g4] = S.dma("sync", xq_sem, (lambda sy, g4=g4, xsl=xsl, xq_sem=xq_sem:
                sy.dma_start(out=xqT_in[:, xsl],
                             in_=hb["xq_v"][:, :, g4 * 512:g4 * 512 + 512]
                             ).then_inc(sems[xq_sem], 16)))

        te_g = {}
        se_silu = {}
        ve_hpart = {}
        for nj, (n0, nw) in enumerate(NJ):
            bsl = nj % 2
            if nj == 0 and tsub == 0:
                S.wait("tensor", f"fxq{xsl}", xq_t[g4])
            if nj >= 2:
                S.wait("tensor", "se", se_silu[nj - 2])
                S.wait("tensor", "ve", ve_hpart[nj - 2])
            elif i > 0:
                S.wait("tensor", "se", prev_silu_last)
                S.wait("tensor", "ve", prev_h_last)
            def _gmm(t, n0=n0, nw=nw, bsl=bsl, xsl=xsl, tsub=tsub):
                last_i = None
                for cj in range(CJ):
                    last_i = t.matmul(psum[bsl][:, 0:nw],
                                      xqT_in[:, xsl, cj, tsub * 128:tsub * 128 + 128],
                                      wgate_q[:, cj, n0:n0 + nw],
                                      start=(cj == 0), stop=(cj == CJ - 1))
                for cj in range(CJ):
                    last_i = t.matmul(psum[2 + bsl][:, 0:nw],
                                      xqT_in[:, xsl, cj, tsub * 128:tsub * 128 + 128],
                                      wup_rhs(cj, n0, nw),
                                      start=(cj == 0), stop=(cj == CJ - 1))
                last_i.then_inc(sems["te"], 1)
            S.emit("tensor", _gmm)
            te_g[nj] = S.inc("te", 1)
            S.wait("scalar", "te", te_g[nj])
            if nj >= 2:
                S.wait("scalar", "ve", ve_hpart[nj - 2])
            elif i > 0:
                S.wait("scalar", "ve", prev_h_last)
            def _silu(s, n0=n0, nw=nw, bsl=bsl):
                s.activation(out=gate_s[:, bsl, 0:nw], in_=psum[bsl][:, 0:nw], func=AF.Silu)
            se_silu[nj] = S.emit("scalar", _silu, n=1)
            S.wait("vector", "se", se_silu[nj])
            S.wait("vector", "te", te_g[nj])
            def _hp(v, n0=n0, nw=nw, bsl=bsl):
                v.tensor_tensor(out=h_f[:, n0:n0 + nw], in0=gate_s[:, bsl, 0:nw],
                                in1=psum[2 + bsl][:, 0:nw], op=ALU.mult)
            ve_hpart[nj] = S.emit("vector", _hp, n=1)
        prev_silu_last = se_silu[len(NJ) - 1]
        prev_h_last = ve_hpart[len(NJ) - 1]

        hsl = i % 2
        gates = [("vector", "ve", ve_hpart[len(NJ) - 1])]
        if i >= 2 and hq_tr_done[hsl]:
            gates.append(("vector", "te", hq_tr_done[hsl]))
        ve_q1 = emit_quant(S, q, h_f[:, 0:QW], hq_bf[:, hsl, 0:QW], QW,
                           pregate=gates)
        ve_q2 = emit_quant(S, q, h_f[:, QW:II], hq_bf[:, hsl, QW:II], II - QW)
        hq_ve[hsl] = ve_q2

        def _emit_tr(ix):
            hslx = ix % 2
            se_hev = {}
            if prev_hqst[hslx]:
                S.wait("scalar", prev_hqst[hslx][0], prev_hqst[hslx][1])
            te_htr = None
            for rnd in range(3):
                lo = rnd * 8
                hi = min(lo + 8, IJ)
                S.wait("tensor", "ve", hq_ve[hslx])
                if rnd > 0:
                    S.wait("tensor", "se", se_hev[rnd - 1])
                elif prev_hev_last[0]:
                    S.wait("tensor", "se", prev_hev_last[0])
                def _htr(t, lo=lo, hi=hi, hslx=hslx):
                    last_i = None
                    for ij in range(lo, hi):
                        last_i = t.transpose(psbf[:, (ij - lo) * 128:(ij - lo) * 128 + 128],
                                             hq_bf[:, hslx, ij * 128:ij * 128 + 128],
                                             identbf[:])
                    last_i.then_inc(sems["te"], 1)
                S.emit("tensor", _htr)
                te_htr = S.inc("te", 1)
                S.wait("scalar", "te", te_htr)
                def _hev(s, lo=lo, hi=hi, hslx=hslx):
                    s.copy(out=hqTst[:, hslx, lo:hi, :],
                           in_=psbf[:, 0:(hi - lo) * 128].rearrange("p (a b) -> p a b", b=128))
                se_hev[rnd] = S.emit("scalar", _hev, n=1)
            prev_hev_last[0] = se_hev[2]
            hq_tr_done[hslx] = te_htr
            hq_sem = S.use_sem(f"fhq{hslx}")
            S.wait("scalar", "se", se_hev[2])
            def _hst(s, ix=ix, hslx=hslx, hq_sem=hq_sem):
                s.dma_start(out=hb["hqT_v"][:, :, ix * 128:ix * 128 + 128],
                            in_=hqTst[:, hslx]).then_inc(sems[hq_sem], 16)
            S.emit("scalar", _hst)
            hqs_marks[ix] = (hq_sem, S.inc(hq_sem, 16))
            prev_hqst[hslx] = hqs_marks[ix]

        if i >= 1:
            _emit_tr(i - 1)
        if i == TT - 1:
            _emit_tr(i)
        if tsub == 3:
            xq_mm_last[xsl] = te_g[len(NJ) - 1]

    S.barrier()

    wdown_full = carve(arena, 0, [128, IJ, 1024], BF16)

    def wdown_q_view(tile, lo, hi):
        return wdown_full[:, lo:hi, tile * 128:tile * 128 + 128]

    ve_wqd = {}
    te_wtrd = {}
    se_wevd = {}
    for tile in range(CJ):
        sl = tile % 2
        wl_sem = S.use_sem("fwl0")
        prev = ve_wqd.get(tile - 1)
        if prev:
            S.wait("sync", "ve", prev)
        wl_t = S.dma("sync", wl_sem, (lambda sy, tile=tile, wl_sem=wl_sem:
            sy.dma_start(out=wtileD[:],
                         in_=hb["wd"][tile * 128:tile * 128 + 128, :]
                         ).then_inc(sems[wl_sem], 16)))
        gates = [("vector", wl_sem, wl_t)]
        prev_tr = te_wtrd.get(tile - 2)
        if prev_tr:
            gates.append(("vector", "te", prev_tr))
        emit_quant(S, q, wtileD[:, 0:QW], wq_bfD[:, sl, 0:QW], QW, pregate=gates)
        ve_wqd[tile] = emit_quant(S, q, wtileD[:, QW:II], wq_bfD[:, sl, QW:II],
                                  II - QW)
        S.wait("tensor", "ve", ve_wqd[tile])
        last_te = None
        for rnd in range(3):
            lo = rnd * 8
            hi = min(lo + 8, IJ)
            if rnd > 0 or (tile - 1) in se_wevd:
                S.wait("tensor", "se", se_wevd[tile - 1] if rnd == 0 else se_rnd_prev)
            def _wtrd(t, sl=sl, lo=lo, hi=hi):
                last_i = None
                for ij in range(lo, hi):
                    last_i = t.transpose(psbf[:, (ij - lo) * 128:(ij - lo) * 128 + 128],
                                         wq_bfD[:, sl, ij * 128:ij * 128 + 128], identbf[:])
                last_i.then_inc(sems["te"], 1)
            S.emit("tensor", _wtrd)
            te_w = S.inc("te", 1)
            S.wait("scalar", "te", te_w)
            def _wevd(s, tile=tile, lo=lo, hi=hi):
                s.copy(out=wdown_q_view(tile, lo, hi),
                       in_=psbf[:, 0:(hi - lo) * 128].rearrange("p (a b) -> p a b", b=128))
            se_rnd_prev = S.emit("scalar", _wevd, n=1)
        se_wevd[tile] = se_rnd_prev
        te_wtrd[tile] = te_w

    S.barrier()

    wdown_q = carve(arena, 0, [128, IJ, 1024], BF16)
    prev_out_st = [None, None]
    prev_dmm = [0, 0]
    prev_out_ve = [0, 0]
    prev_comb = [0, 0]
    for i in range(TT):
        sl = i % 2
        hl_sem = S.use_sem(f"fhl{sl}")
        gates = [hqs_marks[i]]
        if prev_dmm[sl]:
            gates.append(("te", prev_dmm[sl]))
        for gsem, gtgt in gates:
            if gtgt:
                S.wait("sync", gsem, gtgt)
        hl_t = S.dma("sync", hl_sem, (lambda sy, i=i, sl=sl, hl_sem=hl_sem:
            sy.dma_start(out=hqT_in[:, sl],
                         in_=hb["hqT_v"][:, :, i * 128:i * 128 + 128]
                         ).then_inc(sems[hl_sem], 16)))
        rx_sem = S.use_sem(f"frx{sl}")
        if prev_out_ve[sl]:
            S.wait("sync", "ve", prev_out_ve[sl])
        S.dma("sync", rx_sem, (lambda sy, i=i, sl=sl, rx_sem=rx_sem:
            sy.dma_start(out=rx_in[:, sl, 0], in_=hb["rT_v"][i]
                         ).then_inc(sems[rx_sem], 16)))
        rx_t = S.dma("sync", rx_sem, (lambda sy, i=i, sl=sl, rx_sem=rx_sem:
            sy.dma_start(out=rx_in[:, sl, 1], in_=hb["x_v"][i]
                         ).then_inc(sems[rx_sem], 16)))

        ve_o = {}
        for hj in range(2):
            bsl = hj
            S.wait("tensor", hl_sem, hl_t)
            if prev_comb[hj]:
                S.wait("tensor", "ve", prev_comb[hj])
            def _dmm(t, hj=hj, bsl=bsl, sl=sl):
                last_i = None
                for ij in range(IJ):
                    last_i = t.matmul(psum[4 + bsl][:], hqT_in[:, sl, ij],
                                      wdown_q[:, ij, hj * 512:hj * 512 + 512],
                                      start=(ij == 0), stop=(ij == IJ - 1))
                last_i.then_inc(sems["te"], 1)
            S.emit("tensor", _dmm)
            te_d = S.inc("te", 1)
            S.wait("vector", "te", te_d)
            S.wait("vector", rx_sem, rx_t)
            if prev_out_st[sl] and hj == 0:
                S.wait("vector", prev_out_st[sl][0], prev_out_st[sl][1])
            def _comb(v, hj=hj, bsl=bsl, sl=sl):
                v.tensor_tensor(out=out_t[:, sl, hj * 512:hj * 512 + 512],
                                in0=psum[4 + bsl][:],
                                in1=rx_in[:, sl, 0, hj * 512:hj * 512 + 512], op=ALU.add)
                v.tensor_tensor(out=out_t[:, sl, hj * 512:hj * 512 + 512],
                                in0=out_t[:, sl, hj * 512:hj * 512 + 512],
                                in1=rx_in[:, sl, 1, hj * 512:hj * 512 + 512], op=ALU.add)
            ve_o[hj] = S.emit("vector", _comb, n=2)
        prev_dmm[sl] = S.ctr["te"]
        prev_out_ve[sl] = ve_o[1]
        prev_comb[0] = ve_o[0]
        prev_comb[1] = ve_o[1]

        ost_sem = S.use_sem(f"ost{sl}")
        S.wait("scalar", "ve", ve_o[1])
        def _ost(s, i=i, sl=sl, ost_sem=ost_sem):
            s.dma_start(out=hb["out_v"][i], in_=out_t[:, sl]).then_inc(sems[ost_sem], 16)
        S.emit("scalar", _ost)
        prev_out_st[sl] = (ost_sem, S.inc(ost_sem, 16))


def _host_layout(p):
    ac = np.ascontiguousarray
    lay = {}
    lay["wkT"] = ac(np.transpose(p["L0_attn_Wk"], (0, 2, 1)))
    lay["wvT"] = ac(np.transpose(p["L0_attn_Wv"], (0, 2, 1)))
    lay["wrT"] = ac(np.transpose(p["L0_attn_Wr"], (0, 2, 1)))
    lay["wcrT"] = ac(np.transpose(p["L0_attn_Wcr"], (0, 2, 1)))
    lay["woT"] = ac(np.transpose(p["L0_attn_Wo"], (0, 2, 1)))
    lay["wckT"] = ac(np.transpose(p["L0_attn_Wck"], (0, 2, 1)))
    lay["wcvT"] = ac(np.transpose(p["L0_attn_Wcv"], (0, 2, 1)))
    lay["vecs"] = ac(np.stack([p["L0_attn_ln1_w"], p["L0_attn_ln1_b"],
                               p["L0_attn_ln2_w"], p["L0_attn_ln2_b"],
                               p["L0_attn_tmk"], p["L0_attn_tmv"], p["L0_attn_tmr"],
                               p["L0_attn_cmk"], p["L0_attn_cmr"],
                               p["L0_attn_time_decay"], p["L0_attn_time_first"]],
                              axis=-1))
    lay["ln0"] = ac(np.stack([p["L0_attn_ln0_w"], p["L0_attn_ln0_b"]], axis=0))
    lay["wg"] = ac(p["L0_ffn_w_gate"])
    lay["wu"] = ac(p["L0_ffn_w_up"])
    lay["wd"] = ac(p["L0_ffn_w_down"])
    return lay


def kernel(**inputs):
    x = np.asarray(inputs["L0_attn_x"])
    B = x.shape[0]
    lay = _host_layout({k: np.asarray(v) for k, v in inputs.items()})
    nc = build_module(T=x.shape[1], L=4)
    in_maps = []
    for b in range(B):
        m = dict(lay)
        m["x"] = np.ascontiguousarray(x[b])
        in_maps.append(m)
    res = run_bass_kernel_spmd(nc, in_maps, core_ids=list(range(B)))
    return np.stack([res.results[b]["out"] for b in range(B)], axis=0)


# version 07
# speedup vs baseline: 1.0466x; 1.0466x over previous
import math
import contextlib
import numpy as np
import concourse.bass as bass
import concourse.mybir as mybir
from concourse.bass_utils import run_bass_kernel_spmd

F32 = mybir.dt.float32
BF16 = mybir.dt.bfloat16
U8 = mybir.dt.uint8
AF = mybir.ActivationFunctionType
ALU = mybir.AluOpType
AX = mybir.AxisListType

C = 1024
CJ = 8
FF = 4096
FJ = 32
II = 2816
IJ = 22
TC = 512
EPS = 1e-5
RNE = 12582912.0

V_LN1W, V_LN1B, V_LN2W, V_LN2B, V_TMK, V_TMV, V_TMR, V_CMK, V_CMR, V_TD, V_TF = range(11)
NV = 11

GUARD_OPS = {
    "tensor_tensor", "tensor_scalar", "tensor_copy", "tensor_add", "tensor_sub",
    "tensor_mul", "tensor_max", "tensor_relu", "tensor_single_scalar",
    "tensor_scalar_add", "tensor_scalar_sub", "tensor_scalar_mul",
    "tensor_scalar_max", "tensor_scalar_min", "tensor_reduce",
    "scalar_tensor_tensor", "tensor_tensor_scan", "reciprocal", "select",
    "copy_predicated",
    "memset", "memzero", "iota", "affine_select", "activation", "copy",
    "add", "mul", "sign", "square", "sqrt",
}

ENG_SEM = {"sync": "te", "scalar": "se", "vector": "ve", "tensor": "te", "gpsimd": "ge"}


class Guard:

    def __init__(self, eng, sem, guard=True):
        self._e = eng
        self._sem = sem
        self._n = 0
        self._guard = guard

    def bump(self, k):
        self._n += k

    def __getattr__(self, name):
        attr = getattr(self._e, name)
        if self._guard and name in GUARD_OPS:
            def call(*a, **k):
                if self._n > 0:
                    self._e.wait_ge(self._sem, self._n)
                inst = attr(*a, **k)
                inst.then_inc(self._sem, 1)
                self._n += 1
                return inst
            return call
        return attr


class Sched:
    ENGINES = ("sync", "scalar", "vector", "tensor", "gpsimd")

    def __init__(self):
        self.streams = {e: [] for e in self.ENGINES}
        self.ctr = {}
        self.sems = {}
        self.sem_names = []
        for s in ("te", "ve", "se", "ge"):
            self.use_sem(s)

    def use_sem(self, name):
        if name not in self.ctr:
            self.ctr[name] = 0
            self.sem_names.append(name)
        return name

    def dma(self, eng, sem, fn):
        self.use_sem(sem)
        self.streams[eng].append(fn)
        return self.inc(sem, 16)

    def inc(self, name, amount):
        self.ctr[name] += amount
        return self.ctr[name]

    def emit(self, eng, fn, n=0):
        self.streams[eng].append(fn)
        if n:
            sem = ENG_SEM[eng]
            assert sem in ("ve", "se", "ge", "te")
            self.ctr[sem] += n
        return self.ctr[ENG_SEM[eng]]

    def wait(self, eng, sem, tgt):
        assert tgt <= self.ctr[sem], (eng, sem, tgt, self.ctr[sem])
        sems = self.sems

        def _w(e, sem=sem, tgt=tgt):
            e.wait_ge(sems[sem], tgt)
        self.streams[eng].append(_w)

    def barrier(self):
        marks = {sem: self.ctr[sem] for sem in self.sem_names}
        for eng in self.ENGINES:
            for sem, tgt in marks.items():
                if tgt:
                    self.wait(eng, sem, tgt)


def build_module(T=4096, L=4, taps=(), stop_after=None):
    NT = T // TC
    TT = T // 128
    nc = bass.Bass("TRN2", target_bir_lowering=False)
    S = Sched()

    def tk(name):
        return "ExternalOutput" if name in taps else "Internal"

    x_h = nc.dram_tensor("x", [T, C], F32, kind="ExternalInput").ap()
    wkT_h = nc.dram_tensor("wkT", [L, C, C], F32, kind="ExternalInput").ap()
    wvT_h = nc.dram_tensor("wvT", [L, C, C], F32, kind="ExternalInput").ap()
    wrT_h = nc.dram_tensor("wrT", [L, C, C], F32, kind="ExternalInput").ap()
    wcrT_h = nc.dram_tensor("wcrT", [L, C, C], F32, kind="ExternalInput").ap()
    woT_h = nc.dram_tensor("woT", [L, C, C], F32, kind="ExternalInput").ap()
    wckT_h = nc.dram_tensor("wckT", [L, C, FF], F32, kind="ExternalInput").ap()
    wcvT_h = nc.dram_tensor("wcvT", [L, FF, C], F32, kind="ExternalInput").ap()
    vecs_h = nc.dram_tensor("vecs", [L, C, NV], F32, kind="ExternalInput").ap()
    ln0_h = nc.dram_tensor("ln0", [2, C], F32, kind="ExternalInput").ap()
    wg_h = nc.dram_tensor("wg", [II, C], F32, kind="ExternalInput").ap()
    wu_h = nc.dram_tensor("wu", [II, C], F32, kind="ExternalInput").ap()
    wd_h = nc.dram_tensor("wd", [C, II], F32, kind="ExternalInput").ap()

    r_hbm = nc.dram_tensor("r_hbm", [C, T], F32, kind=tk("r_hbm")).ap()
    xq_hbm = nc.dram_tensor("xq_hbm", [C, T], BF16, kind=tk("xq_hbm")).ap()
    xk2_hbm = nc.dram_tensor("xk2_hbm", [C, T], BF16, kind=tk("xk2_hbm")).ap()
    xr2_hbm = nc.dram_tensor("xr2_hbm", [C, T], BF16, kind=tk("xr2_hbm")).ap()
    kf_hbm = nc.dram_tensor("kf_hbm", [FF, T], BF16, kind=tk("kf_hbm")).ap()
    rT_hbm = nc.dram_tensor("rT_hbm", [T, C], F32, kind=tk("rT_hbm")).ap()
    src_hbm = nc.dram_tensor("src_hbm", [C, T], BF16, kind=tk("src_hbm")).ap()
    hqT_hbm = nc.dram_tensor("hqT_hbm", [II, T], BF16, kind=tk("hqT_hbm")).ap()
    out_h = nc.dram_tensor("out", [T, C], F32, kind="ExternalOutput").ap()

    x_v = x_h.rearrange("(tt p) c -> tt p c", p=128)
    out_v = out_h.rearrange("(tt p) c -> tt p c", p=128)
    rT_v = rT_hbm.rearrange("(tt p) c -> tt p c", p=128)
    r_v = r_hbm.rearrange("(cj p) t -> p cj t", p=128)
    xq_v = xq_hbm.rearrange("(cj p) t -> p cj t", p=128)
    xk2_v = xk2_hbm.rearrange("(cj p) t -> p cj t", p=128)
    xr2_v = xr2_hbm.rearrange("(cj p) t -> p cj t", p=128)
    kf_vv = kf_hbm.rearrange("(fj p) t -> p fj t", p=128)
    src_vv = src_hbm.rearrange("(cj p) t -> p cj t", p=128)
    hqT_v = hqT_hbm.rearrange("(ij p) t -> p ij t", p=128)

    alloc = contextlib.ExitStack()
    e = alloc.enter_context

    ident32 = e(nc.sbuf_tensor("ident32", [128, 128], F32))
    identbf = e(nc.sbuf_tensor("identbf", [128, 128], BF16))
    onescol_bf = e(nc.sbuf_tensor("onescol_bf", [128, 1], BF16))
    onesrow_f = e(nc.sbuf_tensor("onesrow_f", [1, 128], F32))
    epsc = e(nc.sbuf_tensor("epsc", [128, 1], F32))
    nc.const_aps.register(F32, EPS, epsc.ap())
    psum = [e(nc.psum_tensor(f"ps{i}", [128, 512], F32)) for i in range(7)]
    psbf = e(nc.psum_tensor("psbf", [128, 1024], BF16))

    AR = 8192
    arena = e(nc.sbuf_tensor("w_arena", [128, 5 * AR], BF16))
    pool = e(nc.sbuf_tensor("poolL", [128, 15872], F32))
    r_sb_t = e(nc.sbuf_tensor("r_sb", [128, 2, CJ, TC], F32))
    h_buf_t = e(nc.sbuf_tensor("h_buf", [128, CJ, TC + 1], F32))

    def carve(base, off, shape, dtype=F32):
        n = math.prod(shape[1:])
        if dtype == F32 and base.dtype == F32:
            v = base[:, off:off + n]
        elif dtype == BF16 and base.dtype == F32:
            v = base[:, off:off + (n + 1) // 2].bitcast(BF16)
        elif dtype == BF16 and base.dtype == BF16:
            v = base[:, off:off + n]
        elif dtype == F32 and base.dtype == BF16:
            v = base[:, off:off + 2 * n].bitcast(F32)
        elif dtype == U8 and base.dtype == F32:
            v = base[:, off:off + (n + 3) // 4].bitcast(U8)
        else:
            raise ValueError((base.dtype, dtype))
        if len(shape) == 2:
            return v
        if len(shape) == 3:
            return v.rearrange("p (a b) -> p a b", b=shape[2])
        return v.rearrange("p (a b c) -> p a b c", b=shape[2], c=shape[3])

    def _ident(g):
        g.memset(epsc[:], EPS)
        g.memset(ident32[:], 0.0)
        g.affine_select(out=ident32[:], in_=ident32[:], compare_op=ALU.not_equal,
                        fill=1.0, base=0, pattern=[[-1, 128]], channel_multiplier=1)
        g.memset(identbf[:], 0.0)
        g.affine_select(out=identbf[:], in_=identbf[:], compare_op=ALU.not_equal,
                        fill=1.0, base=0, pattern=[[-1, 128]], channel_multiplier=1)
        g.memset(onescol_bf[:], 1.0)
        g.memset(onesrow_f[:], 1.0)
    ge_ident = S.emit("gpsimd", _ident, n=7)
    for eng in ("sync", "scalar", "vector", "tensor"):
        S.wait(eng, "ge", ge_ident)

    if True:
        xin = carve(pool, 0, [128, 2, C])
        w0b = carve(pool, 2048, [128, C])
        b0b = carve(pool, 3072, [128, C])
        sqs = carve(pool, 4096, [128, C])
        cps = carve(pool, 5120, [128, C])
        nprime = carve(pool, 6144, [128, C])
        xn_t = carve(pool, 7168, [128, C])
        n2 = carve(pool, 8192, [128, C])
        r0_t = carve(pool, 9216, [128, C])
        scol = carve(pool, 10240, [128, 2, 16])
        bmax = carve(pool, 10272, [128, C // 16])
        qsc = carve(pool, 10336, [128, C // 16])
        qinv = carve(pool, 10400, [128, C // 16])
        qy = carve(pool, 10464, [128, C])
        qa = carve(pool, 11488, [128, C])
        qA = carve(arena, 24576, [128, C], F32)
        qB = carve(arena, 26624, [128, C], F32)
        qC = carve(arena, 28672, [128, C], F32)
        qm1 = carve(pool, 12512, [128, C], U8)
        qm2 = carve(pool, 12768, [128, C], U8)
        xq_t = carve(pool, 13024, [128, C], BF16)
        rstage = carve(arena, 0, [128, 2, CJ, 512], F32)
        qstage = carve(arena, 32768, [128, 2, CJ, 512], BF16)

        S.use_sem("cst")

        def _ld0(sy):
            sy.dma_start(out=w0b[:], in_=ln0_h[0:1, :].partition_broadcast(128)).then_inc(S.sems["cst"], 16)
            sy.dma_start(out=b0b[:], in_=ln0_h[1:2, :].partition_broadcast(128)).then_inc(S.sems["cst"], 16)
        S.emit("sync", _ld0)
        ld_ln0 = S.inc("cst", 32)

        ve_tile = []
        se_tile = []
        st_span = []

        for i in range(TT):
            slot = i % 2
            span = i // 4
            sub = i % 4

            if i >= 2:
                S.wait("sync", "ve", ve_tile[i - 2])
                S.wait("sync", "se", se_tile[i - 2])

            xsem = S.use_sem(f"xin{slot}")

            def _ldx(sy, i=i, slot=slot, xsem=xsem):
                sy.dma_start(out=xin[:, slot], in_=x_v[i]).then_inc(S.sems[xsem], 16)
            S.emit("sync", _ldx)
            ld_x = S.inc(xsem, 16)

            S.wait("scalar", xsem, ld_x)

            def _st1(s, slot=slot):
                s.activation(out=cps[:], in_=xin[:, slot], func=AF.Identity,
                             accum_out=scol[:, slot, 0:1])
                s.activation(out=sqs[:], in_=xin[:, slot], func=AF.Square,
                             accum_out=scol[:, slot, 1:2])
            se_st1 = S.emit("scalar", _st1, n=2)

            S.wait("vector", xsem, ld_x)
            S.wait("vector", "se", se_st1)

            def _mv(v, slot=slot):
                v.tensor_scalar(out=scol[:, slot, 2:3], in0=scol[:, slot, 0:1],
                                scalar1=1.0 / C, scalar2=None, op0=ALU.mult)
                v.tensor_tensor(out=scol[:, slot, 3:4], in0=scol[:, slot, 2:3],
                                in1=scol[:, slot, 2:3], op=ALU.mult)
                v.scalar_tensor_tensor(out=scol[:, slot, 4:5], in0=scol[:, slot, 1:2],
                                       scalar=1.0 / C, in1=scol[:, slot, 3:4],
                                       op0=ALU.mult, op1=ALU.subtract)
                v.tensor_scalar(out=nprime[:], in0=xin[:, slot], scalar1=scol[:, slot, 2:3],
                                scalar2=None, op0=ALU.subtract)
            ve_np = S.emit("vector", _mv, n=4)

            S.wait("scalar", "ve", ve_np)

            def _rs(s, slot=slot):
                s.activation(out=scol[:, slot, 5:6], in_=scol[:, slot, 4:5], func=AF.Ln, bias=EPS)
                s.activation(out=scol[:, slot, 6:7], in_=scol[:, slot, 5:6], func=AF.Exp,
                             scale=-0.5)
            se_rs = S.emit("scalar", _rs, n=2)

            S.wait("vector", "se", se_rs)

            def _xn(v, slot=slot):
                v.tensor_scalar(out=xn_t[:], in0=nprime[:], scalar1=scol[:, slot, 6:7],
                                scalar2=None, op0=ALU.mult)
            ve_xn = S.emit("vector", _xn, n=1)

            S.wait("scalar", "ve", ve_xn)

            def _st2(s, slot=slot):
                s.activation(out=cps[:], in_=xn_t[:], func=AF.Identity,
                             accum_out=scol[:, slot, 7:8])
                s.activation(out=sqs[:], in_=xn_t[:], func=AF.Square,
                             accum_out=scol[:, slot, 8:9])
            se_st2 = S.emit("scalar", _st2, n=2)
            S.wait("vector", "se", se_st2)

            def _mv2(v, slot=slot):
                v.tensor_scalar(out=scol[:, slot, 9:10], in0=scol[:, slot, 7:8],
                                scalar1=1.0 / C, scalar2=None, op0=ALU.mult)
                v.tensor_tensor(out=scol[:, slot, 10:11], in0=scol[:, slot, 9:10],
                                in1=scol[:, slot, 9:10], op=ALU.mult)
                v.scalar_tensor_tensor(out=scol[:, slot, 11:12], in0=scol[:, slot, 8:9],
                                       scalar=1.0 / C, in1=scol[:, slot, 10:11],
                                       op0=ALU.mult, op1=ALU.subtract)
                v.tensor_scalar(out=n2[:], in0=xn_t[:], scalar1=scol[:, slot, 9:10],
                                scalar2=None, op0=ALU.subtract)
            ve_var2 = S.emit("vector", _mv2, n=4)
            S.wait("scalar", "ve", ve_var2)

            def _rs2(s, slot=slot):
                s.activation(out=scol[:, slot, 12:13], in_=scol[:, slot, 11:12], func=AF.Ln, bias=EPS)
                s.activation(out=scol[:, slot, 13:14], in_=scol[:, slot, 12:13], func=AF.Exp,
                             scale=-0.5)
            se_rs2 = S.emit("scalar", _rs2, n=2)
            S.wait("vector", "se", se_rs2)
            if i == 0:
                S.wait("vector", "cst", ld_ln0)

            def _r0(v, slot=slot):
                v.tensor_scalar(out=n2[:], in0=n2[:], scalar1=scol[:, slot, 13:14],
                                scalar2=None, op0=ALU.mult)
                v.tensor_tensor(out=r0_t[:], in0=n2[:], in1=w0b[:], op=ALU.mult)
                v.tensor_tensor(out=r0_t[:], in0=r0_t[:], in1=b0b[:], op=ALU.add)
            ve_r0 = S.emit("vector", _r0, n=3)

            def _bm(v, slot=slot):
                v.tensor_reduce(out=bmax[:], in_=xn_t[:].rearrange("p (g k) -> p g k", k=16),
                                axis=AX.X, op=ALU.max, apply_absolute_value=True)
                v.tensor_scalar(out=qsc[:], in0=bmax[:], scalar1=1.0 / 6.0,
                                scalar2=1e-12, op0=ALU.mult, op1=ALU.max)
            ve_bm = S.emit("vector", _bm, n=2)

            def _qi(v, slot=slot):
                v.reciprocal(out=qinv[:], in_=qsc[:])
            S.emit("vector", _qi, n=1)

            def _qy(v, slot=slot):
                v.tensor_tensor(out=qy[:].rearrange("p (g k) -> p g k", k=16),
                                in0=xn_t[:].rearrange("p (g k) -> p g k", k=16),
                                in1=qinv[:].unsqueeze(2).broadcast_to([128, C // 16, 16]),
                                op=ALU.mult)
            ve_qy = S.emit("vector", _qy, n=1)
            S.wait("scalar", "ve", ve_qy)

            def _qabs(s):
                s.activation(out=qa[:], in_=qy[:], func=AF.Abs)
            se_qabs = S.emit("scalar", _qabs, n=1)

            def _qgrid(v):
                v.tensor_scalar(out=qA[:], in0=qy[:], scalar1=2.0, scalar2=RNE,
                                op0=ALU.mult, op1=ALU.add)
                v.tensor_scalar(out=qA[:], in0=qA[:], scalar1=RNE, scalar2=0.5,
                                op0=ALU.subtract, op1=ALU.mult)
                v.tensor_scalar(out=qB[:], in0=qy[:], scalar1=RNE, scalar2=RNE,
                                op0=ALU.add, op1=ALU.subtract)
                v.tensor_scalar(out=qC[:], in0=qy[:], scalar1=0.5, scalar2=RNE,
                                op0=ALU.mult, op1=ALU.add)
                v.tensor_scalar(out=qC[:], in0=qC[:], scalar1=RNE, scalar2=2.0,
                                op0=ALU.subtract, op1=ALU.mult)
            S.emit("vector", _qgrid, n=5)
            S.wait("vector", "se", se_qabs)

            def _qsel(v, slot=slot):
                v.tensor_single_scalar(out=qm1[:], in_=qa[:], scalar=2.25, op=ALU.is_lt)
                v.tensor_single_scalar(out=qm2[:], in_=qa[:], scalar=4.5, op=ALU.is_lt)
                v.tensor_copy(out=qy[:], in_=qC[:])
                v.copy_predicated(out=qy[:], mask=qm2[:], data=qB[:])
                v.copy_predicated(out=qy[:], mask=qm1[:], data=qA[:])
                v.tensor_tensor(out=xq_t[:].rearrange("p (g k) -> p g k", k=16),
                                in0=qy[:].rearrange("p (g k) -> p g k", k=16),
                                in1=qsc[:].unsqueeze(2).broadcast_to([128, C // 16, 16]),
                                op=ALU.mult)
            ve_q = S.emit("vector", _qsel, n=6)

            if i > 0:
                S.wait("tensor", "ve", ve_tile[i - 1])
                S.wait("tensor", "se", se_tile[i - 1])
            S.wait("tensor", "ve", ve_r0)
            S.wait("tensor", "ve", ve_q)

            def _tr(t):
                for cj in range(CJ):
                    t.transpose(psum[cj // 4][:, (cj % 4) * 128:(cj % 4) * 128 + 128],
                                r0_t[:, cj * 128:cj * 128 + 128], ident32[:])
                last = None
                for cj in range(CJ):
                    last = t.transpose(psbf[:, cj * 128:cj * 128 + 128],
                                       xq_t[:, cj * 128:cj * 128 + 128], identbf[:])
                last.then_inc(S.sems["te"], 1)
            S.emit("tensor", _tr)
            te_tr = S.inc("te", 1)

            S.wait("vector", "te", te_tr)

            def _ev(v, slot2=span % 2, sub=sub):
                v.tensor_copy(out=rstage[:, slot2, 0:4, sub * 128:sub * 128 + 128],
                              in_=psum[0][:].rearrange("p (a b) -> p a b", b=128))
                v.tensor_copy(out=rstage[:, slot2, 4:8, sub * 128:sub * 128 + 128],
                              in_=psum[1][:].rearrange("p (a b) -> p a b", b=128))
            ve_ev = S.emit("vector", _ev, n=2)
            S.wait("scalar", "te", te_tr)

            def _evq(s, slot2=span % 2, sub=sub):
                s.copy(out=qstage[:, slot2, :, sub * 128:sub * 128 + 128],
                       in_=psbf[:].rearrange("p (a b) -> p a b", b=128))
            se_evq = S.emit("scalar", _evq, n=1)

            ve_tile.append(ve_ev)
            se_tile.append(se_evq)

            if sub == 3:
                ssem = S.use_sem(f"stg{span % 2}")
                S.wait("scalar", "ve", ve_ev)
                S.wait("scalar", "se", se_evq)

                def _stsp(s, span=span, slot2=span % 2, ssem=ssem):
                    s.dma_start(out=r_v[:, :, span * 512:span * 512 + 512],
                                in_=rstage[:, slot2]).then_inc(S.sems[ssem], 16)
                    s.dma_start(out=xq_v[:, :, span * 512:span * 512 + 512],
                                in_=qstage[:, slot2]).then_inc(S.sems[ssem], 16)
                S.emit("scalar", _stsp)
                st_span.append(S.inc(ssem, 32))
            if sub == 0 and span >= 2:
                ssem2 = f"stg{span % 2}"
                S.wait("vector", ssem2, st_span[span - 2])
                S.wait("scalar", ssem2, st_span[span - 2])

        S.barrier()

    if stop_after != "init":
        _build_layers(nc, S, T=T, L=L, stop_after=stop_after, psum=psum, psbf=psbf,
                      ident32=ident32, identbf=identbf, onescol_bf=onescol_bf,
                      onesrow_f=onesrow_f, alloc=alloc, arena=arena, pool=pool,
                      carve=carve,
                      hb=dict(r_v=r_v, xq_v=xq_v, xk2_v=xk2_v, xr2_v=xr2_v, kf_v=kf_vv,
                              src_v=src_vv, r_sb=r_sb_t, h_buf=h_buf_t,
                              rT_v=rT_v, hqT_v=hqT_v, x_v=x_v, out_v=out_v,
                              wkT=wkT_h, wvT=wvT_h, wrT=wrT_h, wcrT=wcrT_h, woT=woT_h,
                              wckT=wckT_h, wcvT=wcvT_h, vecs=vecs_h,
                              wg=wg_h, wu=wu_h, wd=wd_h))

    store_marks = {s: S.ctr[s] for s in S.sem_names
                   if s.startswith(("stg", "ost")) and S.ctr[s]}
    semstack = contextlib.ExitStack()
    for name in S.sem_names:
        S.sems[name] = semstack.enter_context(nc.semaphore(name))

    def _fin(sy, marks=store_marks):
        for s, tgt in marks.items():
            sy.wait_ge(S.sems[s], tgt)
    S.emit("sync", _fin)

    guards = {}
    with nc.Block() as block:
        for eng_name, deco in (("sync", block.sync), ("scalar", block.scalar),
                               ("vector", block.vector), ("tensor", block.tensor),
                               ("gpsimd", block.gpsimd)):
            if S.streams[eng_name]:
                def _run(eng, eng_name=eng_name, fns=S.streams[eng_name]):
                    g = Guard(eng, S.sems[ENG_SEM[eng_name]],
                              guard=eng_name in ("scalar", "vector", "gpsimd"))
                    guards[eng_name] = g
                    for fn in fns:
                        fn(g)
                deco(_run)
    for eng_name in ("scalar", "vector", "gpsimd"):
        if eng_name in guards:
            got = guards[eng_name]._n
            want = S.ctr[ENG_SEM[eng_name]]
            assert got == want, (eng_name, got, want)
    semstack.close()
    alloc.close()
    return nc


def _build_layers(nc, S, T, L, stop_after, psum, psbf, ident32, identbf,
                  onescol_bf, onesrow_f, alloc, arena, pool, carve, hb):
    NT = T // TC
    e = alloc.enter_context
    AR = 8192
    wk_v = arena[:, 0 * AR:1 * AR].rearrange("p (c o) -> p c o", o=1024)
    wv_v = arena[:, 1 * AR:2 * AR].rearrange("p (c o) -> p c o", o=1024)
    wr_v = arena[:, 2 * AR:3 * AR].rearrange("p (c o) -> p c o", o=1024)
    wo_v = arena[:, 3 * AR:4 * AR].rearrange("p (c o) -> p c o", o=1024)
    wcr_v = arena[:, 4 * AR:5 * AR].rearrange("p (c o) -> p c o", o=1024)
    wck_v = arena[:, 0:4 * AR].rearrange("p (c f) -> p c f", f=4096)
    wcv_v = arena[:, 0:4 * AR].rearrange("p (f c) -> p f c", c=1024)

    r_sb = hb["r_sb"]
    h_buf = hb["h_buf"]
    hb_sav = e(nc.sbuf_tensor("hb_sav", [128, CJ, 1], F32))
    hb2_sav = e(nc.sbuf_tensor("hb2_sav", [128, CJ, 1], F32))
    a_carry = e(nc.sbuf_tensor("a_carry", [128, CJ, 1], F32))
    b_carry = e(nc.sbuf_tensor("b_carry", [128, CJ, 1], F32))
    stripL = e(nc.sbuf_tensor("stripL", [1, 8, TC], F32))
    vec_sb = e(nc.sbuf_tensor("vec_sb", [128, 1, CJ, NV], F32))
    ew_t = carve(pool, 15366, [128, CJ, 1])
    eu_t = carve(pool, 15374, [128, CJ, 1])

    drb = carve(pool, 0, [128, CJ, TC])
    rb_bf = carve(arena, 4 * AR, [128, CJ, TC], BF16)
    rsq_bf = carve(arena, 4 * AR + CJ * TC, [128, CJ, TC], BF16)
    rwkv_bf = carve(pool, 0, [128, CJ, TC], BF16)
    xm_a = carve(pool, 4096, [128, CJ, TC], BF16)
    xm_b = carve(pool, 6144, [128, CJ, TC], BF16)
    sr_full = carve(pool, 8192, [128, CJ, TC], BF16)
    E_b = carve(pool, 10240, [128, 2, TC])
    srt = carve(pool, 10240, [128, TC])
    R_b = carve(pool, 11264, [128, 2, TC])
    A_b = carve(pool, 12288, [128, 2, TC + 1])
    B_b = carve(pool, 14340, [128, 2, TC + 1])
    xk2in = carve(pool, 0, [128, 2, CJ, TC], BF16)
    kfst = carve(pool, 4096, [128, 2, 4, TC], BF16)
    relu_scr = carve(pool, 6144, [128, 2, TC])
    wcrP = carve(pool, 7168, [128, CJ, 1024], BF16)
    xr2inA = carve(pool, 11264, [128, CJ, TC], BF16)
    srtA = carve(pool, 13312, [128, TC])
    srcA = carve(pool, 13824, [128, CJ, TC], BF16)
    kfin = carve(pool, 0, [128, FJ, TC], BF16)
    src_bf = carve(pool, 8192, [128, CJ, TC], BF16)
    kvscr = carve(pool, 10240, [128, TC])

    sems = S.sems
    r_marks = [None] * NT
    xk2_marks = [None] * NT
    xr2_marks = [None] * NT
    src_marks = [None] * NT
    kf_marks = [[None] * 8 for _ in range(NT)]
    region_free = [("te", 0)] * 5

    def dma_load(sem, fn, gate=(), eng="sync"):
        for gsem, gtgt in gate:
            if gtgt:
                S.wait(eng, gsem, gtgt)
        return S.dma(eng, sem, fn)

    for l in range(L):
        ls = l % 2

        cst_t = dma_load("cst", (lambda sy, l=l, ls=ls:
            sy.dma_start(out=vec_sb[:, 0],
                         in_=hb["vecs"][l].rearrange("(cj p) v -> p cj v", p=128)
                         ).then_inc(sems["cst"], 16)))
        S.wait("scalar", "cst", cst_t)

        def _ewu(s, ls=ls):
            s.activation(out=ew_t[:], in_=vec_sb[:, 0, :, V_TD:V_TD + 1], func=AF.Exp)
            s.activation(out=ew_t[:], in_=ew_t[:], func=AF.Exp, scale=-1.0)
            s.activation(out=eu_t[:], in_=vec_sb[:, 0, :, V_TF:V_TF + 1], func=AF.Exp)
        se_ewu = S.emit("scalar", _ewu, n=3)

        wqm = {}
        for name, view, src_h, reg in (("wk", wk_v, hb["wkT"], 0), ("wv", wv_v, hb["wvT"], 1),
                                       ("wr", wr_v, hb["wrT"], 2), ("wo", wo_v, hb["woT"], 3)):
            sem = S.use_sem("wq" + str(reg))
            wqm[name] = dma_load(sem, (lambda g, view=view, src_h=src_h, l=l, sem=sem:
                g.dma_start(out=view,
                            in_=src_h[l].rearrange("(cj p) o -> p cj o", p=128)
                            ).then_inc(sems[sem], 16)),
                gate=[region_free[reg]], eng="gpsimd")

        def _lrst(g):
            g.memset(a_carry[:], 0.0)
            g.memset(b_carry[:], 0.0)
            g.memset(hb_sav[:], 0.0)
            g.memset(hb2_sav[:], 0.0)
        ge_lrst = S.emit("gpsimd", _lrst, n=4)
        S.wait("vector", "ge", ge_lrst)

        H = [dict() for _ in range(NT + 2)]

        def emit_rld(qx):
            s2 = qx % 2
            rl_sem = S.use_sem(f"rld{s2}")
            gates = []
            hp = H[qx - 2] if qx > 1 else {}
            if hp:
                gates += [("ve", hp["ve_end"]), ("se", hp["se_end"]),
                          ("ge", hp["ge_end"]), hp["r_mark"]]
            if r_marks[qx] is not None:
                gates.append(r_marks[qx])
            H[qx]["rl"] = dma_load(rl_sem, (lambda sy, qx=qx, s2=s2, rl_sem=rl_sem:
                sy.dma_start(out=r_sb[:, s2], in_=hb["r_v"][:, :, qx * TC:qx * TC + TC]
                             ).then_inc(sems[rl_sem], 16)), gate=gates)

        def emit_casts(qx):
            s2 = qx % 2
            S.wait("gpsimd", f"rld{s2}", H[qx]["rl"])
            if qx >= 2:
                S.wait("gpsimd", "te", H[qx - 2]["te_stats2"])
            elif qx == 1:
                S.wait("gpsimd", "te", H[0]["te_stats"])
            def _rb(g, s2=s2):
                g.tensor_copy(out=rb_bf[:], in_=r_sb[:, s2])
            S.emit("gpsimd", _rb, n=1)
            def _rsq(g, s2=s2):
                g.tensor_tensor(out=rsq_bf[:], in0=r_sb[:, s2], in1=r_sb[:, s2],
                                op=ALU.mult)
            H[qx]["ge_rsq"] = S.emit("gpsimd", _rsq, n=1)

        def emit_stats1(qx):
            S.wait("tensor", "ge", H[qx]["ge_rsq"])
            def _stats(t):
                for cj in range(CJ):
                    t.matmul(psum[4][0:1, :], onescol_bf[:], rb_bf[:, cj],
                             start=(cj == 0), stop=(cj == CJ - 1))
                last_i = None
                for cj in range(CJ):
                    last_i = t.matmul(psum[5][0:1, :], onescol_bf[:], rsq_bf[:, cj],
                                      start=(cj == 0), stop=(cj == CJ - 1))
                last_i.then_inc(sems["te"], 1)
            S.emit("tensor", _stats)
            H[qx]["te_stats"] = S.inc("te", 1)

        def emit_strips1(qx):
            S.wait("vector", "te", H[qx]["te_stats"])
            def _strip1(v):
                v.tensor_scalar(out=stripL[0:1, 0, :], in0=psum[4][0:1, :],
                                scalar1=1.0 / C, scalar2=None, op0=ALU.mult)
                v.tensor_tensor(out=stripL[0:1, 1, :], in0=stripL[0:1, 0, :],
                                in1=stripL[0:1, 0, :], op=ALU.mult)
                v.scalar_tensor_tensor(out=stripL[0:1, 2, :], in0=psum[5][0:1, :],
                                       scalar=1.0 / C, in1=stripL[0:1, 1, :],
                                       op0=ALU.mult, op1=ALU.subtract)
            ve_var = S.emit("vector", _strip1, n=3)
            H[qx]["ve_var1"] = ve_var
            S.wait("scalar", "ve", ve_var)
            def _rs1(s):
                s.activation(out=stripL[0:1, 3, :], in_=stripL[0:1, 2, :],
                             func=AF.Ln, bias=EPS)
                s.activation(out=stripL[0:1, 4, :], in_=stripL[0:1, 3, :],
                             func=AF.Exp, scale=-0.5)
            se_rs1 = S.emit("scalar", _rs1, n=2)
            S.wait("vector", "se", se_rs1)
            def _nm1(v):
                v.scalar_tensor_tensor(out=stripL[0:1, 5, :], in0=stripL[0:1, 0, :],
                                       scalar=-1.0, in1=stripL[0:1, 4, :],
                                       op0=ALU.mult, op1=ALU.mult)
            H[qx]["ve_nm1"] = S.emit("vector", _nm1, n=1)

        def emit_bc1(qx, ve_h2_gate=0):
            S.wait("tensor", "ve", H[qx]["ve_nm1"])
            if ve_h2_gate:
                S.wait("tensor", "ve", ve_h2_gate)
            def _bc1(t):
                t.matmul(psum[4][:], onesrow_f[:], stripL[0:1, 5, :],
                         start=True, stop=True)
                t.matmul(psum[5][:], onesrow_f[:], stripL[0:1, 4, :],
                         start=True, stop=True).then_inc(sems["te"], 1)
            S.emit("tensor", _bc1)
            H[qx]["te_bc1"] = S.inc("te", 1)

        for q in range(NT):
            s_ = q % 2
            hq = H[q]
            h1 = H[q - 1] if q > 0 else {}

            if q == 0:
                emit_rld(0)
                emit_casts(0)
                emit_stats1(0)
                emit_strips1(0)
                emit_bc1(0)

            if h1:
                S.wait("gpsimd", "ve", h1["ve_mix2"])
            def _hshift(g):
                g.tensor_copy(out=h_buf[:, :, 0:1], in_=hb_sav[:])
            ge_hshift = S.emit("gpsimd", _hshift, n=1)
            S.wait("vector", "ge", ge_hshift)
            if q == 0:
                S.wait("vector", "cst", cst_t)
            S.wait("vector", "te", hq["te_bc1"])
            def _happ(v, ls=ls, s_=s_):
                v.tensor_tensor(out=h_buf[:, :, 1:TC + 1], in0=r_sb[:, s_],
                                in1=psum[4][:].unsqueeze(1).broadcast_to([128, CJ, TC]),
                                op=ALU.add)
                v.tensor_tensor(out=h_buf[:, :, 1:TC + 1], in0=h_buf[:, :, 1:TC + 1],
                                in1=psum[5][:].unsqueeze(1).broadcast_to([128, CJ, TC]),
                                op=ALU.mult)
                for cj in range(CJ):
                    v.tensor_scalar(out=h_buf[:, cj, 1:TC + 1], in0=h_buf[:, cj, 1:TC + 1],
                                    scalar1=vec_sb[:, 0, cj, V_LN1W:V_LN1W + 1],
                                    scalar2=vec_sb[:, 0, cj, V_LN1B:V_LN1B + 1],
                                    op0=ALU.mult, op1=ALU.add)
            ve_h = S.emit("vector", _happ, n=2 + CJ)

            if h1:
                S.wait("vector", "te", h1["te_o_last"])
            def _dd(v):
                v.tensor_tensor(out=drb[:], in0=h_buf[:, :, 1:TC + 1],
                                in1=h_buf[:, :, 0:TC], op=ALU.subtract)
            S.emit("vector", _dd, n=1)

            if h1:
                S.wait("vector", "te", h1["te_k_last"])
                S.wait("vector", h1["xk2_mark"][0], h1["xk2_mark"][1])
                S.wait("vector", h1["xr2_mark"][0], h1["xr2_mark"][1])
            def _mixr(v, ls=ls):
                for cj in range(CJ):
                    v.scalar_tensor_tensor(out=xm_a[:, cj], in0=drb[:, cj],
                                           scalar=vec_sb[:, 0, cj, V_TMR:V_TMR + 1],
                                           in1=h_buf[:, cj, 0:TC],
                                           op0=ALU.mult, op1=ALU.add)
            ve_xr = S.emit("vector", _mixr, n=CJ)

            if q == 0:
                S.wait("tensor", "wq2", wqm["wr"])
                S.wait("scalar", "se", se_ewu)
            S.wait("tensor", "ve", ve_xr)
            te_r = {}
            se_sr = {}
            for oj in range(CJ):
                sl = oj % 2
                if oj < 2:
                    S.wait("tensor", "ve", ve_h)
                else:
                    S.wait("tensor", "se", se_sr[oj - 2])
                def _rmm(t, oj=oj, sl=sl):
                    last_i = None
                    for cj in range(CJ):
                        last_i = t.matmul(psum[4 + sl][:], wr_v[:, cj, oj * 128:oj * 128 + 128],
                                          xm_a[:, cj], start=(cj == 0), stop=(cj == CJ - 1))
                    last_i.then_inc(sems["te"], 1)
                S.emit("tensor", _rmm)
                te_r[oj] = S.inc("te", 1)
                S.wait("scalar", "te", te_r[oj])
                if oj == 0 and h1:
                    S.wait("scalar", "ve", h1["ve_out_last"])
                if oj == 0 and q == 0:
                    pass
                def _sr(s, oj=oj, sl=sl):
                    s.activation(out=srt[:], in_=psum[4 + sl][:], func=AF.Exp, scale=-1.0)
                    s.activation(out=srt[:], in_=srt[:], func=AF.Ln, bias=1.0)
                    s.activation(out=sr_full[:, oj], in_=srt[:], func=AF.Exp, scale=-1.0)
                se_sr[oj] = S.emit("scalar", _sr, n=3)
            hq["te_r_last"] = te_r[CJ - 1]

            S.wait("vector", "te", te_r[CJ - 1])
            if h1:
                S.wait("vector", "te", h1["te_v_last"])
            def _mixkv(v, ls=ls):
                for cj in range(CJ):
                    v.scalar_tensor_tensor(out=xm_a[:, cj], in0=drb[:, cj],
                                           scalar=vec_sb[:, 0, cj, V_TMK:V_TMK + 1],
                                           in1=h_buf[:, cj, 0:TC],
                                           op0=ALU.mult, op1=ALU.add)
                for cj in range(CJ):
                    v.scalar_tensor_tensor(out=xm_b[:, cj], in0=drb[:, cj],
                                           scalar=vec_sb[:, 0, cj, V_TMV:V_TMV + 1],
                                           in1=h_buf[:, cj, 0:TC],
                                           op0=ALU.mult, op1=ALU.add)
            ve_kv = S.emit("vector", _mixkv, n=2 * CJ)

            if q + 1 < NT:
                emit_rld(q + 1)
                emit_casts(q + 1)

            if q == 0:
                S.wait("tensor", "wq0", wqm["wk"])
                S.wait("tensor", "wq1", wqm["wv"])
            S.wait("tensor", "ve", ve_kv)
            se_E = {}
            ve_R = {}
            se_dinv = {}
            ve_out = {}
            te_k = {}
            te_v = {}
            for oj in range(CJ):
                sl = oj % 2
                if oj >= 2:
                    S.wait("tensor", "se", se_E[oj - 2])
                def _kmm(t, oj=oj, sl=sl):
                    last_i = None
                    for cj in range(CJ):
                        last_i = t.matmul(psum[sl][:], wk_v[:, cj, oj * 128:oj * 128 + 128],
                                          xm_a[:, cj], start=(cj == 0), stop=(cj == CJ - 1))
                    last_i.then_inc(sems["te"], 1)
                S.emit("tensor", _kmm)
                te_k[oj] = S.inc("te", 1)

                S.wait("scalar", "te", te_k[oj])
                if oj >= 2:
                    S.wait("scalar", "ve", ve_out[oj - 2])
                def _E(s, sl=sl):
                    s.activation(out=E_b[:, sl], in_=psum[sl][:], func=AF.Exp)
                se_E[oj] = S.emit("scalar", _E, n=1)

                if oj >= 2:
                    S.wait("tensor", "ve", ve_R[oj - 2])
                def _vmm(t, oj=oj, sl=sl):
                    last_i = None
                    for cj in range(CJ):
                        last_i = t.matmul(psum[2 + sl][:], wv_v[:, cj, oj * 128:oj * 128 + 128],
                                          xm_b[:, cj], start=(cj == 0), stop=(cj == CJ - 1))
                    last_i.then_inc(sems["te"], 1)
                S.emit("tensor", _vmm)
                te_v[oj] = S.inc("te", 1)

                S.wait("vector", "te", te_v[oj])
                S.wait("vector", "se", se_E[oj])
                if oj >= 2:
                    S.wait("vector", "se", se_dinv[oj - 2])
                def _R(v, sl=sl):
                    v.tensor_tensor(out=R_b[:, sl], in0=E_b[:, sl], in1=psum[2 + sl][:],
                                    op=ALU.mult)
                ve_R[oj] = S.emit("vector", _R, n=1)

                def _wkv(v, oj=oj, sl=sl, ls=ls):
                    v.tensor_copy(out=A_b[:, sl, 0:1], in_=a_carry[:, oj])
                    v.tensor_copy(out=B_b[:, sl, 0:1], in_=b_carry[:, oj])
                    v.tensor_tensor_scan(out=A_b[:, sl, 1:TC + 1],
                                         data0=ew_t[:, oj].broadcast_to([128, TC]),
                                         data1=R_b[:, sl], initial=A_b[:, sl, 0:1],
                                         op0=ALU.mult, op1=ALU.add)
                    v.tensor_tensor_scan(out=B_b[:, sl, 1:TC + 1],
                                         data0=ew_t[:, oj].broadcast_to([128, TC]),
                                         data1=E_b[:, sl], initial=B_b[:, sl, 0:1],
                                         op0=ALU.mult, op1=ALU.add)
                    v.tensor_copy(out=a_carry[:, oj], in_=A_b[:, sl, TC:TC + 1])
                    v.tensor_copy(out=b_carry[:, oj], in_=B_b[:, sl, TC:TC + 1])
                    v.scalar_tensor_tensor(out=R_b[:, sl], in0=R_b[:, sl],
                                           scalar=eu_t[:, oj], in1=A_b[:, sl, 0:TC],
                                           op0=ALU.mult, op1=ALU.add)
                    v.scalar_tensor_tensor(out=E_b[:, sl], in0=E_b[:, sl],
                                           scalar=eu_t[:, oj], in1=B_b[:, sl, 0:TC],
                                           op0=ALU.mult, op1=ALU.add)
                ve_scan = S.emit("vector", _wkv, n=8)

                S.wait("scalar", "ve", ve_scan)
                def _dinv(s, sl=sl):
                    s.activation(out=E_b[:, sl], in_=E_b[:, sl], func=AF.Ln)
                    s.activation(out=E_b[:, sl], in_=E_b[:, sl], func=AF.Exp, scale=-1.0)
                se_dinv[oj] = S.emit("scalar", _dinv, n=2)

                def _emit_out(ojx):
                    slx = ojx % 2
                    S.wait("vector", "se", se_dinv[ojx])
                    if ojx < 2 and h1:
                        S.wait("vector", "te", h1["te_o_last"])
                    def _out(v, ojx=ojx, slx=slx):
                        v.tensor_tensor(out=R_b[:, slx], in0=R_b[:, slx], in1=E_b[:, slx],
                                        op=ALU.mult)
                        v.tensor_tensor(out=rwkv_bf[:, ojx], in0=R_b[:, slx],
                                        in1=sr_full[:, ojx], op=ALU.mult)
                    ve_out[ojx] = S.emit("vector", _out, n=2)
                if oj >= 1:
                    _emit_out(oj - 1)
            _emit_out(CJ - 1)
            hq["te_k_last"] = te_k[CJ - 1]
            hq["te_v_last"] = te_v[CJ - 1]
            hq["ve_out_last"] = ve_out[CJ - 1]

            if q + 1 < NT:
                S.wait("tensor", "se", se_sr[CJ - 2])
                S.wait("tensor", "se", se_sr[CJ - 1])
                emit_stats1(q + 1)

            ve_radd = {}
            te_o = {}
            for cj in range(CJ):
                ob = 6 if cj % 2 == 0 else 0
                if cj == 0:
                    for oj in range(CJ):
                        S.wait("tensor", "ve", ve_out[oj])
                    if q == 0:
                        S.wait("tensor", "wq3", wqm["wo"])
                elif cj == 1:
                    S.wait("tensor", "se", se_E[CJ - 1])
                else:
                    S.wait("tensor", "ve", ve_radd[cj - 2])
                def _omm(t, cj=cj, ob=ob):
                    last_i = None
                    for oj in range(CJ):
                        last_i = t.matmul(psum[ob][:], wo_v[:, oj, cj * 128:cj * 128 + 128],
                                          rwkv_bf[:, oj], start=(oj == 0), stop=(oj == CJ - 1))
                    last_i.then_inc(sems["te"], 1)
                S.emit("tensor", _omm)
                te_o[cj] = S.inc("te", 1)
                S.wait("vector", "te", te_o[cj])
                def _radd(v, cj=cj, s_=s_, ob=ob):
                    v.tensor_tensor(out=r_sb[:, s_, cj], in0=r_sb[:, s_, cj], in1=psum[ob][:],
                                    op=ALU.add)
                ve_radd[cj] = S.emit("vector", _radd, n=1)
            hq["te_o_last"] = te_o[CJ - 1]

            if q + 1 < NT:
                emit_strips1(q + 1)

            S.wait("gpsimd", "ve", ve_kv)
            def _hsav(g):
                g.tensor_copy(out=hb_sav[:], in_=h_buf[:, :, TC:TC + 1])
                g.tensor_copy(out=h_buf[:, :, 0:1], in_=hb2_sav[:])
            ge_hsav = S.emit("gpsimd", _hsav, n=2)
            S.wait("vector", "ge", ge_hsav)

            last_stats = H[q + 1]["te_stats"] if q + 1 < NT else hq["te_stats"]
            S.wait("gpsimd", "ve", ve_radd[CJ - 1])
            S.wait("gpsimd", "te", last_stats)
            def _rb2(g, s_=s_):
                g.tensor_copy(out=rb_bf[:], in_=r_sb[:, s_])
            ge_rb2 = S.emit("gpsimd", _rb2, n=1)
            S.wait("scalar", "ve", ve_radd[CJ - 1])
            S.wait("scalar", "te", last_stats)
            def _rsq2(s, s_=s_):
                s.activation(out=rsq_bf[:], in_=r_sb[:, s_], func=AF.Square)
            se_rsq2 = S.emit("scalar", _rsq2, n=1)
            S.wait("tensor", "ge", ge_rb2)
            S.wait("tensor", "se", se_rsq2)
            if q + 1 < NT:
                S.wait("tensor", "ve", H[q + 1]["ve_var1"])
            def _stats2(t):
                for cj in range(CJ):
                    t.matmul(psum[4][0:1, :], onescol_bf[:], rb_bf[:, cj],
                             start=(cj == 0), stop=(cj == CJ - 1))
                last_i = None
                for cj in range(CJ):
                    last_i = t.matmul(psum[5][0:1, :], onescol_bf[:], rsq_bf[:, cj],
                                      start=(cj == 0), stop=(cj == CJ - 1))
                last_i.then_inc(sems["te"], 1)
            S.emit("tensor", _stats2)
            te_stats2 = S.inc("te", 1)
            hq["te_stats2"] = te_stats2
            S.wait("vector", "te", te_stats2)
            def _strip2(v):
                v.tensor_scalar(out=stripL[0:1, 0, :], in0=psum[4][0:1, :],
                                scalar1=1.0 / C, scalar2=None, op0=ALU.mult)
                v.tensor_tensor(out=stripL[0:1, 1, :], in0=stripL[0:1, 0, :],
                                in1=stripL[0:1, 0, :], op=ALU.mult)
                v.scalar_tensor_tensor(out=stripL[0:1, 2, :], in0=psum[5][0:1, :],
                                       scalar=1.0 / C, in1=stripL[0:1, 1, :],
                                       op0=ALU.mult, op1=ALU.subtract)
            ve_var2 = S.emit("vector", _strip2, n=3)
            S.wait("scalar", "ve", ve_var2)
            def _rs2(s):
                s.activation(out=stripL[0:1, 3, :], in_=stripL[0:1, 2, :],
                             func=AF.Ln, bias=EPS)
                s.activation(out=stripL[0:1, 6, :], in_=stripL[0:1, 3, :],
                             func=AF.Exp, scale=-0.5)
            se_rs2 = S.emit("scalar", _rs2, n=2)
            S.wait("vector", "se", se_rs2)
            def _nm2(v):
                v.scalar_tensor_tensor(out=stripL[0:1, 7, :], in0=stripL[0:1, 0, :],
                                       scalar=-1.0, in1=stripL[0:1, 6, :],
                                       op0=ALU.mult, op1=ALU.mult)
            ve_nm2 = S.emit("vector", _nm2, n=1)
            S.wait("tensor", "ve", ve_nm2)
            def _bc2(t):
                t.matmul(psum[4][:], onesrow_f[:], stripL[0:1, 7, :],
                         start=True, stop=True)
                t.matmul(psum[5][:], onesrow_f[:], stripL[0:1, 6, :],
                         start=True, stop=True).then_inc(sems["te"], 1)
            S.emit("tensor", _bc2)
            te_bc2 = S.inc("te", 1)
            S.wait("vector", "te", te_bc2)
            def _happ2(v, ls=ls, s_=s_):
                v.tensor_tensor(out=h_buf[:, :, 1:TC + 1], in0=r_sb[:, s_],
                                in1=psum[4][:].unsqueeze(1).broadcast_to([128, CJ, TC]),
                                op=ALU.add)
                v.tensor_tensor(out=h_buf[:, :, 1:TC + 1], in0=h_buf[:, :, 1:TC + 1],
                                in1=psum[5][:].unsqueeze(1).broadcast_to([128, CJ, TC]),
                                op=ALU.mult)
                for cj in range(CJ):
                    v.tensor_scalar(out=h_buf[:, cj, 1:TC + 1], in0=h_buf[:, cj, 1:TC + 1],
                                    scalar1=vec_sb[:, 0, cj, V_LN2W:V_LN2W + 1],
                                    scalar2=vec_sb[:, 0, cj, V_LN2B:V_LN2B + 1],
                                    op0=ALU.mult, op1=ALU.add)
            hq["ve_h2"] = S.emit("vector", _happ2, n=2 + CJ)

            if q + 1 < NT:
                emit_bc1(q + 1, ve_h2_gate=hq["ve_h2"])

            S.wait("vector", "ge", ge_hsav)
            def _dd2(v):
                v.tensor_tensor(out=drb[:], in0=h_buf[:, :, 1:TC + 1],
                                in1=h_buf[:, :, 0:TC], op=ALU.subtract)
            S.emit("vector", _dd2, n=1)
            if h1:
                for m in (h1["xk2_mark"], h1["xr2_mark"]):
                    S.wait("vector", m[0], m[1])
            def _mix2(v, ls=ls):
                for cj in range(CJ):
                    v.scalar_tensor_tensor(out=xm_a[:, cj], in0=drb[:, cj],
                                           scalar=vec_sb[:, 0, cj, V_CMK:V_CMK + 1],
                                           in1=h_buf[:, cj, 0:TC],
                                           op0=ALU.mult, op1=ALU.add)
                for cj in range(CJ):
                    v.scalar_tensor_tensor(out=xm_b[:, cj], in0=drb[:, cj],
                                           scalar=vec_sb[:, 0, cj, V_CMR:V_CMR + 1],
                                           in1=h_buf[:, cj, 0:TC],
                                           op0=ALU.mult, op1=ALU.add)
            ve_mix2 = S.emit("vector", _mix2, n=2 * CJ)
            hq["ve_mix2"] = ve_mix2
            S.wait("gpsimd", "ve", hq["ve_h2"])
            def _hsav2(g):
                g.tensor_copy(out=hb2_sav[:], in_=h_buf[:, :, TC:TC + 1])
            ge_hs2 = S.emit("gpsimd", _hsav2, n=1)
            ve_hs2 = ve_mix2

            S.wait("scalar", "ve", ve_mix2)
            for nmm in (f"rst{s_}", "xk2s", "xr2s"):
                S.use_sem(nmm)
            def _st(s, q=q, s_=s_):
                s.dma_start(out=hb["r_v"][:, :, q * TC:q * TC + TC],
                            in_=r_sb[:, s_]).then_inc(sems[f"rst{s_}"], 16)
                s.dma_start(out=hb["xk2_v"][:, :, q * TC:q * TC + TC],
                            in_=xm_a[:]).then_inc(sems["xk2s"], 16)
                s.dma_start(out=hb["xr2_v"][:, :, q * TC:q * TC + TC],
                            in_=xm_b[:]).then_inc(sems["xr2s"], 16)
            S.emit("scalar", _st)
            r_marks[q] = (f"rst{s_}", S.inc(f"rst{s_}", 16))
            xk2_marks[q] = ("xk2s", S.inc("xk2s", 16))
            xr2_marks[q] = ("xr2s", S.inc("xr2s", 16))
            hq["r_mark"] = r_marks[q]
            hq["xk2_mark"] = xk2_marks[q]
            hq["xr2_mark"] = xr2_marks[q]
            hq["ve_end"] = ve_hs2
            hq["se_end"] = S.ctr["se"]
            hq["ge_end"] = S.ctr["ge"]

        S.barrier()
        if stop_after == f"tm{l}":
            return

        wck_t = []
        for piece in range(4):
            sem = S.use_sem("wq" + str(piece))
            wck_t.append(dma_load(sem, (lambda g, piece=piece, l=l, sem=sem:
                g.dma_start(out=wck_v[:, piece * 2:piece * 2 + 2, :],
                            in_=hb["wckT"][l].rearrange("(cj p) f -> p cj f", p=128)[:, piece * 2:piece * 2 + 2, :]
                            ).then_inc(sems[sem], 16)), eng="gpsimd"))
        semcr = S.use_sem("wq4")
        wq_cr = dma_load(semcr, (lambda g, l=l:
            g.dma_start(out=wcrP[:],
                        in_=hb["wcrT"][l].rearrange("(cj p) o -> p cj o", p=128)
                        ).then_inc(sems["wq4"], 16)), eng="gpsimd")

        prev_kfst = [("te", 0)] * 2
        prev_xk2_mm = [0, 0]
        prev_sekf_last = 0
        prev_src_mm = 0
        prev_src_se = 0
        prev_xr2_mm = 0
        for q in range(NT):
            sl = q % 2
            x_sem = S.use_sem(f"xk2l{sl}")
            gates = [xk2_marks[q]]
            if prev_xk2_mm[sl]:
                gates.append(("te", prev_xk2_mm[sl]))
            xl_t = dma_load(x_sem, (lambda sy, q=q, sl=sl, x_sem=x_sem:
                sy.dma_start(out=xk2in[:, sl], in_=hb["xk2_v"][:, :, q * TC:q * TC + TC]
                             ).then_inc(sems[x_sem], 16)), gate=gates)
            xr_sem = S.use_sem("xr2lA")
            gates = [xr2_marks[q]]
            if prev_xr2_mm:
                gates.append(("te", prev_xr2_mm))
            xr_t = dma_load(xr_sem, (lambda sy, q=q:
                sy.dma_start(out=xr2inA[:], in_=hb["xr2_v"][:, :, q * TC:q * TC + TC]
                             ).then_inc(sems["xr2lA"], 16)), gate=gates)

            if q == 0:
                S.wait("tensor", "wq4", wq_cr)
            if q > 0 and src_marks[q - 1]:
                S.wait("scalar", src_marks[q - 1][0], src_marks[q - 1][1])
            S.wait("tensor", "xr2lA", xr_t)
            se_src = {}
            te_src = {}
            for oj in range(CJ):
                bsl = oj % 2
                if oj >= 2:
                    S.wait("tensor", "se", se_src[oj - 2])
                elif prev_src_se:
                    S.wait("tensor", "se", prev_src_se)
                def _srmm(t, oj=oj, bsl=bsl):
                    last_i = None
                    for cj in range(CJ):
                        last_i = t.matmul(psum[4 + bsl][:], wcrP[:, cj, oj * 128:oj * 128 + 128],
                                          xr2inA[:, cj], start=(cj == 0), stop=(cj == CJ - 1))
                    last_i.then_inc(sems["te"], 1)
                S.emit("tensor", _srmm)
                te_src[oj] = S.inc("te", 1)
                S.wait("scalar", "te", te_src[oj])
                def _srs(s, oj=oj, bsl=bsl):
                    s.activation(out=srtA[:], in_=psum[4 + bsl][:], func=AF.Exp, scale=-1.0)
                    s.activation(out=srtA[:], in_=srtA[:], func=AF.Ln, bias=1.0)
                    s.activation(out=srcA[:, oj], in_=srtA[:], func=AF.Exp, scale=-1.0)
                se_src[oj] = S.emit("scalar", _srs, n=3)
            prev_xr2_mm = te_src[CJ - 1]
            prev_src_se = se_src[CJ - 1]
            S.use_sem("srcs")
            S.wait("scalar", "se", se_src[CJ - 1])
            def _stsrc(s, q=q):
                s.dma_start(out=hb["src_v"][:, :, q * TC:q * TC + TC],
                            in_=srcA[:]).then_inc(sems["srcs"], 16)
            S.emit("scalar", _stsrc)
            src_marks[q] = ("srcs", S.inc("srcs", 16))

            te_kf = {}
            se_kf = {}
            for fj in range(FJ):
                bsl = fj % 2
                grp = fj // 4
                gsl = grp % 2
                if fj == 0:
                    S.wait("tensor", x_sem, xl_t)
                    if q == 0:
                        for piece in range(4):
                            S.wait("tensor", "wq" + str(piece), wck_t[piece])
                    if prev_sekf_last:
                        S.wait("tensor", "se", prev_sekf_last)
                if fj >= 2:
                    S.wait("tensor", "se", se_kf[fj - 2])
                def _kfmm(t, fj=fj, bsl=bsl, sl=sl):
                    last_i = None
                    for cj in range(CJ):
                        last_i = t.matmul(psum[bsl][:], wck_v[:, cj, fj * 128:fj * 128 + 128],
                                          xk2in[:, sl, cj], start=(cj == 0), stop=(cj == CJ - 1))
                    last_i.then_inc(sems["te"], 1)
                S.emit("tensor", _kfmm)
                te_kf[fj] = S.inc("te", 1)
                S.wait("scalar", "te", te_kf[fj])
                if fj % 4 == 0 and prev_kfst[gsl][1]:
                    S.wait("scalar", prev_kfst[gsl][0], prev_kfst[gsl][1])
                def _kfact(s, fj=fj, bsl=bsl, gsl=gsl):
                    s.activation(out=relu_scr[:, bsl], in_=psum[bsl][:], func=AF.Relu)
                    s.activation(out=kfst[:, gsl, fj % 4], in_=relu_scr[:, bsl], func=AF.Square)
                se_kf[fj] = S.emit("scalar", _kfact, n=2)
                if fj % 4 == 3:
                    kf_sem = S.use_sem(f"kfs{gsl}")
                    S.wait("scalar", "se", se_kf[fj])
                    def _kfstore(s, q=q, grp=grp, gsl=gsl, kf_sem=kf_sem):
                        s.dma_start(out=hb["kf_v"][:, grp * 4:grp * 4 + 4, q * TC:q * TC + TC],
                                    in_=kfst[:, gsl]).then_inc(sems[kf_sem], 16)
                    S.emit("scalar", _kfstore)
                    kf_marks[q][grp] = (kf_sem, S.inc(kf_sem, 16))
                    prev_kfst[gsl] = kf_marks[q][grp]
            prev_sekf_last = se_kf[FJ - 1]
            prev_xk2_mm[sl] = te_kf[FJ - 1]

        S.barrier()
        if stop_after == f"cma{l}":
            return

        for piece in range(4):
            sem = S.use_sem("wq" + str(piece))
            dma_load(sem, (lambda g, piece=piece, l=l, sem=sem:
                g.dma_start(out=wcv_v[:, piece * 8:piece * 8 + 8, :],
                            in_=hb["wcvT"][l].rearrange("(fj p) c -> p fj c", p=128)[:, piece * 8:piece * 8 + 8, :]
                            ).then_inc(sems[sem], 16)), eng="gpsimd")
        wcv_t = {piece: S.ctr["wq" + str(piece)] for piece in range(4)}

        last = (l == L - 1)
        prev_ve_done = 0
        prev_se_done = 0
        prev_rstB = None
        prev_rT = None
        prev_kv_last = 0
        prev_src_mm = 0
        for q in range(NT):
            sl = q % 2
            kf_l = []
            for piece in range(4):
                k_sem = S.use_sem("kfl")
                gates = [kf_marks[q][piece * 2], kf_marks[q][piece * 2 + 1]]
                if prev_kv_last:
                    gates.append(("te", prev_kv_last))
                kf_l.append(dma_load(k_sem, (lambda sy, q=q, piece=piece:
                    sy.dma_start(out=kfin[:, piece * 8:piece * 8 + 8, :],
                                 in_=hb["kf_v"][:, piece * 8:piece * 8 + 8, q * TC:q * TC + TC]
                                 ).then_inc(sems["kfl"], 16)), gate=gates))
            rl_sem = S.use_sem(f"rld{sl}")
            gates = [r_marks[q]]
            if prev_ve_done:
                gates.append(("ve", prev_ve_done))
            if prev_rstB:
                gates.append(prev_rstB)
            if prev_rT:
                gates.append(prev_rT)
            rl_t = dma_load(rl_sem, (lambda sy, q=q, sl=sl, rl_sem=rl_sem:
                sy.dma_start(out=r_sb[:, sl], in_=hb["r_v"][:, :, q * TC:q * TC + TC]
                             ).then_inc(sems[rl_sem], 16)), gate=gates)
            sc_sem = S.use_sem("srcl")
            gates = [src_marks[q]]
            if prev_src_mm:
                gates.append(("ve", prev_src_mm))
            sc_t = dma_load(sc_sem, (lambda sy, q=q:
                sy.dma_start(out=src_bf[:], in_=hb["src_v"][:, :, q * TC:q * TC + TC]
                             ).then_inc(sems["srcl"], 16)), gate=gates)

            ve_radd = {}
            te_kv = {}
            for cj in range(CJ):
                bsl = cj % 2
                if cj == 0:
                    S.wait("tensor", "kfl", kf_l[-1])
                    if q == 0:
                        for piece in range(4):
                            S.wait("tensor", "wq" + str(piece), wcv_t[piece])
                if cj >= 2:
                    S.wait("tensor", "ve", ve_radd[cj - 2])
                elif q > 0:
                    S.wait("tensor", "ve", prev_radd[cj + CJ - 2])
                def _kvmm(t, cj=cj, bsl=bsl):
                    last_i = None
                    for fj in range(FJ):
                        last_i = t.matmul(psum[bsl][:], wcv_v[:, fj, cj * 128:cj * 128 + 128],
                                          kfin[:, fj], start=(fj == 0), stop=(fj == FJ - 1))
                    last_i.then_inc(sems["te"], 1)
                S.emit("tensor", _kvmm)
                te_kv[cj] = S.inc("te", 1)
                S.wait("vector", "te", te_kv[cj])
                if cj == 0:
                    S.wait("vector", f"rld{sl}", rl_t)
                    S.wait("vector", "srcl", sc_t)
                def _kvadd(v, cj=cj, bsl=bsl, sl=sl):
                    v.tensor_tensor(out=kvscr[:], in0=src_bf[:, cj], in1=psum[bsl][:],
                                    op=ALU.mult)
                    v.tensor_tensor(out=r_sb[:, sl, cj], in0=r_sb[:, sl, cj], in1=kvscr[:],
                                    op=ALU.add)
                ve_radd[cj] = S.emit("vector", _kvadd, n=2)
            prev_kv_last = te_kv[CJ - 1]
            prev_ve_done = ve_radd[CJ - 1]
            prev_src_mm = ve_radd[CJ - 1]
            prev_radd = ve_radd

            if not last:
                S.wait("scalar", "ve", ve_radd[CJ - 1])
                S.use_sem(f"rst{sl}")
                def _stR(s, q=q, sl=sl):
                    s.dma_start(out=hb["r_v"][:, :, q * TC:q * TC + TC],
                                in_=r_sb[:, sl]).then_inc(sems[f"rst{sl}"], 16)
                S.emit("scalar", _stR)
                r_marks[q] = (f"rst{sl}", S.inc(f"rst{sl}", 16))
                prev_rstB = r_marks[q]
            else:
                rT_stage = carve(pool, 11776, [128, 4, 1024])
                te_tr = {}
                ve_evr = {}
                for cj in range(CJ):
                    S.wait("tensor", "ve", ve_radd[cj])
                    if cj > 0:
                        S.wait("tensor", "ve", ve_evr[cj - 1])
                    if prev_rT and cj == 0:
                        S.wait("tensor", prev_rT[0], prev_rT[1])
                    def _trr(t, cj=cj, sl=sl):
                        last_i = None
                        for tt in range(4):
                            last_i = t.transpose(psum[6][:, tt * 128:tt * 128 + 128],
                                                 r_sb[:, sl, cj, tt * 128:tt * 128 + 128],
                                                 ident32[:])
                        last_i.then_inc(sems["te"], 1)
                    S.emit("tensor", _trr)
                    te_tr[cj] = S.inc("te", 1)
                    S.wait("vector", "te", te_tr[cj])
                    def _evr(v, cj=cj):
                        v.tensor_copy(out=rT_stage[:, :, cj * 128:cj * 128 + 128],
                                      in_=psum[6][:].rearrange("p (a b) -> p a b", b=128))
                    ve_evr[cj] = S.emit("vector", _evr, n=1)
                S.wait("scalar", "ve", ve_evr[CJ - 1])
                rT_sem = S.use_sem("rTs")
                def _stT(s, q=q):
                    s.dma_start(out=hb["rT_v"][q * 4:q * 4 + 4].rearrange("a p c -> p a c"),
                                in_=rT_stage[:]).then_inc(sems["rTs"], 16)
                S.emit("scalar", _stT)
                prev_rT = ("rTs", S.inc("rTs", 16))

        S.barrier()
        if stop_after == f"cm{l}":
            return
        region_free[0] = ("te", prev_kv_last)
        region_free[1] = ("te", prev_kv_last)
        region_free[2] = ("te", prev_kv_last)
        region_free[3] = ("te", prev_kv_last)
        region_free[4] = ("te", 0)

    if stop_after == "layers":
        return
    _emit_ffn(nc, S, hb, psum, psbf, ident32, identbf, T, arena, pool,
              r_sb, h_buf, carve)


def emit_quant(S, q, src, dst, width, pregate=()):
    G = width // 16
    for eng, sem, tgt in pregate:
        if tgt:
            S.wait(eng, sem, tgt)

    def _v1(v, src=src):
        v.tensor_reduce(out=q["bm"][:, 0:G], in_=src.rearrange("p (g k) -> p g k", k=16),
                        axis=AX.X, op=ALU.max, apply_absolute_value=True)
        v.tensor_scalar(out=q["sc"][:, 0:G], in0=q["bm"][:, 0:G], scalar1=1.0 / 6.0,
                        scalar2=1e-12, op0=ALU.mult, op1=ALU.max)
        v.reciprocal(out=q["iv"][:, 0:G], in_=q["sc"][:, 0:G])
        v.tensor_tensor(out=q["y"][:, 0:width].rearrange("p (g k) -> p g k", k=16),
                        in0=src.rearrange("p (g k) -> p g k", k=16),
                        in1=q["iv"][:, 0:G].unsqueeze(2).broadcast_to([128, G, 16]),
                        op=ALU.mult)
    ve_y = S.emit("vector", _v1, n=4)
    S.wait("scalar", "ve", ve_y)

    def _s1(s):
        s.activation(out=q["a"][:, 0:width], in_=q["y"][:, 0:width], func=AF.Abs)
    se_a = S.emit("scalar", _s1, n=1)

    def _v2(v):
        v.tensor_scalar(out=q["qA"][:, 0:width], in0=q["y"][:, 0:width], scalar1=2.0,
                        scalar2=RNE, op0=ALU.mult, op1=ALU.add)
        v.tensor_scalar(out=q["qA"][:, 0:width], in0=q["qA"][:, 0:width], scalar1=RNE,
                        scalar2=0.5, op0=ALU.subtract, op1=ALU.mult)
        v.tensor_scalar(out=q["qB"][:, 0:width], in0=q["y"][:, 0:width], scalar1=RNE,
                        scalar2=RNE, op0=ALU.add, op1=ALU.subtract)
        v.tensor_scalar(out=q["qC"][:, 0:width], in0=q["y"][:, 0:width], scalar1=0.5,
                        scalar2=RNE, op0=ALU.mult, op1=ALU.add)
        v.tensor_scalar(out=q["qC"][:, 0:width], in0=q["qC"][:, 0:width], scalar1=RNE,
                        scalar2=2.0, op0=ALU.subtract, op1=ALU.mult)
    S.emit("vector", _v2, n=5)
    S.wait("vector", "se", se_a)

    def _v3(v, dst=dst):
        v.tensor_single_scalar(out=q["m1"][:, 0:width], in_=q["a"][:, 0:width],
                               scalar=2.25, op=ALU.is_lt)
        v.tensor_single_scalar(out=q["m2"][:, 0:width], in_=q["a"][:, 0:width],
                               scalar=4.5, op=ALU.is_lt)
        v.tensor_copy(out=q["y"][:, 0:width], in_=q["qC"][:, 0:width])
        v.copy_predicated(out=q["y"][:, 0:width], mask=q["m2"][:, 0:width],
                          data=q["qB"][:, 0:width])
        v.copy_predicated(out=q["y"][:, 0:width], mask=q["m1"][:, 0:width],
                          data=q["qA"][:, 0:width])
        v.tensor_tensor(out=dst.rearrange("p (g k) -> p g k", k=16),
                        in0=q["y"][:, 0:width].rearrange("p (g k) -> p g k", k=16),
                        in1=q["sc"][:, 0:G].unsqueeze(2).broadcast_to([128, G, 16]),
                        op=ALU.mult)
    return S.emit("vector", _v3, n=6)


def _emit_ffn(nc, S, hb, psum, psbf, ident32, identbf, T, arena, pool,
              r_sb, h_buf, carve):
    TT = T // 128
    sems = S.sems
    NJ = [(n0, min(512, II - n0)) for n0 in range(0, II, 512)]
    QW = 1408

    rflat = r_sb[:].rearrange("p a b c -> p (a b c)")
    hflat = h_buf[:].rearrange("p a b -> p (a b)")

    wgate_q = carve(arena, 0, [128, CJ, II], BF16)
    wup_main = carve(arena, CJ * II, [128, 6, II], BF16)
    wup_tail = carve(pool, 0, [128, 2, II], BF16)

    def wup_rhs(cj, n0, nw):
        if cj < 6:
            return wup_main[:, cj, n0:n0 + nw]
        return wup_tail[:, cj - 6, n0:n0 + nw]

    q = {"bm": carve(pool, 2816, [128, QW // 16]),
         "sc": carve(pool, 2904, [128, QW // 16]),
         "iv": carve(pool, 2992, [128, QW // 16]),
         "y": carve(pool, 3080, [128, QW]),
         "a": carve(pool, 4488, [128, QW]),
         "qA": carve(pool, 5896, [128, QW]),
         "qB": carve(pool, 7304, [128, QW]),
         "qC": carve(pool, 8712, [128, QW]),
         "m1": carve(pool, 10120, [128, QW], U8),
         "m2": carve(pool, 10472, [128, QW], U8)}
    gate_s = carve(pool, 10824, [128, 2, 512])
    h_f = carve(rflat, 4096, [128, II])
    hq_bf = carve(pool, 11848, [128, 2, II], BF16)
    wtile = carve(pool, 10824, [128, 2, 1024])
    wq_bf = carve(pool, 12872, [128, 2, 1024], BF16)
    xqT_in = carve(rflat, 0, [128, 2, CJ, 512], BF16)
    hqTst = carve(hflat, 0, [128, 2, IJ, 128], BF16)
    wtileD = carve(pool, 10824, [128, II])
    wq_bfD = carve(rflat, 0, [128, 2, II], BF16)
    hqT_in = carve(rflat, 2816, [128, 2, IJ, 128], BF16)
    rx_in = carve(hflat, 0, [128, 2, 2, 1024])
    out_t = carve(pool, 0, [128, 2, 1024])

    ve_wq = {}
    te_wtr = {}
    se_wev = {}
    for gi in range(2 * IJ):
        wi, tile = divmod(gi, IJ)
        sl = gi % 2
        wl_sem = S.use_sem(f"fwl{sl}")
        prev = ve_wq.get(gi - 2)
        if prev:
            S.wait("sync", "ve", prev)
        w_h = hb["wg"] if wi == 0 else hb["wu"]
        wl_t = S.dma("sync", wl_sem, (lambda sy, w_h=w_h, tile=tile, sl=sl, wl_sem=wl_sem:
            sy.dma_start(out=wtile[:, sl],
                         in_=w_h[tile * 128:tile * 128 + 128, :]
                         ).then_inc(sems[wl_sem], 16)))
        gates = [("vector", wl_sem, wl_t)]
        prev_tr = te_wtr.get(gi - 2)
        if prev_tr:
            gates.append(("vector", "te", prev_tr))
        ve_wq[gi] = emit_quant(S, q, wtile[:, sl], wq_bf[:, sl], 1024,
                               pregate=gates)
        S.wait("tensor", "ve", ve_wq[gi])
        if (gi - 1) in se_wev:
            S.wait("tensor", "se", se_wev[gi - 1])
        def _wtr(t, sl=sl):
            last_i = None
            for cj in range(CJ):
                last_i = t.transpose(psbf[:, cj * 128:cj * 128 + 128],
                                     wq_bf[:, sl, cj * 128:cj * 128 + 128], identbf[:])
            last_i.then_inc(sems["te"], 1)
        S.emit("tensor", _wtr)
        te_wtr[gi] = S.inc("te", 1)
        S.wait("scalar", "te", te_wtr[gi])

        if wi == 0:
            def _wev0(s, tile=tile):
                s.copy(out=wgate_q[:, :, tile * 128:tile * 128 + 128],
                       in_=psbf[:].rearrange("p (a b) -> p a b", b=128))
            se_wev[gi] = S.emit("scalar", _wev0, n=1)
        else:
            def _wev1(s, tile=tile):
                s.copy(out=wup_main[:, 0:6, tile * 128:tile * 128 + 128],
                       in_=psbf[:, 0:768].rearrange("p (a b) -> p a b", b=128))
                s.copy(out=wup_tail[:, :, tile * 128:tile * 128 + 128],
                       in_=psbf[:, 768:1024].rearrange("p (a b) -> p a b", b=128))
            se_wev[gi] = S.emit("scalar", _wev1, n=2)

    S.barrier()

    hqs_marks = [None] * TT
    prev_hqst = [None, None]
    prev_silu_last = 0
    prev_h_last = 0
    prev_hev_last = [0]
    hq_ve = [0, 0]
    hq_tr_done = [0, 0]
    xq_t = {}
    xq_mm_last = [0, 0]
    for i in range(TT):
        g4 = i // 4
        xsl = g4 % 2
        tsub = i % 4
        if tsub == 0:
            xq_sem = S.use_sem(f"fxq{xsl}")
            if g4 >= 2:
                S.wait("sync", "te", xq_mm_last[xsl])
            xq_t[g4] = S.dma("sync", xq_sem, (lambda sy, g4=g4, xsl=xsl, xq_sem=xq_sem:
                sy.dma_start(out=xqT_in[:, xsl],
                             in_=hb["xq_v"][:, :, g4 * 512:g4 * 512 + 512]
                             ).then_inc(sems[xq_sem], 16)))

        te_g = {}
        se_silu = {}
        ve_hpart = {}
        for nj, (n0, nw) in enumerate(NJ):
            bsl = nj % 2
            if nj == 0 and tsub == 0:
                S.wait("tensor", f"fxq{xsl}", xq_t[g4])
            if nj >= 2:
                S.wait("tensor", "se", se_silu[nj - 2])
                S.wait("tensor", "ve", ve_hpart[nj - 2])
            elif i > 0:
                S.wait("tensor", "se", prev_silu_last)
                S.wait("tensor", "ve", prev_h_last)
            def _gmm(t, n0=n0, nw=nw, bsl=bsl, xsl=xsl, tsub=tsub):
                last_i = None
                for cj in range(CJ):
                    last_i = t.matmul(psum[bsl][:, 0:nw],
                                      xqT_in[:, xsl, cj, tsub * 128:tsub * 128 + 128],
                                      wgate_q[:, cj, n0:n0 + nw],
                                      start=(cj == 0), stop=(cj == CJ - 1))
                for cj in range(CJ):
                    last_i = t.matmul(psum[2 + bsl][:, 0:nw],
                                      xqT_in[:, xsl, cj, tsub * 128:tsub * 128 + 128],
                                      wup_rhs(cj, n0, nw),
                                      start=(cj == 0), stop=(cj == CJ - 1))
                last_i.then_inc(sems["te"], 1)
            S.emit("tensor", _gmm)
            te_g[nj] = S.inc("te", 1)
            S.wait("scalar", "te", te_g[nj])
            if nj >= 2:
                S.wait("scalar", "ve", ve_hpart[nj - 2])
            elif i > 0:
                S.wait("scalar", "ve", prev_h_last)
            def _silu(s, n0=n0, nw=nw, bsl=bsl):
                s.activation(out=gate_s[:, bsl, 0:nw], in_=psum[bsl][:, 0:nw], func=AF.Silu)
            se_silu[nj] = S.emit("scalar", _silu, n=1)
            S.wait("vector", "se", se_silu[nj])
            S.wait("vector", "te", te_g[nj])
            def _hp(v, n0=n0, nw=nw, bsl=bsl):
                v.tensor_tensor(out=h_f[:, n0:n0 + nw], in0=gate_s[:, bsl, 0:nw],
                                in1=psum[2 + bsl][:, 0:nw], op=ALU.mult)
            ve_hpart[nj] = S.emit("vector", _hp, n=1)
        prev_silu_last = se_silu[len(NJ) - 1]
        prev_h_last = ve_hpart[len(NJ) - 1]

        hsl = i % 2
        gates = [("vector", "ve", ve_hpart[len(NJ) - 1])]
        if i >= 2 and hq_tr_done[hsl]:
            gates.append(("vector", "te", hq_tr_done[hsl]))
        ve_q1 = emit_quant(S, q, h_f[:, 0:QW], hq_bf[:, hsl, 0:QW], QW,
                           pregate=gates)
        ve_q2 = emit_quant(S, q, h_f[:, QW:II], hq_bf[:, hsl, QW:II], II - QW)
        hq_ve[hsl] = ve_q2

        def _emit_tr(ix):
            hslx = ix % 2
            se_hev = {}
            if prev_hqst[hslx]:
                S.wait("scalar", prev_hqst[hslx][0], prev_hqst[hslx][1])
            te_htr = None
            for rnd in range(3):
                lo = rnd * 8
                hi = min(lo + 8, IJ)
                S.wait("tensor", "ve", hq_ve[hslx])
                if rnd > 0:
                    S.wait("tensor", "se", se_hev[rnd - 1])
                elif prev_hev_last[0]:
                    S.wait("tensor", "se", prev_hev_last[0])
                def _htr(t, lo=lo, hi=hi, hslx=hslx):
                    last_i = None
                    for ij in range(lo, hi):
                        last_i = t.transpose(psbf[:, (ij - lo) * 128:(ij - lo) * 128 + 128],
                                             hq_bf[:, hslx, ij * 128:ij * 128 + 128],
                                             identbf[:])
                    last_i.then_inc(sems["te"], 1)
                S.emit("tensor", _htr)
                te_htr = S.inc("te", 1)
                S.wait("scalar", "te", te_htr)
                def _hev(s, lo=lo, hi=hi, hslx=hslx):
                    s.copy(out=hqTst[:, hslx, lo:hi, :],
                           in_=psbf[:, 0:(hi - lo) * 128].rearrange("p (a b) -> p a b", b=128))
                se_hev[rnd] = S.emit("scalar", _hev, n=1)
            prev_hev_last[0] = se_hev[2]
            hq_tr_done[hslx] = te_htr
            hq_sem = S.use_sem(f"fhq{hslx}")
            S.wait("scalar", "se", se_hev[2])
            def _hst(s, ix=ix, hslx=hslx, hq_sem=hq_sem):
                s.dma_start(out=hb["hqT_v"][:, :, ix * 128:ix * 128 + 128],
                            in_=hqTst[:, hslx]).then_inc(sems[hq_sem], 16)
            S.emit("scalar", _hst)
            hqs_marks[ix] = (hq_sem, S.inc(hq_sem, 16))
            prev_hqst[hslx] = hqs_marks[ix]

        if i >= 1:
            _emit_tr(i - 1)
        if i == TT - 1:
            _emit_tr(i)
        if tsub == 3:
            xq_mm_last[xsl] = te_g[len(NJ) - 1]

    S.barrier()

    wdown_full = carve(arena, 0, [128, IJ, 1024], BF16)

    def wdown_q_view(tile, lo, hi):
        return wdown_full[:, lo:hi, tile * 128:tile * 128 + 128]

    ve_wqd = {}
    te_wtrd = {}
    se_wevd = {}
    for tile in range(CJ):
        sl = tile % 2
        wl_sem = S.use_sem("fwl0")
        prev = ve_wqd.get(tile - 1)
        if prev:
            S.wait("sync", "ve", prev)
        wl_t = S.dma("sync", wl_sem, (lambda sy, tile=tile, wl_sem=wl_sem:
            sy.dma_start(out=wtileD[:],
                         in_=hb["wd"][tile * 128:tile * 128 + 128, :]
                         ).then_inc(sems[wl_sem], 16)))
        gates = [("vector", wl_sem, wl_t)]
        prev_tr = te_wtrd.get(tile - 2)
        if prev_tr:
            gates.append(("vector", "te", prev_tr))
        emit_quant(S, q, wtileD[:, 0:QW], wq_bfD[:, sl, 0:QW], QW, pregate=gates)
        ve_wqd[tile] = emit_quant(S, q, wtileD[:, QW:II], wq_bfD[:, sl, QW:II],
                                  II - QW)
        S.wait("tensor", "ve", ve_wqd[tile])
        last_te = None
        for rnd in range(3):
            lo = rnd * 8
            hi = min(lo + 8, IJ)
            if rnd > 0 or (tile - 1) in se_wevd:
                S.wait("tensor", "se", se_wevd[tile - 1] if rnd == 0 else se_rnd_prev)
            def _wtrd(t, sl=sl, lo=lo, hi=hi):
                last_i = None
                for ij in range(lo, hi):
                    last_i = t.transpose(psbf[:, (ij - lo) * 128:(ij - lo) * 128 + 128],
                                         wq_bfD[:, sl, ij * 128:ij * 128 + 128], identbf[:])
                last_i.then_inc(sems["te"], 1)
            S.emit("tensor", _wtrd)
            te_w = S.inc("te", 1)
            S.wait("scalar", "te", te_w)
            def _wevd(s, tile=tile, lo=lo, hi=hi):
                s.copy(out=wdown_q_view(tile, lo, hi),
                       in_=psbf[:, 0:(hi - lo) * 128].rearrange("p (a b) -> p a b", b=128))
            se_rnd_prev = S.emit("scalar", _wevd, n=1)
        se_wevd[tile] = se_rnd_prev
        te_wtrd[tile] = te_w

    S.barrier()

    wdown_q = carve(arena, 0, [128, IJ, 1024], BF16)
    prev_out_st = [None, None]
    prev_dmm = [0, 0]
    prev_out_ve = [0, 0]
    prev_comb = [0, 0]
    for i in range(TT):
        sl = i % 2
        hl_sem = S.use_sem(f"fhl{sl}")
        gates = [hqs_marks[i]]
        if prev_dmm[sl]:
            gates.append(("te", prev_dmm[sl]))
        for gsem, gtgt in gates:
            if gtgt:
                S.wait("sync", gsem, gtgt)
        hl_t = S.dma("sync", hl_sem, (lambda sy, i=i, sl=sl, hl_sem=hl_sem:
            sy.dma_start(out=hqT_in[:, sl],
                         in_=hb["hqT_v"][:, :, i * 128:i * 128 + 128]
                         ).then_inc(sems[hl_sem], 16)))
        rx_sem = S.use_sem(f"frx{sl}")
        if prev_out_ve[sl]:
            S.wait("sync", "ve", prev_out_ve[sl])
        S.dma("sync", rx_sem, (lambda sy, i=i, sl=sl, rx_sem=rx_sem:
            sy.dma_start(out=rx_in[:, sl, 0], in_=hb["rT_v"][i]
                         ).then_inc(sems[rx_sem], 16)))
        rx_t = S.dma("sync", rx_sem, (lambda sy, i=i, sl=sl, rx_sem=rx_sem:
            sy.dma_start(out=rx_in[:, sl, 1], in_=hb["x_v"][i]
                         ).then_inc(sems[rx_sem], 16)))

        ve_o = {}
        for hj in range(2):
            bsl = hj
            S.wait("tensor", hl_sem, hl_t)
            if prev_comb[hj]:
                S.wait("tensor", "ve", prev_comb[hj])
            def _dmm(t, hj=hj, bsl=bsl, sl=sl):
                last_i = None
                for ij in range(IJ):
                    last_i = t.matmul(psum[4 + bsl][:], hqT_in[:, sl, ij],
                                      wdown_q[:, ij, hj * 512:hj * 512 + 512],
                                      start=(ij == 0), stop=(ij == IJ - 1))
                last_i.then_inc(sems["te"], 1)
            S.emit("tensor", _dmm)
            te_d = S.inc("te", 1)
            S.wait("vector", "te", te_d)
            S.wait("vector", rx_sem, rx_t)
            if prev_out_st[sl] and hj == 0:
                S.wait("vector", prev_out_st[sl][0], prev_out_st[sl][1])
            def _comb(v, hj=hj, bsl=bsl, sl=sl):
                v.tensor_tensor(out=out_t[:, sl, hj * 512:hj * 512 + 512],
                                in0=psum[4 + bsl][:],
                                in1=rx_in[:, sl, 0, hj * 512:hj * 512 + 512], op=ALU.add)
                v.tensor_tensor(out=out_t[:, sl, hj * 512:hj * 512 + 512],
                                in0=out_t[:, sl, hj * 512:hj * 512 + 512],
                                in1=rx_in[:, sl, 1, hj * 512:hj * 512 + 512], op=ALU.add)
            ve_o[hj] = S.emit("vector", _comb, n=2)
        prev_dmm[sl] = S.ctr["te"]
        prev_out_ve[sl] = ve_o[1]
        prev_comb[0] = ve_o[0]
        prev_comb[1] = ve_o[1]

        ost_sem = S.use_sem(f"ost{sl}")
        S.wait("scalar", "ve", ve_o[1])
        def _ost(s, i=i, sl=sl, ost_sem=ost_sem):
            s.dma_start(out=hb["out_v"][i], in_=out_t[:, sl]).then_inc(sems[ost_sem], 16)
        S.emit("scalar", _ost)
        prev_out_st[sl] = (ost_sem, S.inc(ost_sem, 16))


def _host_layout(p):
    ac = np.ascontiguousarray
    lay = {}
    lay["wkT"] = ac(np.transpose(p["L0_attn_Wk"], (0, 2, 1)))
    lay["wvT"] = ac(np.transpose(p["L0_attn_Wv"], (0, 2, 1)))
    lay["wrT"] = ac(np.transpose(p["L0_attn_Wr"], (0, 2, 1)))
    lay["wcrT"] = ac(np.transpose(p["L0_attn_Wcr"], (0, 2, 1)))
    lay["woT"] = ac(np.transpose(p["L0_attn_Wo"], (0, 2, 1)))
    lay["wckT"] = ac(np.transpose(p["L0_attn_Wck"], (0, 2, 1)))
    lay["wcvT"] = ac(np.transpose(p["L0_attn_Wcv"], (0, 2, 1)))
    lay["vecs"] = ac(np.stack([p["L0_attn_ln1_w"], p["L0_attn_ln1_b"],
                               p["L0_attn_ln2_w"], p["L0_attn_ln2_b"],
                               p["L0_attn_tmk"], p["L0_attn_tmv"], p["L0_attn_tmr"],
                               p["L0_attn_cmk"], p["L0_attn_cmr"],
                               p["L0_attn_time_decay"], p["L0_attn_time_first"]],
                              axis=-1))
    lay["ln0"] = ac(np.stack([p["L0_attn_ln0_w"], p["L0_attn_ln0_b"]], axis=0))
    lay["wg"] = ac(p["L0_ffn_w_gate"])
    lay["wu"] = ac(p["L0_ffn_w_up"])
    lay["wd"] = ac(p["L0_ffn_w_down"])
    return lay


def kernel(**inputs):
    x = np.asarray(inputs["L0_attn_x"])
    B = x.shape[0]
    lay = _host_layout({k: np.asarray(v) for k, v in inputs.items()})
    nc = build_module(T=x.shape[1], L=4)
    in_maps = []
    for b in range(B):
        m = dict(lay)
        m["x"] = np.ascontiguousarray(x[b])
        in_maps.append(m)
    res = run_bass_kernel_spmd(nc, in_maps, core_ids=list(range(B)))
    return np.stack([res.results[b]["out"] for b in range(B)], axis=0)


# version 08
# speedup vs baseline: 1.1318x; 1.0814x over previous
import math
import contextlib
import numpy as np
import concourse.bass as bass
import concourse.mybir as mybir
from concourse.bass_utils import run_bass_kernel_spmd

F32 = mybir.dt.float32
BF16 = mybir.dt.bfloat16
U8 = mybir.dt.uint8
AF = mybir.ActivationFunctionType
ALU = mybir.AluOpType
AX = mybir.AxisListType

C = 1024
CJ = 8
FF = 4096
FJ = 32
II = 2816
IJ = 22
TC = 512
EPS = 1e-5
RNE = 12582912.0

V_LN1W, V_LN1B, V_LN2W, V_LN2B, V_TMK, V_TMV, V_TMR, V_CMK, V_CMR, V_TD, V_TF = range(11)
NV = 11

GUARD_OPS = {
    "tensor_tensor", "tensor_scalar", "tensor_copy", "tensor_add", "tensor_sub",
    "tensor_mul", "tensor_max", "tensor_relu", "tensor_single_scalar",
    "tensor_scalar_add", "tensor_scalar_sub", "tensor_scalar_mul",
    "tensor_scalar_max", "tensor_scalar_min", "tensor_reduce",
    "scalar_tensor_tensor", "tensor_tensor_scan", "reciprocal", "select",
    "copy_predicated",
    "memset", "memzero", "iota", "affine_select", "activation", "copy",
    "add", "mul", "sign", "square", "sqrt",
}

ENG_SEM = {"sync": "te", "scalar": "se", "vector": "ve", "tensor": "te", "gpsimd": "ge"}


class Guard:

    def __init__(self, eng, sem, guard=True):
        self._e = eng
        self._sem = sem
        self._n = 0
        self._guard = guard

    def bump(self, k):
        self._n += k

    def __getattr__(self, name):
        attr = getattr(self._e, name)
        if self._guard and name in GUARD_OPS:
            def call(*a, **k):
                if self._n > 0:
                    self._e.wait_ge(self._sem, self._n)
                inst = attr(*a, **k)
                inst.then_inc(self._sem, 1)
                self._n += 1
                return inst
            return call
        return attr


class Sched:
    ENGINES = ("sync", "scalar", "vector", "tensor", "gpsimd")

    def __init__(self):
        self.streams = {e: [] for e in self.ENGINES}
        self.ctr = {}
        self.sems = {}
        self.sem_names = []
        for s in ("te", "ve", "se", "ge"):
            self.use_sem(s)

    def use_sem(self, name):
        if name not in self.ctr:
            self.ctr[name] = 0
            self.sem_names.append(name)
        return name

    def dma(self, eng, sem, fn):
        self.use_sem(sem)
        self.streams[eng].append(fn)
        return self.inc(sem, 16)

    def inc(self, name, amount):
        self.ctr[name] += amount
        return self.ctr[name]

    def emit(self, eng, fn, n=0):
        self.streams[eng].append(fn)
        if n:
            sem = ENG_SEM[eng]
            assert sem in ("ve", "se", "ge", "te")
            self.ctr[sem] += n
        return self.ctr[ENG_SEM[eng]]

    def wait(self, eng, sem, tgt):
        assert tgt <= self.ctr[sem], (eng, sem, tgt, self.ctr[sem])
        sems = self.sems

        def _w(e, sem=sem, tgt=tgt):
            e.wait_ge(sems[sem], tgt)
        self.streams[eng].append(_w)

    def barrier(self):
        marks = {sem: self.ctr[sem] for sem in self.sem_names}
        for eng in self.ENGINES:
            for sem, tgt in marks.items():
                if tgt:
                    self.wait(eng, sem, tgt)


def build_module(T=4096, L=4, taps=(), stop_after=None):
    NT = T // TC
    TT = T // 128
    nc = bass.Bass("TRN2", target_bir_lowering=False)
    S = Sched()

    def tk(name):
        return "ExternalOutput" if name in taps else "Internal"

    x_h = nc.dram_tensor("x", [T, C], F32, kind="ExternalInput").ap()
    wkT_h = nc.dram_tensor("wkT", [L, C, C], F32, kind="ExternalInput").ap()
    wvT_h = nc.dram_tensor("wvT", [L, C, C], F32, kind="ExternalInput").ap()
    wrT_h = nc.dram_tensor("wrT", [L, C, C], F32, kind="ExternalInput").ap()
    wcrT_h = nc.dram_tensor("wcrT", [L, C, C], F32, kind="ExternalInput").ap()
    woT_h = nc.dram_tensor("woT", [L, C, C], F32, kind="ExternalInput").ap()
    wckT_h = nc.dram_tensor("wckT", [L, C, FF], F32, kind="ExternalInput").ap()
    wcvT_h = nc.dram_tensor("wcvT", [L, FF, C], F32, kind="ExternalInput").ap()
    vecs_h = nc.dram_tensor("vecs", [L, C, NV], F32, kind="ExternalInput").ap()
    ln0_h = nc.dram_tensor("ln0", [2, C], F32, kind="ExternalInput").ap()
    wg_h = nc.dram_tensor("wg", [II, C], F32, kind="ExternalInput").ap()
    wu_h = nc.dram_tensor("wu", [II, C], F32, kind="ExternalInput").ap()
    wd_h = nc.dram_tensor("wd", [C, II], F32, kind="ExternalInput").ap()

    r_hbm = nc.dram_tensor("r_hbm", [C, T], F32, kind=tk("r_hbm")).ap()
    xq_hbm = nc.dram_tensor("xq_hbm", [C, T], BF16, kind=tk("xq_hbm")).ap()
    xk2_hbm = nc.dram_tensor("xk2_hbm", [C, T], BF16, kind=tk("xk2_hbm")).ap()
    xr2_hbm = nc.dram_tensor("xr2_hbm", [C, T], BF16, kind=tk("xr2_hbm")).ap()
    kf_hbm = nc.dram_tensor("kf_hbm", [FF, T], BF16, kind=tk("kf_hbm")).ap()
    rT_hbm = nc.dram_tensor("rT_hbm", [T, C], F32, kind=tk("rT_hbm")).ap()
    src_hbm = nc.dram_tensor("src_hbm", [C, T], BF16, kind=tk("src_hbm")).ap()
    hqT_hbm = nc.dram_tensor("hqT_hbm", [II, T], BF16, kind=tk("hqT_hbm")).ap()
    out_h = nc.dram_tensor("out", [T, C], F32, kind="ExternalOutput").ap()

    x_v = x_h.rearrange("(tt p) c -> tt p c", p=128)
    out_v = out_h.rearrange("(tt p) c -> tt p c", p=128)
    rT_v = rT_hbm.rearrange("(tt p) c -> tt p c", p=128)
    r_v = r_hbm.rearrange("(cj p) t -> p cj t", p=128)
    xq_v = xq_hbm.rearrange("(cj p) t -> p cj t", p=128)
    xk2_v = xk2_hbm.rearrange("(cj p) t -> p cj t", p=128)
    xr2_v = xr2_hbm.rearrange("(cj p) t -> p cj t", p=128)
    kf_vv = kf_hbm.rearrange("(fj p) t -> p fj t", p=128)
    src_vv = src_hbm.rearrange("(cj p) t -> p cj t", p=128)
    hqT_v = hqT_hbm.rearrange("(ij p) t -> p ij t", p=128)

    alloc = contextlib.ExitStack()
    e = alloc.enter_context

    ident32 = e(nc.sbuf_tensor("ident32", [128, 128], F32))
    identbf = e(nc.sbuf_tensor("identbf", [128, 128], BF16))
    onescol_bf = e(nc.sbuf_tensor("onescol_bf", [128, 1], BF16))
    onesrow_f = e(nc.sbuf_tensor("onesrow_f", [1, 128], F32))
    epsc = e(nc.sbuf_tensor("epsc", [128, 1], F32))
    nc.const_aps.register(F32, EPS, epsc.ap())
    psum = [e(nc.psum_tensor(f"ps{i}", [128, 512], F32)) for i in range(7)]
    psbf = e(nc.psum_tensor("psbf", [128, 1024], BF16))

    AR = 8192
    arena = e(nc.sbuf_tensor("w_arena", [128, 5 * AR], BF16))
    pool = e(nc.sbuf_tensor("poolL", [128, 15872], F32))
    r_sb_t = e(nc.sbuf_tensor("r_sb", [128, 2, CJ, TC], F32))
    h_buf_t = e(nc.sbuf_tensor("h_buf", [128, CJ, TC + 1], F32))

    def carve(base, off, shape, dtype=F32):
        n = math.prod(shape[1:])
        if dtype == F32 and base.dtype == F32:
            v = base[:, off:off + n]
        elif dtype == BF16 and base.dtype == F32:
            v = base[:, off:off + (n + 1) // 2].bitcast(BF16)
        elif dtype == BF16 and base.dtype == BF16:
            v = base[:, off:off + n]
        elif dtype == F32 and base.dtype == BF16:
            v = base[:, off:off + 2 * n].bitcast(F32)
        elif dtype == U8 and base.dtype == F32:
            v = base[:, off:off + (n + 3) // 4].bitcast(U8)
        else:
            raise ValueError((base.dtype, dtype))
        if len(shape) == 2:
            return v
        if len(shape) == 3:
            return v.rearrange("p (a b) -> p a b", b=shape[2])
        return v.rearrange("p (a b c) -> p a b c", b=shape[2], c=shape[3])

    def _ident(g):
        g.memset(epsc[:], EPS)
        g.memset(ident32[:], 0.0)
        g.affine_select(out=ident32[:], in_=ident32[:], compare_op=ALU.not_equal,
                        fill=1.0, base=0, pattern=[[-1, 128]], channel_multiplier=1)
        g.memset(identbf[:], 0.0)
        g.affine_select(out=identbf[:], in_=identbf[:], compare_op=ALU.not_equal,
                        fill=1.0, base=0, pattern=[[-1, 128]], channel_multiplier=1)
        g.memset(onescol_bf[:], 1.0)
        g.memset(onesrow_f[:], 1.0)
    ge_ident = S.emit("gpsimd", _ident, n=7)
    for eng in ("sync", "scalar", "vector", "tensor"):
        S.wait(eng, "ge", ge_ident)

    if True:
        xin = carve(pool, 0, [128, 2, C])
        w0b = carve(pool, 2048, [128, C])
        b0b = carve(pool, 3072, [128, C])
        sqs = carve(pool, 4096, [128, C])
        cps = carve(pool, 5120, [128, C])
        nprime = carve(pool, 6144, [128, C])
        xn_t = carve(pool, 7168, [128, C])
        n2 = carve(pool, 8192, [128, C])
        r0_t = carve(pool, 9216, [128, C])
        scol = carve(pool, 10240, [128, 2, 16])
        bmax = carve(pool, 10272, [128, C // 16])
        qsc = carve(pool, 10336, [128, C // 16])
        qinv = carve(pool, 10400, [128, C // 16])
        qy = carve(pool, 10464, [128, C])
        qa = carve(pool, 11488, [128, C])
        qA = carve(arena, 24576, [128, C], F32)
        qB = carve(arena, 26624, [128, C], F32)
        qC = carve(arena, 28672, [128, C], F32)
        qm1 = carve(pool, 12512, [128, C], U8)
        qm2 = carve(pool, 12768, [128, C], U8)
        xq_t = carve(pool, 13024, [128, C], BF16)
        rstage = carve(arena, 0, [128, 2, CJ, 512], F32)
        qstage = carve(arena, 32768, [128, 2, CJ, 512], BF16)

        S.use_sem("cst")

        def _ld0(sy):
            sy.dma_start(out=w0b[:], in_=ln0_h[0:1, :].partition_broadcast(128)).then_inc(S.sems["cst"], 16)
            sy.dma_start(out=b0b[:], in_=ln0_h[1:2, :].partition_broadcast(128)).then_inc(S.sems["cst"], 16)
        S.emit("sync", _ld0)
        ld_ln0 = S.inc("cst", 32)

        ve_tile = []
        se_tile = []
        st_span = []

        for i in range(TT):
            slot = i % 2
            span = i // 4
            sub = i % 4

            if i >= 2:
                S.wait("sync", "ve", ve_tile[i - 2])
                S.wait("sync", "se", se_tile[i - 2])

            xsem = S.use_sem(f"xin{slot}")

            def _ldx(sy, i=i, slot=slot, xsem=xsem):
                sy.dma_start(out=xin[:, slot], in_=x_v[i]).then_inc(S.sems[xsem], 16)
            S.emit("sync", _ldx)
            ld_x = S.inc(xsem, 16)

            S.wait("scalar", xsem, ld_x)

            def _st1(s, slot=slot):
                s.activation(out=cps[:], in_=xin[:, slot], func=AF.Identity,
                             accum_out=scol[:, slot, 0:1])
                s.activation(out=sqs[:], in_=xin[:, slot], func=AF.Square,
                             accum_out=scol[:, slot, 1:2])
            se_st1 = S.emit("scalar", _st1, n=2)

            S.wait("vector", xsem, ld_x)
            S.wait("vector", "se", se_st1)

            def _mv(v, slot=slot):
                v.tensor_scalar(out=scol[:, slot, 2:3], in0=scol[:, slot, 0:1],
                                scalar1=1.0 / C, scalar2=None, op0=ALU.mult)
                v.tensor_tensor(out=scol[:, slot, 3:4], in0=scol[:, slot, 2:3],
                                in1=scol[:, slot, 2:3], op=ALU.mult)
                v.scalar_tensor_tensor(out=scol[:, slot, 4:5], in0=scol[:, slot, 1:2],
                                       scalar=1.0 / C, in1=scol[:, slot, 3:4],
                                       op0=ALU.mult, op1=ALU.subtract)
                v.tensor_scalar(out=nprime[:], in0=xin[:, slot], scalar1=scol[:, slot, 2:3],
                                scalar2=None, op0=ALU.subtract)
            ve_np = S.emit("vector", _mv, n=4)

            S.wait("scalar", "ve", ve_np)

            def _rs(s, slot=slot):
                s.activation(out=scol[:, slot, 5:6], in_=scol[:, slot, 4:5], func=AF.Ln, bias=EPS)
                s.activation(out=scol[:, slot, 6:7], in_=scol[:, slot, 5:6], func=AF.Exp,
                             scale=-0.5)
            se_rs = S.emit("scalar", _rs, n=2)

            S.wait("vector", "se", se_rs)

            def _xn(v, slot=slot):
                v.tensor_scalar(out=xn_t[:], in0=nprime[:], scalar1=scol[:, slot, 6:7],
                                scalar2=None, op0=ALU.mult)
            ve_xn = S.emit("vector", _xn, n=1)

            S.wait("scalar", "ve", ve_xn)

            def _st2(s, slot=slot):
                s.activation(out=cps[:], in_=xn_t[:], func=AF.Identity,
                             accum_out=scol[:, slot, 7:8])
                s.activation(out=sqs[:], in_=xn_t[:], func=AF.Square,
                             accum_out=scol[:, slot, 8:9])
            se_st2 = S.emit("scalar", _st2, n=2)
            S.wait("vector", "se", se_st2)

            def _mv2(v, slot=slot):
                v.tensor_scalar(out=scol[:, slot, 9:10], in0=scol[:, slot, 7:8],
                                scalar1=1.0 / C, scalar2=None, op0=ALU.mult)
                v.tensor_tensor(out=scol[:, slot, 10:11], in0=scol[:, slot, 9:10],
                                in1=scol[:, slot, 9:10], op=ALU.mult)
                v.scalar_tensor_tensor(out=scol[:, slot, 11:12], in0=scol[:, slot, 8:9],
                                       scalar=1.0 / C, in1=scol[:, slot, 10:11],
                                       op0=ALU.mult, op1=ALU.subtract)
                v.tensor_scalar(out=n2[:], in0=xn_t[:], scalar1=scol[:, slot, 9:10],
                                scalar2=None, op0=ALU.subtract)
            ve_var2 = S.emit("vector", _mv2, n=4)
            S.wait("scalar", "ve", ve_var2)

            def _rs2(s, slot=slot):
                s.activation(out=scol[:, slot, 12:13], in_=scol[:, slot, 11:12], func=AF.Ln, bias=EPS)
                s.activation(out=scol[:, slot, 13:14], in_=scol[:, slot, 12:13], func=AF.Exp,
                             scale=-0.5)
            se_rs2 = S.emit("scalar", _rs2, n=2)
            S.wait("vector", "se", se_rs2)
            if i == 0:
                S.wait("vector", "cst", ld_ln0)

            def _r0(v, slot=slot):
                v.tensor_scalar(out=n2[:], in0=n2[:], scalar1=scol[:, slot, 13:14],
                                scalar2=None, op0=ALU.mult)
                v.tensor_tensor(out=r0_t[:], in0=n2[:], in1=w0b[:], op=ALU.mult)
                v.tensor_tensor(out=r0_t[:], in0=r0_t[:], in1=b0b[:], op=ALU.add)
            ve_r0 = S.emit("vector", _r0, n=3)

            def _bm(v, slot=slot):
                v.tensor_reduce(out=bmax[:], in_=xn_t[:].rearrange("p (g k) -> p g k", k=16),
                                axis=AX.X, op=ALU.max, apply_absolute_value=True)
                v.tensor_scalar(out=qsc[:], in0=bmax[:], scalar1=1.0 / 6.0,
                                scalar2=1e-12, op0=ALU.mult, op1=ALU.max)
            ve_bm = S.emit("vector", _bm, n=2)

            def _qi(v, slot=slot):
                v.reciprocal(out=qinv[:], in_=qsc[:])
            S.emit("vector", _qi, n=1)

            def _qy(v, slot=slot):
                v.tensor_tensor(out=qy[:].rearrange("p (g k) -> p g k", k=16),
                                in0=xn_t[:].rearrange("p (g k) -> p g k", k=16),
                                in1=qinv[:].unsqueeze(2).broadcast_to([128, C // 16, 16]),
                                op=ALU.mult)
            ve_qy = S.emit("vector", _qy, n=1)
            S.wait("scalar", "ve", ve_qy)

            def _qabs(s):
                s.activation(out=qa[:], in_=qy[:], func=AF.Abs)
            se_qabs = S.emit("scalar", _qabs, n=1)

            def _qgrid(v):
                v.tensor_scalar(out=qA[:], in0=qy[:], scalar1=2.0, scalar2=RNE,
                                op0=ALU.mult, op1=ALU.add)
                v.tensor_scalar(out=qA[:], in0=qA[:], scalar1=RNE, scalar2=0.5,
                                op0=ALU.subtract, op1=ALU.mult)
                v.tensor_scalar(out=qB[:], in0=qy[:], scalar1=RNE, scalar2=RNE,
                                op0=ALU.add, op1=ALU.subtract)
                v.tensor_scalar(out=qC[:], in0=qy[:], scalar1=0.5, scalar2=RNE,
                                op0=ALU.mult, op1=ALU.add)
                v.tensor_scalar(out=qC[:], in0=qC[:], scalar1=RNE, scalar2=2.0,
                                op0=ALU.subtract, op1=ALU.mult)
            S.emit("vector", _qgrid, n=5)
            S.wait("vector", "se", se_qabs)

            def _qsel(v, slot=slot):
                v.tensor_single_scalar(out=qm1[:], in_=qa[:], scalar=2.25, op=ALU.is_lt)
                v.tensor_single_scalar(out=qm2[:], in_=qa[:], scalar=4.5, op=ALU.is_lt)
                v.tensor_copy(out=qy[:], in_=qC[:])
                v.copy_predicated(out=qy[:], mask=qm2[:], data=qB[:])
                v.copy_predicated(out=qy[:], mask=qm1[:], data=qA[:])
                v.tensor_tensor(out=xq_t[:].rearrange("p (g k) -> p g k", k=16),
                                in0=qy[:].rearrange("p (g k) -> p g k", k=16),
                                in1=qsc[:].unsqueeze(2).broadcast_to([128, C // 16, 16]),
                                op=ALU.mult)
            ve_q = S.emit("vector", _qsel, n=6)

            if i > 0:
                S.wait("tensor", "ve", ve_tile[i - 1])
                S.wait("tensor", "se", se_tile[i - 1])
            S.wait("tensor", "ve", ve_r0)
            S.wait("tensor", "ve", ve_q)

            def _tr(t):
                for cj in range(CJ):
                    t.transpose(psum[cj // 4][:, (cj % 4) * 128:(cj % 4) * 128 + 128],
                                r0_t[:, cj * 128:cj * 128 + 128], ident32[:])
                last = None
                for cj in range(CJ):
                    last = t.transpose(psbf[:, cj * 128:cj * 128 + 128],
                                       xq_t[:, cj * 128:cj * 128 + 128], identbf[:])
                last.then_inc(S.sems["te"], 1)
            S.emit("tensor", _tr)
            te_tr = S.inc("te", 1)

            S.wait("vector", "te", te_tr)

            def _ev(v, slot2=span % 2, sub=sub):
                v.tensor_copy(out=rstage[:, slot2, 0:4, sub * 128:sub * 128 + 128],
                              in_=psum[0][:].rearrange("p (a b) -> p a b", b=128))
                v.tensor_copy(out=rstage[:, slot2, 4:8, sub * 128:sub * 128 + 128],
                              in_=psum[1][:].rearrange("p (a b) -> p a b", b=128))
            ve_ev = S.emit("vector", _ev, n=2)
            S.wait("scalar", "te", te_tr)

            def _evq(s, slot2=span % 2, sub=sub):
                s.copy(out=qstage[:, slot2, :, sub * 128:sub * 128 + 128],
                       in_=psbf[:].rearrange("p (a b) -> p a b", b=128))
            se_evq = S.emit("scalar", _evq, n=1)

            ve_tile.append(ve_ev)
            se_tile.append(se_evq)

            if sub == 3:
                ssem = S.use_sem(f"stg{span % 2}")
                S.wait("scalar", "ve", ve_ev)
                S.wait("scalar", "se", se_evq)

                def _stsp(s, span=span, slot2=span % 2, ssem=ssem):
                    s.dma_start(out=r_v[:, :, span * 512:span * 512 + 512],
                                in_=rstage[:, slot2]).then_inc(S.sems[ssem], 16)
                    s.dma_start(out=xq_v[:, :, span * 512:span * 512 + 512],
                                in_=qstage[:, slot2]).then_inc(S.sems[ssem], 16)
                S.emit("scalar", _stsp)
                st_span.append(S.inc(ssem, 32))
            if sub == 0 and span >= 2:
                ssem2 = f"stg{span % 2}"
                S.wait("vector", ssem2, st_span[span - 2])
                S.wait("scalar", ssem2, st_span[span - 2])

        S.barrier()

    if stop_after != "init":
        _build_layers(nc, S, T=T, L=L, stop_after=stop_after, psum=psum, psbf=psbf,
                      ident32=ident32, identbf=identbf, onescol_bf=onescol_bf,
                      onesrow_f=onesrow_f, alloc=alloc, arena=arena, pool=pool,
                      carve=carve,
                      hb=dict(r_v=r_v, xq_v=xq_v, xk2_v=xk2_v, xr2_v=xr2_v, kf_v=kf_vv,
                              src_v=src_vv, r_sb=r_sb_t, h_buf=h_buf_t,
                              rT_v=rT_v, hqT_v=hqT_v, x_v=x_v, out_v=out_v,
                              wkT=wkT_h, wvT=wvT_h, wrT=wrT_h, wcrT=wcrT_h, woT=woT_h,
                              wckT=wckT_h, wcvT=wcvT_h, vecs=vecs_h,
                              wg=wg_h, wu=wu_h, wd=wd_h))

    store_marks = {s: S.ctr[s] for s in S.sem_names
                   if s.startswith(("stg", "ost")) and S.ctr[s]}
    semstack = contextlib.ExitStack()
    for name in S.sem_names:
        S.sems[name] = semstack.enter_context(nc.semaphore(name))

    def _fin(sy, marks=store_marks):
        for s, tgt in marks.items():
            sy.wait_ge(S.sems[s], tgt)
    S.emit("sync", _fin)

    guards = {}
    with nc.Block() as block:
        for eng_name, deco in (("sync", block.sync), ("scalar", block.scalar),
                               ("vector", block.vector), ("tensor", block.tensor),
                               ("gpsimd", block.gpsimd)):
            if S.streams[eng_name]:
                def _run(eng, eng_name=eng_name, fns=S.streams[eng_name]):
                    g = Guard(eng, S.sems[ENG_SEM[eng_name]],
                              guard=eng_name in ("scalar", "vector", "gpsimd"))
                    guards[eng_name] = g
                    for fn in fns:
                        fn(g)
                deco(_run)
    for eng_name in ("scalar", "vector", "gpsimd"):
        if eng_name in guards:
            got = guards[eng_name]._n
            want = S.ctr[ENG_SEM[eng_name]]
            assert got == want, (eng_name, got, want)
    semstack.close()
    alloc.close()
    return nc


def _build_layers(nc, S, T, L, stop_after, psum, psbf, ident32, identbf,
                  onescol_bf, onesrow_f, alloc, arena, pool, carve, hb):
    NT = T // TC
    e = alloc.enter_context
    AR = 8192
    wk_v = arena[:, 0 * AR:1 * AR].rearrange("p (c o) -> p c o", o=1024)
    wv_v = arena[:, 1 * AR:2 * AR].rearrange("p (c o) -> p c o", o=1024)
    wr_v = arena[:, 2 * AR:3 * AR].rearrange("p (c o) -> p c o", o=1024)
    wo_v = arena[:, 3 * AR:4 * AR].rearrange("p (c o) -> p c o", o=1024)
    wcr_v = arena[:, 4 * AR:5 * AR].rearrange("p (c o) -> p c o", o=1024)
    wck_v = arena[:, 0:4 * AR].rearrange("p (c f) -> p c f", f=4096)
    wcv_v = arena[:, 0:4 * AR].rearrange("p (f c) -> p f c", c=1024)

    r_sb = hb["r_sb"]
    h_buf = hb["h_buf"]
    hb_sav = e(nc.sbuf_tensor("hb_sav", [128, CJ, 1], F32))
    hb2_sav = e(nc.sbuf_tensor("hb2_sav", [128, CJ, 1], F32))
    a_carry = e(nc.sbuf_tensor("a_carry", [128, CJ, 1], F32))
    b_carry = e(nc.sbuf_tensor("b_carry", [128, CJ, 1], F32))
    stripL = e(nc.sbuf_tensor("stripL", [1, 8, TC], F32))
    vec_sb = e(nc.sbuf_tensor("vec_sb", [128, 1, CJ, NV], F32))
    ew_t = carve(pool, 15366, [128, CJ, 1])
    eu_t = carve(pool, 15374, [128, CJ, 1])

    drb = carve(pool, 0, [128, CJ, TC])
    rb_bf = carve(arena, 4 * AR, [128, CJ, TC], BF16)
    rsq_bf = carve(arena, 4 * AR + CJ * TC, [128, CJ, TC], BF16)
    rwkv_bf = carve(pool, 0, [128, CJ, TC], BF16)
    xm_a = carve(pool, 4096, [128, CJ, TC], BF16)
    xm_b = carve(pool, 6144, [128, CJ, TC], BF16)
    sr_full = carve(pool, 8192, [128, CJ, TC], BF16)
    E_b = carve(pool, 10240, [128, 2, TC])
    srt = carve(pool, 10240, [128, TC])
    R_b = carve(pool, 11264, [128, 2, TC])
    A_b = carve(pool, 12288, [128, 2, TC + 1])
    B_b = carve(pool, 14340, [128, 2, TC + 1])
    xk2in = carve(pool, 0, [128, 2, CJ, TC], BF16)
    kfst = carve(pool, 4096, [128, 2, 4, TC], BF16)
    relu_scr = carve(pool, 6144, [128, 2, TC])
    wcrP = carve(pool, 7168, [128, CJ, 1024], BF16)
    xr2inA = carve(pool, 11264, [128, CJ, TC], BF16)
    srtA = carve(pool, 13312, [128, TC])
    srcA = carve(pool, 13824, [128, CJ, TC], BF16)
    kfin = carve(pool, 0, [128, FJ, TC], BF16)
    src_bf = carve(pool, 8192, [128, CJ, TC], BF16)
    kvscr = carve(pool, 10240, [128, TC])

    sems = S.sems
    r_marks = [None] * NT
    xk2_marks = [None] * NT
    xr2_marks = [None] * NT
    src_marks = [None] * NT
    kf_marks = [[None] * 8 for _ in range(NT)]
    region_free = [("te", 0)] * 5

    def dma_load(sem, fn, gate=(), eng="sync"):
        for gsem, gtgt in gate:
            if gtgt:
                S.wait(eng, gsem, gtgt)
        return S.dma(eng, sem, fn)

    for l in range(L):
        ls = l % 2

        cst_t = dma_load("cst", (lambda sy, l=l, ls=ls:
            sy.dma_start(out=vec_sb[:, 0],
                         in_=hb["vecs"][l].rearrange("(cj p) v -> p cj v", p=128)
                         ).then_inc(sems["cst"], 16)))
        S.wait("scalar", "cst", cst_t)

        def _ewu(s, ls=ls):
            s.activation(out=ew_t[:], in_=vec_sb[:, 0, :, V_TD:V_TD + 1], func=AF.Exp)
            s.activation(out=ew_t[:], in_=ew_t[:], func=AF.Exp, scale=-1.0)
            s.activation(out=eu_t[:], in_=vec_sb[:, 0, :, V_TF:V_TF + 1], func=AF.Exp)
        se_ewu = S.emit("scalar", _ewu, n=3)

        wqm = {}
        for name, view, src_h, reg in (("wk", wk_v, hb["wkT"], 0), ("wv", wv_v, hb["wvT"], 1),
                                       ("wr", wr_v, hb["wrT"], 2), ("wo", wo_v, hb["woT"], 3)):
            sem = S.use_sem("wq" + str(reg))
            wqm[name] = dma_load(sem, (lambda g, view=view, src_h=src_h, l=l, sem=sem:
                g.dma_start(out=view,
                            in_=src_h[l].rearrange("(cj p) o -> p cj o", p=128)
                            ).then_inc(sems[sem], 16)),
                gate=[region_free[reg]], eng="gpsimd")

        def _lrst(g):
            g.memset(a_carry[:], 0.0)
            g.memset(b_carry[:], 0.0)
            g.memset(hb_sav[:], 0.0)
            g.memset(hb2_sav[:], 0.0)
        ge_lrst = S.emit("gpsimd", _lrst, n=4)
        S.wait("vector", "ge", ge_lrst)

        H = [dict() for _ in range(NT + 2)]
        ndm = [0, 0]

        def emit_rld(qx):
            s2 = qx % 2
            rl_sem = S.use_sem(f"rld{s2}")
            gates = []
            hp = H[qx - 2] if qx > 1 else {}
            if hp:
                gates += [("ve", hp["ve_end"]), ("se", hp["se_end"]),
                          ("ge", hp["ge_end"]), hp["r_mark"]]
            if r_marks[qx] is not None:
                gates.append(r_marks[qx])
            H[qx]["rl"] = dma_load(rl_sem, (lambda sy, qx=qx, s2=s2, rl_sem=rl_sem:
                sy.dma_start(out=r_sb[:, s2], in_=hb["r_v"][:, :, qx * TC:qx * TC + TC]
                             ).then_inc(sems[rl_sem], 16)), gate=gates)

        def emit_casts(qx):
            s2 = qx % 2
            S.wait("scalar", f"rld{s2}", H[qx]["rl"])
            if qx >= 2:
                S.wait("scalar", "te", H[qx - 2]["te_stats2"])
            elif qx == 1:
                S.wait("scalar", "te", H[0]["te_stats"])
            def _rb(s, s2=s2):
                s.activation(out=rb_bf[:], in_=r_sb[:, s2], func=AF.Copy)
            S.emit("scalar", _rb, n=1)
            def _rsq(s, s2=s2):
                s.activation(out=rsq_bf[:], in_=r_sb[:, s2], func=AF.Square)
            H[qx]["se_rsq"] = S.emit("scalar", _rsq, n=1)

        def emit_stats1(qx):
            S.wait("tensor", "se", H[qx]["se_rsq"])
            def _stats(t):
                for cj in range(CJ):
                    t.matmul(psum[4][0:1, :], onescol_bf[:], rb_bf[:, cj],
                             start=(cj == 0), stop=(cj == CJ - 1))
                last_i = None
                for cj in range(CJ):
                    last_i = t.matmul(psum[5][0:1, :], onescol_bf[:], rsq_bf[:, cj],
                                      start=(cj == 0), stop=(cj == CJ - 1))
                last_i.then_inc(sems["te"], 1)
            S.emit("tensor", _stats)
            H[qx]["te_stats"] = S.inc("te", 1)

        def emit_strips1(qx):
            S.wait("scalar", "te", H[qx]["te_stats"])
            def _sevac(s):
                s.activation(out=stripL[0:1, 0, :], in_=psum[4][0:1, :],
                             func=AF.Copy, scale=-1.0 / C)
                s.activation(out=stripL[0:1, 1, :], in_=psum[4][0:1, :],
                             func=AF.Square, scale=1.0 / C)
                s.activation(out=stripL[0:1, 3, :], in_=psum[5][0:1, :],
                             func=AF.Copy, scale=1.0 / C)
            se_var = S.emit("scalar", _sevac, n=3)
            H[qx]["se_var1"] = se_var
            S.wait("gpsimd", "se", se_var)
            def _gvar(g):
                g.tensor_tensor(out=stripL[0:1, 2, :], in0=stripL[0:1, 3, :],
                                in1=stripL[0:1, 1, :], op=ALU.subtract)
            ge_var = S.emit("gpsimd", _gvar, n=1)
            S.wait("scalar", "ge", ge_var)
            def _rs1(s):
                s.activation(out=stripL[0:1, 3, :], in_=stripL[0:1, 2, :],
                             func=AF.Ln, bias=EPS)
                s.activation(out=stripL[0:1, 4, :], in_=stripL[0:1, 3, :],
                             func=AF.Exp, scale=-0.5)
            se_rs1 = S.emit("scalar", _rs1, n=2)
            S.wait("gpsimd", "se", se_rs1)
            def _gnm1(g):
                g.tensor_tensor(out=stripL[0:1, 5, :], in0=stripL[0:1, 0, :],
                                in1=stripL[0:1, 4, :], op=ALU.mult)
            H[qx]["ge_nm1"] = S.emit("gpsimd", _gnm1, n=1)

        def emit_bc1(qx, ve_h2_gate=0):
            S.wait("tensor", "ge", H[qx]["ge_nm1"])
            if ve_h2_gate:
                S.wait("tensor", "ve", ve_h2_gate)
            def _bc1(t):
                t.matmul(psum[4][:], onesrow_f[:], stripL[0:1, 5, :],
                         start=True, stop=True)
                t.matmul(psum[5][:], onesrow_f[:], stripL[0:1, 4, :],
                         start=True, stop=True).then_inc(sems["te"], 1)
            S.emit("tensor", _bc1)
            H[qx]["te_bc1"] = S.inc("te", 1)

        for q in range(NT):
            s_ = q % 2
            hq = H[q]
            h1 = H[q - 1] if q > 0 else {}

            if q == 0:
                emit_rld(0)
                emit_casts(0)
                emit_stats1(0)
                emit_strips1(0)
                emit_bc1(0)

            if h1:
                S.wait("gpsimd", "ve", h1["ve_mix2"])
            def _hshift(g):
                g.tensor_copy(out=h_buf[:, :, 0:1], in_=hb_sav[:])
            ge_hshift = S.emit("gpsimd", _hshift, n=1)
            S.wait("vector", "ge", ge_hshift)
            if q == 0:
                S.wait("vector", "cst", cst_t)
            S.wait("vector", "te", hq["te_bc1"])
            def _happ(v, ls=ls, s_=s_):
                v.tensor_tensor(out=h_buf[:, :, 1:TC + 1], in0=r_sb[:, s_],
                                in1=psum[4][:].unsqueeze(1).broadcast_to([128, CJ, TC]),
                                op=ALU.add)
                v.tensor_tensor(out=h_buf[:, :, 1:TC + 1], in0=h_buf[:, :, 1:TC + 1],
                                in1=psum[5][:].unsqueeze(1).broadcast_to([128, CJ, TC]),
                                op=ALU.mult)
                for cj in range(CJ):
                    v.tensor_scalar(out=h_buf[:, cj, 1:TC + 1], in0=h_buf[:, cj, 1:TC + 1],
                                    scalar1=vec_sb[:, 0, cj, V_LN1W:V_LN1W + 1],
                                    scalar2=vec_sb[:, 0, cj, V_LN1B:V_LN1B + 1],
                                    op0=ALU.mult, op1=ALU.add)
            ve_h = S.emit("vector", _happ, n=2 + CJ)

            if h1:
                S.wait("vector", "te", h1["te_o_last"])
            def _dd(v):
                v.tensor_tensor(out=drb[:], in0=h_buf[:, :, 1:TC + 1],
                                in1=h_buf[:, :, 0:TC], op=ALU.subtract)
            S.emit("vector", _dd, n=1)

            if h1:
                S.wait("vector", "te", h1["te_k_last"])
                S.wait("vector", h1["xk2_mark"][0], h1["xk2_mark"][1])
                S.wait("vector", h1["xr2_mark"][0], h1["xr2_mark"][1])
            def _mixr(v, ls=ls):
                for cj in range(CJ):
                    v.scalar_tensor_tensor(out=xm_a[:, cj], in0=drb[:, cj],
                                           scalar=vec_sb[:, 0, cj, V_TMR:V_TMR + 1],
                                           in1=h_buf[:, cj, 0:TC],
                                           op0=ALU.mult, op1=ALU.add)
            ve_xr = S.emit("vector", _mixr, n=CJ)

            if h1:
                S.wait("vector", "te", h1["te_v_last"])
            def _mixv(v):
                for cj in range(CJ):
                    v.scalar_tensor_tensor(out=xm_b[:, cj], in0=drb[:, cj],
                                           scalar=vec_sb[:, 0, cj, V_TMV:V_TMV + 1],
                                           in1=h_buf[:, cj, 0:TC],
                                           op0=ALU.mult, op1=ALU.add)
            S.emit("vector", _mixv, n=CJ)

            if q == 0:
                S.wait("tensor", "wq2", wqm["wr"])
                S.wait("scalar", "se", se_ewu)
            S.wait("tensor", "ve", ve_xr)
            te_r = {}
            se_sr = {}
            for oj in range(CJ):
                sl = oj % 2
                if oj < 2:
                    S.wait("tensor", "ve", ve_h)
                else:
                    S.wait("tensor", "se", se_sr[oj - 2])
                def _rmm(t, oj=oj, sl=sl):
                    last_i = None
                    for cj in range(CJ):
                        last_i = t.matmul(psum[4 + sl][:], wr_v[:, cj, oj * 128:oj * 128 + 128],
                                          xm_a[:, cj], start=(cj == 0), stop=(cj == CJ - 1))
                    last_i.then_inc(sems["te"], 1)
                S.emit("tensor", _rmm)
                te_r[oj] = S.inc("te", 1)
                S.wait("scalar", "te", te_r[oj])
                if oj == 0 and h1:
                    S.wait("scalar", "ve", h1["ve_out_last"])
                if oj == 0 and q == 0:
                    pass
                def _sr(s, oj=oj, sl=sl):
                    s.activation(out=sr_full[:, oj], in_=psum[4 + sl][:], func=AF.Sigmoid)
                se_sr[oj] = S.emit("scalar", _sr, n=1)
            hq["te_r_last"] = te_r[CJ - 1]

            S.wait("vector", "te", te_r[CJ - 1])
            def _mixk(v):
                for cj in range(CJ):
                    v.scalar_tensor_tensor(out=xm_a[:, cj], in0=drb[:, cj],
                                           scalar=vec_sb[:, 0, cj, V_TMK:V_TMK + 1],
                                           in1=h_buf[:, cj, 0:TC],
                                           op0=ALU.mult, op1=ALU.add)
            ve_kv = S.emit("vector", _mixk, n=CJ)

            if q + 1 < NT:
                emit_rld(q + 1)

            if q == 0:
                S.wait("tensor", "wq0", wqm["wk"])
                S.wait("tensor", "wq1", wqm["wv"])
            S.wait("tensor", "ve", ve_kv)
            se_E = {}
            ve_R = {}
            se_dinv = {}
            ve_out = {}
            te_k = {}
            te_v = {}
            for oj in range(CJ):
                sl = oj % 2
                if ndm[sl]:
                    S.wait("gpsimd", "ve", ndm[sl])
                def _cld(g, oj=oj, sl=sl):
                    g.tensor_copy(out=A_b[:, sl, 0:1], in_=a_carry[:, oj])
                    g.tensor_copy(out=B_b[:, sl, 0:1], in_=b_carry[:, oj])
                ge_cld = S.emit("gpsimd", _cld, n=2)
                if oj >= 2:
                    S.wait("tensor", "se", se_E[oj - 2])
                def _kmm(t, oj=oj, sl=sl):
                    last_i = None
                    for cj in range(CJ):
                        last_i = t.matmul(psum[sl][:], wk_v[:, cj, oj * 128:oj * 128 + 128],
                                          xm_a[:, cj], start=(cj == 0), stop=(cj == CJ - 1))
                    last_i.then_inc(sems["te"], 1)
                S.emit("tensor", _kmm)
                te_k[oj] = S.inc("te", 1)

                S.wait("scalar", "te", te_k[oj])
                if oj >= 2:
                    S.wait("scalar", "ve", ve_out[oj - 2])
                def _E(s, sl=sl):
                    s.activation(out=E_b[:, sl], in_=psum[sl][:], func=AF.Exp)
                se_E[oj] = S.emit("scalar", _E, n=1)

                if oj >= 2:
                    S.wait("tensor", "ve", ve_R[oj - 2])
                def _vmm(t, oj=oj, sl=sl):
                    last_i = None
                    for cj in range(CJ):
                        last_i = t.matmul(psum[2 + sl][:], wv_v[:, cj, oj * 128:oj * 128 + 128],
                                          xm_b[:, cj], start=(cj == 0), stop=(cj == CJ - 1))
                    last_i.then_inc(sems["te"], 1)
                S.emit("tensor", _vmm)
                te_v[oj] = S.inc("te", 1)

                S.wait("vector", "te", te_v[oj])
                S.wait("vector", "se", se_E[oj])
                if oj >= 2:
                    S.wait("vector", "se", se_dinv[oj - 2])
                def _R(v, sl=sl):
                    v.tensor_tensor(out=R_b[:, sl], in0=E_b[:, sl], in1=psum[2 + sl][:],
                                    op=ALU.mult)
                ve_R[oj] = S.emit("vector", _R, n=1)

                S.wait("vector", "ge", ge_cld)
                def _scan(v, oj=oj, sl=sl):
                    v.tensor_tensor_scan(out=A_b[:, sl, 1:TC + 1],
                                         data0=ew_t[:, oj].broadcast_to([128, TC]),
                                         data1=R_b[:, sl], initial=A_b[:, sl, 0:1],
                                         op0=ALU.mult, op1=ALU.add)
                    v.tensor_tensor_scan(out=B_b[:, sl, 1:TC + 1],
                                         data0=ew_t[:, oj].broadcast_to([128, TC]),
                                         data1=E_b[:, sl], initial=B_b[:, sl, 0:1],
                                         op0=ALU.mult, op1=ALU.add)
                ve_sc = S.emit("vector", _scan, n=2)
                S.wait("gpsimd", "ve", ve_sc)
                def _csav(g, oj=oj, sl=sl):
                    g.tensor_copy(out=a_carry[:, oj], in_=A_b[:, sl, TC:TC + 1])
                    g.tensor_copy(out=b_carry[:, oj], in_=B_b[:, sl, TC:TC + 1])
                S.emit("gpsimd", _csav, n=2)
                def _nd(v, oj=oj, sl=sl):
                    v.scalar_tensor_tensor(out=R_b[:, sl], in0=R_b[:, sl],
                                           scalar=eu_t[:, oj], in1=A_b[:, sl, 0:TC],
                                           op0=ALU.mult, op1=ALU.add)
                    v.scalar_tensor_tensor(out=E_b[:, sl], in0=E_b[:, sl],
                                           scalar=eu_t[:, oj], in1=B_b[:, sl, 0:TC],
                                           op0=ALU.mult, op1=ALU.add)
                ve_scan = S.emit("vector", _nd, n=2)
                ndm[sl] = ve_scan

                S.wait("scalar", "ve", ve_scan)
                def _dinv(s, sl=sl):
                    s.activation(out=E_b[:, sl], in_=E_b[:, sl], func=AF.Ln)
                    s.activation(out=E_b[:, sl], in_=E_b[:, sl], func=AF.Exp, scale=-1.0)
                se_dinv[oj] = S.emit("scalar", _dinv, n=2)

                def _emit_out(ojx):
                    slx = ojx % 2
                    S.wait("vector", "se", se_dinv[ojx])
                    if ojx < 2 and h1:
                        S.wait("vector", "te", h1["te_o_last"])
                    def _out(v, ojx=ojx, slx=slx):
                        v.tensor_tensor(out=R_b[:, slx], in0=R_b[:, slx], in1=E_b[:, slx],
                                        op=ALU.mult)
                        v.tensor_tensor(out=rwkv_bf[:, ojx], in0=R_b[:, slx],
                                        in1=sr_full[:, ojx], op=ALU.mult)
                    ve_out[ojx] = S.emit("vector", _out, n=2)
                if oj >= 1:
                    _emit_out(oj - 1)
            _emit_out(CJ - 1)
            hq["te_k_last"] = te_k[CJ - 1]
            hq["te_v_last"] = te_v[CJ - 1]
            hq["ve_out_last"] = ve_out[CJ - 1]

            if q + 1 < NT:
                emit_casts(q + 1)
                S.wait("tensor", "se", se_sr[CJ - 2])
                S.wait("tensor", "se", se_sr[CJ - 1])
                emit_stats1(q + 1)

            ve_radd = {}
            te_o = {}
            for cj in range(CJ):
                ob = 6 if cj % 2 == 0 else 0
                if cj == 0:
                    for oj in range(CJ):
                        S.wait("tensor", "ve", ve_out[oj])
                    if q == 0:
                        S.wait("tensor", "wq3", wqm["wo"])
                elif cj == 1:
                    S.wait("tensor", "se", se_E[CJ - 1])
                else:
                    S.wait("tensor", "ve", ve_radd[cj - 2])
                def _omm(t, cj=cj, ob=ob):
                    last_i = None
                    for oj in range(CJ):
                        last_i = t.matmul(psum[ob][:], wo_v[:, oj, cj * 128:cj * 128 + 128],
                                          rwkv_bf[:, oj], start=(oj == 0), stop=(oj == CJ - 1))
                    last_i.then_inc(sems["te"], 1)
                S.emit("tensor", _omm)
                te_o[cj] = S.inc("te", 1)
                S.wait("vector", "te", te_o[cj])
                def _radd(v, cj=cj, s_=s_, ob=ob):
                    v.tensor_tensor(out=r_sb[:, s_, cj], in0=r_sb[:, s_, cj], in1=psum[ob][:],
                                    op=ALU.add)
                ve_radd[cj] = S.emit("vector", _radd, n=1)
            hq["te_o_last"] = te_o[CJ - 1]

            if q + 1 < NT:
                emit_strips1(q + 1)

            S.wait("gpsimd", "ve", ve_kv)
            def _hsav(g):
                g.tensor_copy(out=hb_sav[:], in_=h_buf[:, :, TC:TC + 1])
                g.tensor_copy(out=h_buf[:, :, 0:1], in_=hb2_sav[:])
            ge_hsav = S.emit("gpsimd", _hsav, n=2)
            S.wait("vector", "ge", ge_hsav)

            last_stats = H[q + 1]["te_stats"] if q + 1 < NT else hq["te_stats"]
            S.wait("gpsimd", "ve", ve_radd[CJ - 1])
            S.wait("gpsimd", "te", last_stats)
            def _rb2(g, s_=s_):
                g.tensor_copy(out=rb_bf[:], in_=r_sb[:, s_])
            ge_rb2 = S.emit("gpsimd", _rb2, n=1)
            S.wait("scalar", "ve", ve_radd[CJ - 1])
            S.wait("scalar", "te", last_stats)
            def _rsq2(s, s_=s_):
                s.activation(out=rsq_bf[:], in_=r_sb[:, s_], func=AF.Square)
            se_rsq2 = S.emit("scalar", _rsq2, n=1)
            S.wait("tensor", "ge", ge_rb2)
            S.wait("tensor", "se", se_rsq2)
            if q + 1 < NT:
                S.wait("tensor", "se", H[q + 1]["se_var1"])
            def _stats2(t):
                for cj in range(CJ):
                    t.matmul(psum[4][0:1, :], onescol_bf[:], rb_bf[:, cj],
                             start=(cj == 0), stop=(cj == CJ - 1))
                last_i = None
                for cj in range(CJ):
                    last_i = t.matmul(psum[5][0:1, :], onescol_bf[:], rsq_bf[:, cj],
                                      start=(cj == 0), stop=(cj == CJ - 1))
                last_i.then_inc(sems["te"], 1)
            S.emit("tensor", _stats2)
            te_stats2 = S.inc("te", 1)
            hq["te_stats2"] = te_stats2
            S.wait("scalar", "te", te_stats2)
            def _sevac2(s):
                s.activation(out=stripL[0:1, 0, :], in_=psum[4][0:1, :],
                             func=AF.Copy, scale=-1.0 / C)
                s.activation(out=stripL[0:1, 1, :], in_=psum[4][0:1, :],
                             func=AF.Square, scale=1.0 / C)
                s.activation(out=stripL[0:1, 3, :], in_=psum[5][0:1, :],
                             func=AF.Copy, scale=1.0 / C)
            se_var2 = S.emit("scalar", _sevac2, n=3)
            S.wait("gpsimd", "se", se_var2)
            def _gvar2(g):
                g.tensor_tensor(out=stripL[0:1, 2, :], in0=stripL[0:1, 3, :],
                                in1=stripL[0:1, 1, :], op=ALU.subtract)
            ge_var2 = S.emit("gpsimd", _gvar2, n=1)
            S.wait("scalar", "ge", ge_var2)
            def _rs2(s):
                s.activation(out=stripL[0:1, 3, :], in_=stripL[0:1, 2, :],
                             func=AF.Ln, bias=EPS)
                s.activation(out=stripL[0:1, 6, :], in_=stripL[0:1, 3, :],
                             func=AF.Exp, scale=-0.5)
            se_rs2 = S.emit("scalar", _rs2, n=2)
            S.wait("gpsimd", "se", se_rs2)
            def _gnm2(g):
                g.tensor_tensor(out=stripL[0:1, 7, :], in0=stripL[0:1, 0, :],
                                in1=stripL[0:1, 6, :], op=ALU.mult)
            ge_nm2 = S.emit("gpsimd", _gnm2, n=1)
            S.wait("tensor", "ge", ge_nm2)
            def _bc2(t):
                t.matmul(psum[4][:], onesrow_f[:], stripL[0:1, 7, :],
                         start=True, stop=True)
                t.matmul(psum[5][:], onesrow_f[:], stripL[0:1, 6, :],
                         start=True, stop=True).then_inc(sems["te"], 1)
            S.emit("tensor", _bc2)
            te_bc2 = S.inc("te", 1)
            S.wait("vector", "te", te_bc2)
            def _happ2(v, ls=ls, s_=s_):
                v.tensor_tensor(out=h_buf[:, :, 1:TC + 1], in0=r_sb[:, s_],
                                in1=psum[4][:].unsqueeze(1).broadcast_to([128, CJ, TC]),
                                op=ALU.add)
                v.tensor_tensor(out=h_buf[:, :, 1:TC + 1], in0=h_buf[:, :, 1:TC + 1],
                                in1=psum[5][:].unsqueeze(1).broadcast_to([128, CJ, TC]),
                                op=ALU.mult)
                for cj in range(CJ):
                    v.tensor_scalar(out=h_buf[:, cj, 1:TC + 1], in0=h_buf[:, cj, 1:TC + 1],
                                    scalar1=vec_sb[:, 0, cj, V_LN2W:V_LN2W + 1],
                                    scalar2=vec_sb[:, 0, cj, V_LN2B:V_LN2B + 1],
                                    op0=ALU.mult, op1=ALU.add)
            hq["ve_h2"] = S.emit("vector", _happ2, n=2 + CJ)

            if q + 1 < NT:
                emit_bc1(q + 1, ve_h2_gate=hq["ve_h2"])

            S.wait("vector", "ge", ge_hsav)
            def _dd2(v):
                v.tensor_tensor(out=drb[:], in0=h_buf[:, :, 1:TC + 1],
                                in1=h_buf[:, :, 0:TC], op=ALU.subtract)
            S.emit("vector", _dd2, n=1)
            if h1:
                for m in (h1["xk2_mark"], h1["xr2_mark"]):
                    S.wait("vector", m[0], m[1])
            def _mix2(v, ls=ls):
                for cj in range(CJ):
                    v.scalar_tensor_tensor(out=xm_a[:, cj], in0=drb[:, cj],
                                           scalar=vec_sb[:, 0, cj, V_CMK:V_CMK + 1],
                                           in1=h_buf[:, cj, 0:TC],
                                           op0=ALU.mult, op1=ALU.add)
                for cj in range(CJ):
                    v.scalar_tensor_tensor(out=xm_b[:, cj], in0=drb[:, cj],
                                           scalar=vec_sb[:, 0, cj, V_CMR:V_CMR + 1],
                                           in1=h_buf[:, cj, 0:TC],
                                           op0=ALU.mult, op1=ALU.add)
            ve_mix2 = S.emit("vector", _mix2, n=2 * CJ)
            hq["ve_mix2"] = ve_mix2
            S.wait("gpsimd", "ve", hq["ve_h2"])
            def _hsav2(g):
                g.tensor_copy(out=hb2_sav[:], in_=h_buf[:, :, TC:TC + 1])
            ge_hs2 = S.emit("gpsimd", _hsav2, n=1)
            ve_hs2 = ve_mix2

            S.wait("scalar", "ve", ve_mix2)
            for nmm in (f"rst{s_}", "xk2s", "xr2s"):
                S.use_sem(nmm)
            def _st(s, q=q, s_=s_):
                s.dma_start(out=hb["r_v"][:, :, q * TC:q * TC + TC],
                            in_=r_sb[:, s_]).then_inc(sems[f"rst{s_}"], 16)
                s.dma_start(out=hb["xk2_v"][:, :, q * TC:q * TC + TC],
                            in_=xm_a[:]).then_inc(sems["xk2s"], 16)
                s.dma_start(out=hb["xr2_v"][:, :, q * TC:q * TC + TC],
                            in_=xm_b[:]).then_inc(sems["xr2s"], 16)
            S.emit("scalar", _st)
            r_marks[q] = (f"rst{s_}", S.inc(f"rst{s_}", 16))
            xk2_marks[q] = ("xk2s", S.inc("xk2s", 16))
            xr2_marks[q] = ("xr2s", S.inc("xr2s", 16))
            hq["r_mark"] = r_marks[q]
            hq["xk2_mark"] = xk2_marks[q]
            hq["xr2_mark"] = xr2_marks[q]
            hq["ve_end"] = ve_hs2
            hq["se_end"] = S.ctr["se"]
            hq["ge_end"] = S.ctr["ge"]

        S.barrier()
        if stop_after == f"tm{l}":
            return

        wck_t = []
        for piece in range(4):
            sem = S.use_sem("wq" + str(piece))
            wck_t.append(dma_load(sem, (lambda g, piece=piece, l=l, sem=sem:
                g.dma_start(out=wck_v[:, piece * 2:piece * 2 + 2, :],
                            in_=hb["wckT"][l].rearrange("(cj p) f -> p cj f", p=128)[:, piece * 2:piece * 2 + 2, :]
                            ).then_inc(sems[sem], 16)), eng="gpsimd"))
        semcr = S.use_sem("wq4")
        wq_cr = dma_load(semcr, (lambda g, l=l:
            g.dma_start(out=wcrP[:],
                        in_=hb["wcrT"][l].rearrange("(cj p) o -> p cj o", p=128)
                        ).then_inc(sems["wq4"], 16)), eng="gpsimd")

        prev_kfst = [("te", 0)] * 2
        prev_xk2_mm = [0, 0]
        prev_sekf_last = 0
        prev_src_mm = 0
        prev_src_se = 0
        prev_xr2_mm = 0
        for q in range(NT):
            sl = q % 2
            x_sem = S.use_sem(f"xk2l{sl}")
            gates = [xk2_marks[q]]
            if prev_xk2_mm[sl]:
                gates.append(("te", prev_xk2_mm[sl]))
            xl_t = dma_load(x_sem, (lambda sy, q=q, sl=sl, x_sem=x_sem:
                sy.dma_start(out=xk2in[:, sl], in_=hb["xk2_v"][:, :, q * TC:q * TC + TC]
                             ).then_inc(sems[x_sem], 16)), gate=gates)
            xr_sem = S.use_sem("xr2lA")
            gates = [xr2_marks[q]]
            if prev_xr2_mm:
                gates.append(("te", prev_xr2_mm))
            xr_t = dma_load(xr_sem, (lambda sy, q=q:
                sy.dma_start(out=xr2inA[:], in_=hb["xr2_v"][:, :, q * TC:q * TC + TC]
                             ).then_inc(sems["xr2lA"], 16)), gate=gates)

            if q == 0:
                S.wait("tensor", "wq4", wq_cr)
            if q > 0 and src_marks[q - 1]:
                S.wait("scalar", src_marks[q - 1][0], src_marks[q - 1][1])
            S.wait("tensor", "xr2lA", xr_t)
            se_src = {}
            te_src = {}
            for oj in range(CJ):
                bsl = oj % 2
                if oj >= 2:
                    S.wait("tensor", "se", se_src[oj - 2])
                elif prev_src_se:
                    S.wait("tensor", "se", prev_src_se)
                def _srmm(t, oj=oj, bsl=bsl):
                    last_i = None
                    for cj in range(CJ):
                        last_i = t.matmul(psum[4 + bsl][:], wcrP[:, cj, oj * 128:oj * 128 + 128],
                                          xr2inA[:, cj], start=(cj == 0), stop=(cj == CJ - 1))
                    last_i.then_inc(sems["te"], 1)
                S.emit("tensor", _srmm)
                te_src[oj] = S.inc("te", 1)
                S.wait("scalar", "te", te_src[oj])
                def _srs(s, oj=oj, bsl=bsl):
                    s.activation(out=srcA[:, oj], in_=psum[4 + bsl][:], func=AF.Sigmoid)
                se_src[oj] = S.emit("scalar", _srs, n=1)
            prev_xr2_mm = te_src[CJ - 1]
            prev_src_se = se_src[CJ - 1]
            S.use_sem("srcs")
            S.wait("scalar", "se", se_src[CJ - 1])
            def _stsrc(s, q=q):
                s.dma_start(out=hb["src_v"][:, :, q * TC:q * TC + TC],
                            in_=srcA[:]).then_inc(sems["srcs"], 16)
            S.emit("scalar", _stsrc)
            src_marks[q] = ("srcs", S.inc("srcs", 16))

            te_kf = {}
            se_kf = {}
            for fj in range(FJ):
                bsl = fj % 2
                grp = fj // 4
                gsl = grp % 2
                if fj == 0:
                    S.wait("tensor", x_sem, xl_t)
                    if q == 0:
                        for piece in range(4):
                            S.wait("tensor", "wq" + str(piece), wck_t[piece])
                    if prev_sekf_last:
                        S.wait("tensor", "se", prev_sekf_last)
                if fj >= 2:
                    S.wait("tensor", "se", se_kf[fj - 2])
                def _kfmm(t, fj=fj, bsl=bsl, sl=sl):
                    last_i = None
                    for cj in range(CJ):
                        last_i = t.matmul(psum[bsl][:], wck_v[:, cj, fj * 128:fj * 128 + 128],
                                          xk2in[:, sl, cj], start=(cj == 0), stop=(cj == CJ - 1))
                    last_i.then_inc(sems["te"], 1)
                S.emit("tensor", _kfmm)
                te_kf[fj] = S.inc("te", 1)
                S.wait("scalar", "te", te_kf[fj])
                if fj % 4 == 0 and prev_kfst[gsl][1]:
                    S.wait("scalar", prev_kfst[gsl][0], prev_kfst[gsl][1])
                def _kfact(s, fj=fj, bsl=bsl, gsl=gsl):
                    s.activation(out=relu_scr[:, bsl], in_=psum[bsl][:], func=AF.Relu)
                    s.activation(out=kfst[:, gsl, fj % 4], in_=relu_scr[:, bsl], func=AF.Square)
                se_kf[fj] = S.emit("scalar", _kfact, n=2)
                if fj % 4 == 3:
                    kf_sem = S.use_sem(f"kfs{gsl}")
                    S.wait("scalar", "se", se_kf[fj])
                    def _kfstore(s, q=q, grp=grp, gsl=gsl, kf_sem=kf_sem):
                        s.dma_start(out=hb["kf_v"][:, grp * 4:grp * 4 + 4, q * TC:q * TC + TC],
                                    in_=kfst[:, gsl]).then_inc(sems[kf_sem], 16)
                    S.emit("scalar", _kfstore)
                    kf_marks[q][grp] = (kf_sem, S.inc(kf_sem, 16))
                    prev_kfst[gsl] = kf_marks[q][grp]
            prev_sekf_last = se_kf[FJ - 1]
            prev_xk2_mm[sl] = te_kf[FJ - 1]

        S.barrier()
        if stop_after == f"cma{l}":
            return

        for piece in range(4):
            sem = S.use_sem("wq" + str(piece))
            dma_load(sem, (lambda g, piece=piece, l=l, sem=sem:
                g.dma_start(out=wcv_v[:, piece * 8:piece * 8 + 8, :],
                            in_=hb["wcvT"][l].rearrange("(fj p) c -> p fj c", p=128)[:, piece * 8:piece * 8 + 8, :]
                            ).then_inc(sems[sem], 16)), eng="gpsimd")
        wcv_t = {piece: S.ctr["wq" + str(piece)] for piece in range(4)}

        last = (l == L - 1)
        prev_ve_done = 0
        prev_se_done = 0
        prev_rstB = None
        prev_rT = None
        prev_kv_last = 0
        prev_src_mm = 0
        for q in range(NT):
            sl = q % 2
            kf_l = []
            for piece in range(4):
                k_sem = S.use_sem("kfl")
                gates = [kf_marks[q][piece * 2], kf_marks[q][piece * 2 + 1]]
                if prev_kv_last:
                    gates.append(("te", prev_kv_last))
                kf_l.append(dma_load(k_sem, (lambda sy, q=q, piece=piece:
                    sy.dma_start(out=kfin[:, piece * 8:piece * 8 + 8, :],
                                 in_=hb["kf_v"][:, piece * 8:piece * 8 + 8, q * TC:q * TC + TC]
                                 ).then_inc(sems["kfl"], 16)), gate=gates))
            rl_sem = S.use_sem(f"rld{sl}")
            gates = [r_marks[q]]
            if prev_ve_done:
                gates.append(("ve", prev_ve_done))
            if prev_rstB:
                gates.append(prev_rstB)
            if prev_rT:
                gates.append(prev_rT)
            rl_t = dma_load(rl_sem, (lambda sy, q=q, sl=sl, rl_sem=rl_sem:
                sy.dma_start(out=r_sb[:, sl], in_=hb["r_v"][:, :, q * TC:q * TC + TC]
                             ).then_inc(sems[rl_sem], 16)), gate=gates)
            sc_sem = S.use_sem("srcl")
            gates = [src_marks[q]]
            if prev_src_mm:
                gates.append(("ve", prev_src_mm))
            sc_t = dma_load(sc_sem, (lambda sy, q=q:
                sy.dma_start(out=src_bf[:], in_=hb["src_v"][:, :, q * TC:q * TC + TC]
                             ).then_inc(sems["srcl"], 16)), gate=gates)

            ve_radd = {}
            te_kv = {}
            for cj in range(CJ):
                bsl = cj % 2
                if cj == 0:
                    S.wait("tensor", "kfl", kf_l[-1])
                    if q == 0:
                        for piece in range(4):
                            S.wait("tensor", "wq" + str(piece), wcv_t[piece])
                if cj >= 2:
                    S.wait("tensor", "ve", ve_radd[cj - 2])
                elif q > 0:
                    S.wait("tensor", "ve", prev_radd[cj + CJ - 2])
                def _kvmm(t, cj=cj, bsl=bsl):
                    last_i = None
                    for fj in range(FJ):
                        last_i = t.matmul(psum[bsl][:], wcv_v[:, fj, cj * 128:cj * 128 + 128],
                                          kfin[:, fj], start=(fj == 0), stop=(fj == FJ - 1))
                    last_i.then_inc(sems["te"], 1)
                S.emit("tensor", _kvmm)
                te_kv[cj] = S.inc("te", 1)
                S.wait("vector", "te", te_kv[cj])
                if cj == 0:
                    S.wait("vector", f"rld{sl}", rl_t)
                    S.wait("vector", "srcl", sc_t)
                def _kvadd(v, cj=cj, bsl=bsl, sl=sl):
                    v.tensor_tensor(out=kvscr[:], in0=src_bf[:, cj], in1=psum[bsl][:],
                                    op=ALU.mult)
                    v.tensor_tensor(out=r_sb[:, sl, cj], in0=r_sb[:, sl, cj], in1=kvscr[:],
                                    op=ALU.add)
                ve_radd[cj] = S.emit("vector", _kvadd, n=2)
            prev_kv_last = te_kv[CJ - 1]
            prev_ve_done = ve_radd[CJ - 1]
            prev_src_mm = ve_radd[CJ - 1]
            prev_radd = ve_radd

            if not last:
                S.wait("scalar", "ve", ve_radd[CJ - 1])
                S.use_sem(f"rst{sl}")
                def _stR(s, q=q, sl=sl):
                    s.dma_start(out=hb["r_v"][:, :, q * TC:q * TC + TC],
                                in_=r_sb[:, sl]).then_inc(sems[f"rst{sl}"], 16)
                S.emit("scalar", _stR)
                r_marks[q] = (f"rst{sl}", S.inc(f"rst{sl}", 16))
                prev_rstB = r_marks[q]
            else:
                rT_stage = carve(pool, 11776, [128, 4, 1024])
                te_tr = {}
                ve_evr = {}
                for cj in range(CJ):
                    S.wait("tensor", "ve", ve_radd[cj])
                    if cj > 0:
                        S.wait("tensor", "ve", ve_evr[cj - 1])
                    if prev_rT and cj == 0:
                        S.wait("tensor", prev_rT[0], prev_rT[1])
                    def _trr(t, cj=cj, sl=sl):
                        last_i = None
                        for tt in range(4):
                            last_i = t.transpose(psum[6][:, tt * 128:tt * 128 + 128],
                                                 r_sb[:, sl, cj, tt * 128:tt * 128 + 128],
                                                 ident32[:])
                        last_i.then_inc(sems["te"], 1)
                    S.emit("tensor", _trr)
                    te_tr[cj] = S.inc("te", 1)
                    S.wait("vector", "te", te_tr[cj])
                    def _evr(v, cj=cj):
                        v.tensor_copy(out=rT_stage[:, :, cj * 128:cj * 128 + 128],
                                      in_=psum[6][:].rearrange("p (a b) -> p a b", b=128))
                    ve_evr[cj] = S.emit("vector", _evr, n=1)
                S.wait("scalar", "ve", ve_evr[CJ - 1])
                rT_sem = S.use_sem("rTs")
                def _stT(s, q=q):
                    s.dma_start(out=hb["rT_v"][q * 4:q * 4 + 4].rearrange("a p c -> p a c"),
                                in_=rT_stage[:]).then_inc(sems["rTs"], 16)
                S.emit("scalar", _stT)
                prev_rT = ("rTs", S.inc("rTs", 16))

        S.barrier()
        if stop_after == f"cm{l}":
            return
        region_free[0] = ("te", prev_kv_last)
        region_free[1] = ("te", prev_kv_last)
        region_free[2] = ("te", prev_kv_last)
        region_free[3] = ("te", prev_kv_last)
        region_free[4] = ("te", 0)

    if stop_after == "layers":
        return
    _emit_ffn(nc, S, hb, psum, psbf, ident32, identbf, T, arena, pool,
              r_sb, h_buf, carve)


def emit_quant(S, q, src, dst, width, pregate=()):
    G = width // 16
    for eng, sem, tgt in pregate:
        if tgt:
            S.wait(eng, sem, tgt)

    def _v1(v, src=src):
        v.tensor_reduce(out=q["bm"][:, 0:G], in_=src.rearrange("p (g k) -> p g k", k=16),
                        axis=AX.X, op=ALU.max, apply_absolute_value=True)
        v.tensor_scalar(out=q["sc"][:, 0:G], in0=q["bm"][:, 0:G], scalar1=1.0 / 6.0,
                        scalar2=1e-12, op0=ALU.mult, op1=ALU.max)
        v.reciprocal(out=q["iv"][:, 0:G], in_=q["sc"][:, 0:G])
        v.tensor_tensor(out=q["y"][:, 0:width].rearrange("p (g k) -> p g k", k=16),
                        in0=src.rearrange("p (g k) -> p g k", k=16),
                        in1=q["iv"][:, 0:G].unsqueeze(2).broadcast_to([128, G, 16]),
                        op=ALU.mult)
    ve_y = S.emit("vector", _v1, n=4)
    S.wait("scalar", "ve", ve_y)

    def _s1(s):
        s.activation(out=q["a"][:, 0:width], in_=q["y"][:, 0:width], func=AF.Abs)
    se_a = S.emit("scalar", _s1, n=1)

    def _v2(v):
        v.tensor_scalar(out=q["qA"][:, 0:width], in0=q["y"][:, 0:width], scalar1=2.0,
                        scalar2=RNE, op0=ALU.mult, op1=ALU.add)
        v.tensor_scalar(out=q["qA"][:, 0:width], in0=q["qA"][:, 0:width], scalar1=RNE,
                        scalar2=0.5, op0=ALU.subtract, op1=ALU.mult)
        v.tensor_scalar(out=q["qB"][:, 0:width], in0=q["y"][:, 0:width], scalar1=RNE,
                        scalar2=RNE, op0=ALU.add, op1=ALU.subtract)
        v.tensor_scalar(out=q["qC"][:, 0:width], in0=q["y"][:, 0:width], scalar1=0.5,
                        scalar2=RNE, op0=ALU.mult, op1=ALU.add)
        v.tensor_scalar(out=q["qC"][:, 0:width], in0=q["qC"][:, 0:width], scalar1=RNE,
                        scalar2=2.0, op0=ALU.subtract, op1=ALU.mult)
    S.emit("vector", _v2, n=5)
    S.wait("vector", "se", se_a)

    def _v3(v, dst=dst):
        v.tensor_single_scalar(out=q["m1"][:, 0:width], in_=q["a"][:, 0:width],
                               scalar=2.25, op=ALU.is_lt)
        v.tensor_single_scalar(out=q["m2"][:, 0:width], in_=q["a"][:, 0:width],
                               scalar=4.5, op=ALU.is_lt)
        v.tensor_copy(out=q["y"][:, 0:width], in_=q["qC"][:, 0:width])
        v.copy_predicated(out=q["y"][:, 0:width], mask=q["m2"][:, 0:width],
                          data=q["qB"][:, 0:width])
        v.copy_predicated(out=q["y"][:, 0:width], mask=q["m1"][:, 0:width],
                          data=q["qA"][:, 0:width])
        v.tensor_tensor(out=dst.rearrange("p (g k) -> p g k", k=16),
                        in0=q["y"][:, 0:width].rearrange("p (g k) -> p g k", k=16),
                        in1=q["sc"][:, 0:G].unsqueeze(2).broadcast_to([128, G, 16]),
                        op=ALU.mult)
    return S.emit("vector", _v3, n=6)


def _emit_ffn(nc, S, hb, psum, psbf, ident32, identbf, T, arena, pool,
              r_sb, h_buf, carve):
    TT = T // 128
    sems = S.sems
    NJ = [(n0, min(512, II - n0)) for n0 in range(0, II, 512)]
    QW = 1408

    rflat = r_sb[:].rearrange("p a b c -> p (a b c)")
    hflat = h_buf[:].rearrange("p a b -> p (a b)")

    wgate_q = carve(arena, 0, [128, CJ, II], BF16)
    wup_main = carve(arena, CJ * II, [128, 6, II], BF16)
    wup_tail = carve(pool, 0, [128, 2, II], BF16)

    def wup_rhs(cj, n0, nw):
        if cj < 6:
            return wup_main[:, cj, n0:n0 + nw]
        return wup_tail[:, cj - 6, n0:n0 + nw]

    q = {"bm": carve(pool, 2816, [128, QW // 16]),
         "sc": carve(pool, 2904, [128, QW // 16]),
         "iv": carve(pool, 2992, [128, QW // 16]),
         "y": carve(pool, 3080, [128, QW]),
         "a": carve(pool, 4488, [128, QW]),
         "qA": carve(pool, 5896, [128, QW]),
         "qB": carve(pool, 7304, [128, QW]),
         "qC": carve(pool, 8712, [128, QW]),
         "m1": carve(pool, 10120, [128, QW], U8),
         "m2": carve(pool, 10472, [128, QW], U8)}
    gate_s = carve(pool, 10824, [128, 2, 512])
    h_f = carve(rflat, 4096, [128, II])
    hq_bf = carve(pool, 11848, [128, 2, II], BF16)
    wtile = carve(pool, 10824, [128, 2, 1024])
    wq_bf = carve(pool, 12872, [128, 2, 1024], BF16)
    xqT_in = carve(rflat, 0, [128, 2, CJ, 512], BF16)
    hqTst = carve(hflat, 0, [128, 2, IJ, 128], BF16)
    wtileD = carve(pool, 10824, [128, II])
    wq_bfD = carve(rflat, 0, [128, 2, II], BF16)
    hqT_in = carve(rflat, 2816, [128, 2, IJ, 128], BF16)
    rx_in = carve(hflat, 0, [128, 2, 2, 1024])
    out_t = carve(pool, 0, [128, 2, 1024])

    ve_wq = {}
    te_wtr = {}
    se_wev = {}
    for gi in range(2 * IJ):
        wi, tile = divmod(gi, IJ)
        sl = gi % 2
        wl_sem = S.use_sem(f"fwl{sl}")
        prev = ve_wq.get(gi - 2)
        if prev:
            S.wait("sync", "ve", prev)
        w_h = hb["wg"] if wi == 0 else hb["wu"]
        wl_t = S.dma("sync", wl_sem, (lambda sy, w_h=w_h, tile=tile, sl=sl, wl_sem=wl_sem:
            sy.dma_start(out=wtile[:, sl],
                         in_=w_h[tile * 128:tile * 128 + 128, :]
                         ).then_inc(sems[wl_sem], 16)))
        gates = [("vector", wl_sem, wl_t)]
        prev_tr = te_wtr.get(gi - 2)
        if prev_tr:
            gates.append(("vector", "te", prev_tr))
        ve_wq[gi] = emit_quant(S, q, wtile[:, sl], wq_bf[:, sl], 1024,
                               pregate=gates)
        S.wait("tensor", "ve", ve_wq[gi])
        if (gi - 1) in se_wev:
            S.wait("tensor", "se", se_wev[gi - 1])
        def _wtr(t, sl=sl):
            last_i = None
            for cj in range(CJ):
                last_i = t.transpose(psbf[:, cj * 128:cj * 128 + 128],
                                     wq_bf[:, sl, cj * 128:cj * 128 + 128], identbf[:])
            last_i.then_inc(sems["te"], 1)
        S.emit("tensor", _wtr)
        te_wtr[gi] = S.inc("te", 1)
        S.wait("scalar", "te", te_wtr[gi])

        if wi == 0:
            def _wev0(s, tile=tile):
                s.copy(out=wgate_q[:, :, tile * 128:tile * 128 + 128],
                       in_=psbf[:].rearrange("p (a b) -> p a b", b=128))
            se_wev[gi] = S.emit("scalar", _wev0, n=1)
        else:
            def _wev1(s, tile=tile):
                s.copy(out=wup_main[:, 0:6, tile * 128:tile * 128 + 128],
                       in_=psbf[:, 0:768].rearrange("p (a b) -> p a b", b=128))
                s.copy(out=wup_tail[:, :, tile * 128:tile * 128 + 128],
                       in_=psbf[:, 768:1024].rearrange("p (a b) -> p a b", b=128))
            se_wev[gi] = S.emit("scalar", _wev1, n=2)

    S.barrier()

    hqs_marks = [None] * TT
    prev_hqst = [None, None]
    prev_silu_last = 0
    prev_h_last = 0
    prev_hev_last = [0]
    hq_ve = [0, 0]
    hq_tr_done = [0, 0]
    xq_t = {}
    xq_mm_last = [0, 0]
    for i in range(TT):
        g4 = i // 4
        xsl = g4 % 2
        tsub = i % 4
        if tsub == 0:
            xq_sem = S.use_sem(f"fxq{xsl}")
            if g4 >= 2:
                S.wait("sync", "te", xq_mm_last[xsl])
            xq_t[g4] = S.dma("sync", xq_sem, (lambda sy, g4=g4, xsl=xsl, xq_sem=xq_sem:
                sy.dma_start(out=xqT_in[:, xsl],
                             in_=hb["xq_v"][:, :, g4 * 512:g4 * 512 + 512]
                             ).then_inc(sems[xq_sem], 16)))

        te_g = {}
        se_silu = {}
        ve_hpart = {}
        for nj, (n0, nw) in enumerate(NJ):
            bsl = nj % 2
            if nj == 0 and tsub == 0:
                S.wait("tensor", f"fxq{xsl}", xq_t[g4])
            if nj >= 2:
                S.wait("tensor", "se", se_silu[nj - 2])
                S.wait("tensor", "ve", ve_hpart[nj - 2])
            elif i > 0:
                S.wait("tensor", "se", prev_silu_last)
                S.wait("tensor", "ve", prev_h_last)
            def _gmm(t, n0=n0, nw=nw, bsl=bsl, xsl=xsl, tsub=tsub):
                last_i = None
                for cj in range(CJ):
                    last_i = t.matmul(psum[bsl][:, 0:nw],
                                      xqT_in[:, xsl, cj, tsub * 128:tsub * 128 + 128],
                                      wgate_q[:, cj, n0:n0 + nw],
                                      start=(cj == 0), stop=(cj == CJ - 1))
                for cj in range(CJ):
                    last_i = t.matmul(psum[2 + bsl][:, 0:nw],
                                      xqT_in[:, xsl, cj, tsub * 128:tsub * 128 + 128],
                                      wup_rhs(cj, n0, nw),
                                      start=(cj == 0), stop=(cj == CJ - 1))
                last_i.then_inc(sems["te"], 1)
            S.emit("tensor", _gmm)
            te_g[nj] = S.inc("te", 1)
            S.wait("scalar", "te", te_g[nj])
            if nj >= 2:
                S.wait("scalar", "ve", ve_hpart[nj - 2])
            elif i > 0:
                S.wait("scalar", "ve", prev_h_last)
            def _silu(s, n0=n0, nw=nw, bsl=bsl):
                s.activation(out=gate_s[:, bsl, 0:nw], in_=psum[bsl][:, 0:nw], func=AF.Silu)
            se_silu[nj] = S.emit("scalar", _silu, n=1)
            S.wait("vector", "se", se_silu[nj])
            S.wait("vector", "te", te_g[nj])
            def _hp(v, n0=n0, nw=nw, bsl=bsl):
                v.tensor_tensor(out=h_f[:, n0:n0 + nw], in0=gate_s[:, bsl, 0:nw],
                                in1=psum[2 + bsl][:, 0:nw], op=ALU.mult)
            ve_hpart[nj] = S.emit("vector", _hp, n=1)
        prev_silu_last = se_silu[len(NJ) - 1]
        prev_h_last = ve_hpart[len(NJ) - 1]

        hsl = i % 2
        gates = [("vector", "ve", ve_hpart[len(NJ) - 1])]
        if i >= 2 and hq_tr_done[hsl]:
            gates.append(("vector", "te", hq_tr_done[hsl]))
        ve_q1 = emit_quant(S, q, h_f[:, 0:QW], hq_bf[:, hsl, 0:QW], QW,
                           pregate=gates)
        ve_q2 = emit_quant(S, q, h_f[:, QW:II], hq_bf[:, hsl, QW:II], II - QW)
        hq_ve[hsl] = ve_q2

        def _emit_tr(ix):
            hslx = ix % 2
            se_hev = {}
            if prev_hqst[hslx]:
                S.wait("scalar", prev_hqst[hslx][0], prev_hqst[hslx][1])
            te_htr = None
            for rnd in range(3):
                lo = rnd * 8
                hi = min(lo + 8, IJ)
                S.wait("tensor", "ve", hq_ve[hslx])
                if rnd > 0:
                    S.wait("tensor", "se", se_hev[rnd - 1])
                elif prev_hev_last[0]:
                    S.wait("tensor", "se", prev_hev_last[0])
                def _htr(t, lo=lo, hi=hi, hslx=hslx):
                    last_i = None
                    for ij in range(lo, hi):
                        last_i = t.transpose(psbf[:, (ij - lo) * 128:(ij - lo) * 128 + 128],
                                             hq_bf[:, hslx, ij * 128:ij * 128 + 128],
                                             identbf[:])
                    last_i.then_inc(sems["te"], 1)
                S.emit("tensor", _htr)
                te_htr = S.inc("te", 1)
                S.wait("scalar", "te", te_htr)
                def _hev(s, lo=lo, hi=hi, hslx=hslx):
                    s.copy(out=hqTst[:, hslx, lo:hi, :],
                           in_=psbf[:, 0:(hi - lo) * 128].rearrange("p (a b) -> p a b", b=128))
                se_hev[rnd] = S.emit("scalar", _hev, n=1)
            prev_hev_last[0] = se_hev[2]
            hq_tr_done[hslx] = te_htr
            hq_sem = S.use_sem(f"fhq{hslx}")
            S.wait("scalar", "se", se_hev[2])
            def _hst(s, ix=ix, hslx=hslx, hq_sem=hq_sem):
                s.dma_start(out=hb["hqT_v"][:, :, ix * 128:ix * 128 + 128],
                            in_=hqTst[:, hslx]).then_inc(sems[hq_sem], 16)
            S.emit("scalar", _hst)
            hqs_marks[ix] = (hq_sem, S.inc(hq_sem, 16))
            prev_hqst[hslx] = hqs_marks[ix]

        if i >= 1:
            _emit_tr(i - 1)
        if i == TT - 1:
            _emit_tr(i)
        if tsub == 3:
            xq_mm_last[xsl] = te_g[len(NJ) - 1]

    S.barrier()

    wdown_full = carve(arena, 0, [128, IJ, 1024], BF16)

    def wdown_q_view(tile, lo, hi):
        return wdown_full[:, lo:hi, tile * 128:tile * 128 + 128]

    ve_wqd = {}
    te_wtrd = {}
    se_wevd = {}
    for tile in range(CJ):
        sl = tile % 2
        wl_sem = S.use_sem("fwl0")
        prev = ve_wqd.get(tile - 1)
        if prev:
            S.wait("sync", "ve", prev)
        wl_t = S.dma("sync", wl_sem, (lambda sy, tile=tile, wl_sem=wl_sem:
            sy.dma_start(out=wtileD[:],
                         in_=hb["wd"][tile * 128:tile * 128 + 128, :]
                         ).then_inc(sems[wl_sem], 16)))
        gates = [("vector", wl_sem, wl_t)]
        prev_tr = te_wtrd.get(tile - 2)
        if prev_tr:
            gates.append(("vector", "te", prev_tr))
        emit_quant(S, q, wtileD[:, 0:QW], wq_bfD[:, sl, 0:QW], QW, pregate=gates)
        ve_wqd[tile] = emit_quant(S, q, wtileD[:, QW:II], wq_bfD[:, sl, QW:II],
                                  II - QW)
        S.wait("tensor", "ve", ve_wqd[tile])
        last_te = None
        for rnd in range(3):
            lo = rnd * 8
            hi = min(lo + 8, IJ)
            if rnd > 0 or (tile - 1) in se_wevd:
                S.wait("tensor", "se", se_wevd[tile - 1] if rnd == 0 else se_rnd_prev)
            def _wtrd(t, sl=sl, lo=lo, hi=hi):
                last_i = None
                for ij in range(lo, hi):
                    last_i = t.transpose(psbf[:, (ij - lo) * 128:(ij - lo) * 128 + 128],
                                         wq_bfD[:, sl, ij * 128:ij * 128 + 128], identbf[:])
                last_i.then_inc(sems["te"], 1)
            S.emit("tensor", _wtrd)
            te_w = S.inc("te", 1)
            S.wait("scalar", "te", te_w)
            def _wevd(s, tile=tile, lo=lo, hi=hi):
                s.copy(out=wdown_q_view(tile, lo, hi),
                       in_=psbf[:, 0:(hi - lo) * 128].rearrange("p (a b) -> p a b", b=128))
            se_rnd_prev = S.emit("scalar", _wevd, n=1)
        se_wevd[tile] = se_rnd_prev
        te_wtrd[tile] = te_w

    S.barrier()

    wdown_q = carve(arena, 0, [128, IJ, 1024], BF16)
    prev_out_st = [None, None]
    prev_dmm = [0, 0]
    prev_out_ve = [0, 0]
    prev_comb = [0, 0]
    for i in range(TT):
        sl = i % 2
        hl_sem = S.use_sem(f"fhl{sl}")
        gates = [hqs_marks[i]]
        if prev_dmm[sl]:
            gates.append(("te", prev_dmm[sl]))
        for gsem, gtgt in gates:
            if gtgt:
                S.wait("sync", gsem, gtgt)
        hl_t = S.dma("sync", hl_sem, (lambda sy, i=i, sl=sl, hl_sem=hl_sem:
            sy.dma_start(out=hqT_in[:, sl],
                         in_=hb["hqT_v"][:, :, i * 128:i * 128 + 128]
                         ).then_inc(sems[hl_sem], 16)))
        rx_sem = S.use_sem(f"frx{sl}")
        if prev_out_ve[sl]:
            S.wait("sync", "ve", prev_out_ve[sl])
        S.dma("sync", rx_sem, (lambda sy, i=i, sl=sl, rx_sem=rx_sem:
            sy.dma_start(out=rx_in[:, sl, 0], in_=hb["rT_v"][i]
                         ).then_inc(sems[rx_sem], 16)))
        rx_t = S.dma("sync", rx_sem, (lambda sy, i=i, sl=sl, rx_sem=rx_sem:
            sy.dma_start(out=rx_in[:, sl, 1], in_=hb["x_v"][i]
                         ).then_inc(sems[rx_sem], 16)))

        ve_o = {}
        for hj in range(2):
            bsl = hj
            S.wait("tensor", hl_sem, hl_t)
            if prev_comb[hj]:
                S.wait("tensor", "ve", prev_comb[hj])
            def _dmm(t, hj=hj, bsl=bsl, sl=sl):
                last_i = None
                for ij in range(IJ):
                    last_i = t.matmul(psum[4 + bsl][:], hqT_in[:, sl, ij],
                                      wdown_q[:, ij, hj * 512:hj * 512 + 512],
                                      start=(ij == 0), stop=(ij == IJ - 1))
                last_i.then_inc(sems["te"], 1)
            S.emit("tensor", _dmm)
            te_d = S.inc("te", 1)
            S.wait("vector", "te", te_d)
            S.wait("vector", rx_sem, rx_t)
            if prev_out_st[sl] and hj == 0:
                S.wait("vector", prev_out_st[sl][0], prev_out_st[sl][1])
            def _comb(v, hj=hj, bsl=bsl, sl=sl):
                v.tensor_tensor(out=out_t[:, sl, hj * 512:hj * 512 + 512],
                                in0=psum[4 + bsl][:],
                                in1=rx_in[:, sl, 0, hj * 512:hj * 512 + 512], op=ALU.add)
                v.tensor_tensor(out=out_t[:, sl, hj * 512:hj * 512 + 512],
                                in0=out_t[:, sl, hj * 512:hj * 512 + 512],
                                in1=rx_in[:, sl, 1, hj * 512:hj * 512 + 512], op=ALU.add)
            ve_o[hj] = S.emit("vector", _comb, n=2)
        prev_dmm[sl] = S.ctr["te"]
        prev_out_ve[sl] = ve_o[1]
        prev_comb[0] = ve_o[0]
        prev_comb[1] = ve_o[1]

        ost_sem = S.use_sem(f"ost{sl}")
        S.wait("scalar", "ve", ve_o[1])
        def _ost(s, i=i, sl=sl, ost_sem=ost_sem):
            s.dma_start(out=hb["out_v"][i], in_=out_t[:, sl]).then_inc(sems[ost_sem], 16)
        S.emit("scalar", _ost)
        prev_out_st[sl] = (ost_sem, S.inc(ost_sem, 16))


def _host_layout(p):
    ac = np.ascontiguousarray
    lay = {}
    lay["wkT"] = ac(np.transpose(p["L0_attn_Wk"], (0, 2, 1)))
    lay["wvT"] = ac(np.transpose(p["L0_attn_Wv"], (0, 2, 1)))
    lay["wrT"] = ac(np.transpose(p["L0_attn_Wr"], (0, 2, 1)))
    lay["wcrT"] = ac(np.transpose(p["L0_attn_Wcr"], (0, 2, 1)))
    lay["woT"] = ac(np.transpose(p["L0_attn_Wo"], (0, 2, 1)))
    lay["wckT"] = ac(np.transpose(p["L0_attn_Wck"], (0, 2, 1)))
    lay["wcvT"] = ac(np.transpose(p["L0_attn_Wcv"], (0, 2, 1)))
    lay["vecs"] = ac(np.stack([p["L0_attn_ln1_w"], p["L0_attn_ln1_b"],
                               p["L0_attn_ln2_w"], p["L0_attn_ln2_b"],
                               p["L0_attn_tmk"], p["L0_attn_tmv"], p["L0_attn_tmr"],
                               p["L0_attn_cmk"], p["L0_attn_cmr"],
                               p["L0_attn_time_decay"], p["L0_attn_time_first"]],
                              axis=-1))
    lay["ln0"] = ac(np.stack([p["L0_attn_ln0_w"], p["L0_attn_ln0_b"]], axis=0))
    lay["wg"] = ac(p["L0_ffn_w_gate"])
    lay["wu"] = ac(p["L0_ffn_w_up"])
    lay["wd"] = ac(p["L0_ffn_w_down"])
    return lay


def kernel(**inputs):
    x = np.asarray(inputs["L0_attn_x"])
    B = x.shape[0]
    lay = _host_layout({k: np.asarray(v) for k, v in inputs.items()})
    nc = build_module(T=x.shape[1], L=4)
    in_maps = []
    for b in range(B):
        m = dict(lay)
        m["x"] = np.ascontiguousarray(x[b])
        in_maps.append(m)
    res = run_bass_kernel_spmd(nc, in_maps, core_ids=list(range(B)))
    return np.stack([res.results[b]["out"] for b in range(B)], axis=0)
